# version 01
# baseline (speedup 1.0000x reference)
import math
import numpy as np
import ml_dtypes
import concourse.bass as bass
import concourse.mybir as mybir
from concourse.alu_op_type import AluOpType as OP
from concourse.bass_utils import run_bass_kernel_spmd
from contextlib import ExitStack

F32 = mybir.dt.float32
BF16 = mybir.dt.bfloat16
AF = mybir.ActivationFunctionType
AX = mybir.AxisListType

NB, S, D = 2, 2048, 1024
NH, HD = 16, 64
E, HE = 64, 256
ES, HS = 2, 2048
TT = S // 128
GT = NB * TT
LAMBDA_INIT = 0.8 - 0.6 * math.exp(-0.3 * 11.0)
LN_EPS = 1e-5
NPASS = 4
TC = (NB * S) // NPASS
NGRP = 10

PHASES = ("p1", "p2", "p3")
TAPS = ()
P1_STAGES = 4
P1_RAW = False
P1_KN = 16
P2_STAGES = 2
P2_NOX2 = False


def build_nc(phases=PHASES, taps=TAPS):
    nc = bass.Bass("TRN2", target_bir_lowering=False,
                   dynamic_dma_scratch_size=2048,
                   detect_race_conditions=False)
    eg = ExitStack()
    tap_out = {}

    def din(name, shape, dt=F32):
        return nc.dram_tensor(name, list(shape), dt, kind="ExternalInput").ap()

    x_in = din("x_sh", [NB, S, D])
    wq_d = din("wq", [D, 2 * D]); wk_d = din("wk", [D, 2 * D])
    wv_d = din("wv", [D, D]); wo_d = din("wo", [D, D])
    lq1 = din("lq1", [1, HD]); lk1 = din("lk1", [1, HD])
    lq2 = din("lq2", [1, HD]); lk2 = din("lk2", [1, HD])
    gam = din("gam", [1, D]); bet = din("bet", [1, D])
    rw_d = din("rw", [D, E]); rb_d = din("rb", [1, E])
    w1_d = din("w1", [E, D, HE]); b1_d = din("b1", [E, HE])
    w2_d = din("w2", [E, HE, D]); b2_d = din("b2", [E, D])
    sw1_d = din("sw1", [ES, D, HS]); sb1_d = din("sb1", [ES, HS])
    sw2_d = din("sw2", [ES, HS, D]); sb2_d = din("sb2", [ES, D])
    id_bf = din("id_bf", [128, 128], BF16)
    id_f = din("id_f32", [128, 128])
    ones_f = din("ones_f32", [1, 128])
    ones_b = din("ones_bf", [1, 128], BF16)
    oh_d = din("oh_bf", [E, E * 128], BF16)

    out = nc.dram_tensor("out", [NB, S, D], F32, kind="ExternalOutput").ap()
    out_t = out.rearrange("b (t p) d -> (b t) p d", p=128)

    def dscr(name, shape, dt):
        kind = "ExternalOutput" if name in taps else "Internal"
        return nc.dram_tensor(name, list(shape), dt, kind=kind).ap()

    qt_d = dscr("qt_d", [NB, TT, 128, S], BF16)
    x2_d = dscr("x2_d", [GT, 128, D], F32)
    lnft_d = dscr("lnft_d", [8, 128, NB * S], BF16)

    x_t = x_in.rearrange("b (t p) d -> b t p d", p=128)
    w2_f = w2_d.rearrange("e h d -> (e h) d")
    sw2_f = sw2_d.rearrange("e h d -> (e h) d")

    def tapout(name, shape, dt):
        ap = nc.dram_tensor(name, list(shape), dt, kind="ExternalOutput").ap()
        tap_out[name] = name
        return ap

    def sbg(name, shape, dt=F32):
        return eg.enter_context(nc.sbuf_tensor(name, list(shape), dt))

    c_idb = sbg("c_idb", [128, 128], BF16)
    c_idf = sbg("c_idf", [128, 128], F32)
    c_onf = sbg("c_onf", [1, 128], F32)
    c_onb = sbg("c_onb", [1, 128], BF16)
    c_eps = sbg("c_eps", [128, 1], F32)
    c_m8 = sbg("c_m8", [128, 1], F32)
    lamv = sbg("lamv", [1, 1], F32)
    lam_b = sbg("lam_b", [1, HD], BF16)
    gb = sbg("gb", [128, D], F32)
    bb = sbg("bb", [128, D], F32)
    rbb = sbg("rbb", [128, E], F32)
    b1T = sbg("b1T", [128, 128], F32)
    sb1T = sbg("sb1T", [128, 32], F32)

    from collections import defaultdict
    cnt = defaultdict(int)
    sems = {}

    def S_(name):
        if name not in sems:
            sems[name] = eg.enter_context(nc.semaphore(name))
        return sems[name]

    for n in ("bar ld misc "
              "x sq var r mbn apply tr trev "
              "wld wc kmm kev qmm qev qst vmm vev "
              "qt mmA exp pv rb rbmm rbe comb womm wored wosq wovar wor "
              "wombn woap xr x2c x2st "
              "rt3 cw1 cw2 cw3 cw4 lnst lnld ln3 xr3 "
              "lnf w1ld w1c gmm silu cwb cwu w2ld w2c w2mm acc oadd ost hz hz2 hz3"
              ).split():
        S_(n)

    bar_n = [0]
    hz2n = [0]
    hz3n = [0]

    def ssync(s, inst):
        hz2n[0] += 1
        inst.then_inc(sems["hz2"], 1)
        s.wait_ge(sems["hz2"], hz2n[0])

    def vsync(v, inst):
        hz3n[0] += 1
        inst.then_inc(sems["hz3"], 1)
        v.wait_ge(sems["hz3"], hz3n[0])

    def barrier(blk):
        bar_n[0] += 5
        tgt = bar_n[0]
        for en in ("tensor", "vector", "scalar", "gpsimd", "sync"):
            def mk(eng):
                eng.nop().then_inc(sems["bar"], 1)
                eng.wait_ge(sems["bar"], tgt)
            getattr(blk, en)(mk)

    with nc.Block() as blk, ExitStack() as ps:
        pA = ps.enter_context(nc.psum_tensor("pA", [128, 512], F32))
        pB = ps.enter_context(nc.psum_tensor("pB", [128, 512], F32))
        pC = ps.enter_context(nc.psum_tensor("pC", [128, 512], F32))
        smv = ps.enter_context(nc.sbuf_tensor("smv", [1, 8], F32))
        lrow = ps.enter_context(nc.sbuf_tensor("lrow", [1, 4 * HD], F32))
        srow = ps.enter_context(nc.sbuf_tensor("srow", [1, D], F32))
        b1s = ps.enter_context(nc.sbuf_tensor("b1s", [64, HE], F32))
        sb1s = ps.enter_context(nc.sbuf_tensor("sb1s", [2, HS], F32))

        @blk.sync
        def _(sy):
            loads = [(c_idb[:], id_bf), (c_idf[:], id_f), (c_onf[:], ones_f),
                     (c_onb[:], ones_b), (b1s[:], b1_d),
                     (sb1s[:], sb1_d),
                     (lrow[:, 0:HD], lq1), (lrow[:, HD:2 * HD], lk1),
                     (lrow[:, 2 * HD:3 * HD], lq2), (lrow[:, 3 * HD:], lk2)]
            for dst, src in loads:
                sy.dma_start(out=dst, in_=src).then_inc(sems["ld"], 16)
            sy.dma_start(out=srow[:, 0:D], in_=gam).then_inc(sems["ld"], 16)
        n_ld0 = 11

        @blk.vector
        def _(v):
            v.wait_ge(sems["ld"], 16 * n_ld0)
            v.memset(c_eps[:], LN_EPS)
            v.memset(c_m8[:], -8.0)
            vsync(v, v.tensor_tensor(lrow[:, 0:HD], lrow[:, 0:HD],
                                     lrow[:, HD:2 * HD], op=OP.mult))
            vsync(v, v.tensor_tensor(lrow[:, 2 * HD:3 * HD],
                                     lrow[:, 2 * HD:3 * HD],
                                     lrow[:, 3 * HD:], op=OP.mult))
            v.reduce_sum(smv[:, 0:1], lrow[:, 0:HD], axis=AX.X)
            v.reduce_sum(smv[:, 1:2], lrow[:, 2 * HD:3 * HD],
                         axis=AX.X).then_inc(sems["misc"], 1)
            v.wait_ge(sems["misc"], 2)
            vsync(v, v.tensor_tensor(smv[:, 4:5], smv[:, 2:3], smv[:, 3:4],
                                     op=OP.subtract))
            vsync(v, v.tensor_scalar(lamv[:], smv[:, 4:5], LAMBDA_INIT, None,
                                     op0=OP.add))
            vsync(v, v.tensor_scalar(lrow[:, 0:HD], c_onf[:, 0:HD],
                                     lamv[:, 0:1], None, op0=OP.mult))
            v.tensor_copy(lam_b[:], lrow[:, 0:HD])
            v.sem_inc(sems["misc"], 1)

        @blk.scalar
        def _(s):
            s.wait_ge(sems["misc"], 1)
            s.activation(smv[:, 2:3], smv[:, 0:1], AF.Exp)
            s.activation(smv[:, 3:4], smv[:, 1:2],
                         AF.Exp).then_inc(sems["misc"], 1)

        @blk.tensor
        def _(t):
            t.wait_ge(sems["misc"], 3)
            t.wait_ge(sems["ld"], 16 * n_ld0)
            for i in range(23):
                pt = pA if i % 2 == 0 else pB
                if i >= 2:
                    t.wait_ge(sems["misc"], 3 + i - 1)
                if i < 5:
                    t.wait_ge(sems["ld"], 16 * (n_ld0 + min(i // 2, 2)))
                    if i < 2:
                        rhs = srow[:, i * 512:i * 512 + 512]
                    elif i < 4:
                        rhs = srow[:, (i - 2) * 512:(i - 2) * 512 + 512]
                    else:
                        rhs = srow[:, 0:E]
                    t.matmul(pt[:, 0:rhs.shape[1]], c_onf[:], rhs,
                             start=True, stop=True).then_inc(sems["mbn"], 1)
                elif i < 7:
                    m = i - 5
                    t.transpose(pt[:, 0:64], b1s[0:64, 128 * m:128 * m + 128],
                                c_idf[0:64, 0:64]).then_inc(sems["mbn"], 1)
                else:
                    j = i - 7
                    t.transpose(pt[:, 0:2], sb1s[0:2, 128 * j:128 * j + 128],
                                c_idf[0:2, 0:2]).then_inc(sems["mbn"], 1)

        @blk.vector
        def _(v):
            for i in range(23):
                pt = pA if i % 2 == 0 else pB
                v.wait_ge(sems["mbn"], i + 1)
                if i < 2:
                    v.tensor_copy(gb[:, i * 512:i * 512 + 512], pt[:, 0:512])
                elif i < 4:
                    v.tensor_copy(bb[:, (i - 2) * 512:(i - 2) * 512 + 512],
                                  pt[:, 0:512])
                elif i == 4:
                    v.tensor_copy(rbb[:], pt[:, 0:E])
                elif i < 7:
                    v.tensor_copy(b1T[:, 64 * (i - 5):64 * (i - 5) + 64],
                                  pt[:, 0:64])
                else:
                    j = i - 7
                    v.tensor_copy(sb1T[:, 2 * j:2 * j + 2], pt[:, 0:2])
                v.sem_inc(sems["misc"], 1)

        @blk.sync
        def _(sy):
            sy.wait_ge(sems["misc"], 5)
            sy.dma_start(out=srow[:, 0:D], in_=bet).then_inc(sems["ld"], 16)
            sy.wait_ge(sems["misc"], 7)
            sy.dma_start(out=srow[:, 0:E], in_=rb_d).then_inc(sems["ld"], 16)
        barrier(blk)
    cnt["ld"] = 16 * (n_ld0 + 2)
    cnt["mbn"] = 23
    cnt["misc"] = 26

    e12 = ExitStack()
    if "p1" in phases:
        def sb12(name, shape, dt):
            return e12.enter_context(nc.sbuf_tensor(name, list(shape), dt))
        KT = sb12("KT", [128, TT * S], BF16)
        Vaug = sb12("Vaug", [128, TT * 1040], BF16)
        xstg = sb12("xstg", [128, 2 * D], F32)
        sqt = sb12("sqt", [128, D], F32)
        st2 = sb12("st2", [128, 16], F32)
        sdum = sb12("sdum", [128, 1], F32)

    def emit_p1(b):
        B = defaultdict(int, cnt)
        with nc.Block() as blk, ExitStack() as ps:
            dbg = (ps.enter_context(nc.sbuf_tensor(f"dbg{b}", [128, 96], F32))
                   if f"dbg_{b}" in taps else None)
            qk = [ps.enter_context(nc.psum_tensor(f"qk{b}_{i}", [128, 512], F32))
                  for i in range(4)]
            trp = [ps.enter_context(nc.psum_tensor(f"tp{b}_{i}", [128, 1024], BF16))
                   for i in range(2)]
            ln1T = ps.enter_context(nc.sbuf_tensor(f"ln1T{b}", [128, 8 * S], BF16))
            wbuf = ps.enter_context(nc.sbuf_tensor(f"wbuf{b}", [128, 8 * 2048], BF16))
            wstg = ps.enter_context(nc.sbuf_tensor(f"wstg{b}", [128, 2 * 2048], F32))
            ln1b = ps.enter_context(nc.sbuf_tensor(f"ln1b{b}", [128, D], BF16))
            qstg = ps.enter_context(nc.sbuf_tensor(f"qstg{b}", [128, 2 * 2048], BF16))

            @blk.sync
            def _(sy):
                for tt in range(TT):
                    if tt >= 2:
                        sy.wait_ge(sems["apply"], B["apply"] + tt - 1)
                    sy.dma_start(out=xstg[:, (tt % 2) * D:(tt % 2) * D + D],
                                 in_=x_t[b, tt]).then_inc(sems["x"], 16)
                for wi, (wap, ncol) in enumerate([(wk_d, 2048), (wq_d, 2048),
                                                  (wv_d, 1024)][:P1_STAGES - 1]):
                    for ch in range(8):
                        gi = wi * 8 + ch
                        if gi >= 2:
                            sy.wait_ge(sems["wc"], B["wc"] + gi - 1)
                        sy.dma_start(
                            out=wstg[:, (gi % 2) * 2048:(gi % 2) * 2048 + ncol],
                            in_=wap[128 * ch:128 * ch + 128, :]
                        ).then_inc(sems["wld"], 16)

            @blk.scalar
            def _(s):
                for tt in range(TT):
                    par = tt % 2
                    s.wait_ge(sems["x"], B["x"] + 16 * (tt + 1))
                    s.activation(sqt[:], xstg[:, par * D:par * D + D], AF.Square,
                                 accum_out=st2[:, 2 + par:3 + par]
                                 ).then_inc(sems["sq"], 1)
                    s.wait_ge(sems["var"], B["var"] + tt + 1)
                    ssync(s, s.activation(st2[:, 12 + par:13 + par],
                                          st2[:, 6 + par:7 + par], AF.Ln,
                                          bias=c_eps[:, 0:1]))
                    s.activation(st2[:, 8 + par:9 + par],
                                 st2[:, 12 + par:13 + par], AF.Exp,
                                 scale=-0.5).then_inc(sems["r"], 1)
                    s.wait_ge(sems["mbn"], B["mbn"] + tt + 1)
                    if tt >= 1:
                        s.wait_ge(sems["tr"], B["tr"] + 8 * tt)
                    if P1_RAW:
                        s.activation(ln1b[:], xstg[:, par * D:par * D + D],
                                     AF.Identity, bias=c_eps[:, 0:1],
                                     scale=1.0).then_inc(sems["apply"], 1)
                    else:
                        s.activation(ln1b[:], xstg[:, par * D:par * D + D],
                                     AF.Identity, bias=st2[:, 10 + par:11 + par],
                                     scale=st2[:, 8 + par:9 + par]
                                     ).then_inc(sems["apply"], 1)
                for ft in range(TT if P1_STAGES >= 3 else 0):
                    par = ft % 2
                    s.wait_ge(sems["qev"], B["qev"] + 4 * (ft + 1))
                    s.dma_start(out=qt_d[b, ft],
                                in_=qstg[:, par * 2048:par * 2048 + 2048]
                                ).then_inc(sems["qst"], 16)
                if P1_STAGES >= 3:
                    s.wait_ge(sems["qst"], B["qst"] + 16 * TT)

            @blk.vector
            def _(v):
                for tt in range(TT):
                    par = tt % 2
                    v.wait_ge(sems["x"], B["x"] + 16 * (tt + 1))
                    vsync(v, v.reduce_sum(st2[:, par:par + 1],
                                          xstg[:, par * D:par * D + D],
                                          axis=AX.X))
                    v.wait_ge(sems["sq"], B["sq"] + tt + 1)
                    vsync(v, v.tensor_scalar(st2[:, 4 + par:5 + par],
                                             st2[:, par:par + 1],
                                             1.0 / D, None, op0=OP.mult))
                    v.tensor_scalar(st2[:, 14 + par:15 + par],
                                    st2[:, 2 + par:3 + par], 1.0 / D, None,
                                    op0=OP.mult)
                    vsync(v, v.tensor_tensor(st2[:, 12 + par:13 + par],
                                             st2[:, 4 + par:5 + par],
                                             st2[:, 4 + par:5 + par],
                                             op=OP.mult))
                    v.tensor_tensor(st2[:, 6 + par:7 + par],
                                    st2[:, 14 + par:15 + par],
                                    st2[:, 12 + par:13 + par],
                                    op=OP.subtract).then_inc(sems["var"], 1)
                    v.wait_ge(sems["r"], B["r"] + tt + 1)
                    v.scalar_tensor_tensor(st2[:, 10 + par:11 + par],
                                           st2[:, 4 + par:5 + par], -1.0,
                                           st2[:, 8 + par:9 + par],
                                           op0=OP.mult, op1=OP.mult
                                           ).then_inc(sems["mbn"], 1)
                    if dbg is not None:
                        v.tensor_copy(dbg[:, 6 * tt:6 * tt + 1],
                                      st2[:, 4 + par:5 + par])
                        v.tensor_copy(dbg[:, 6 * tt + 1:6 * tt + 2],
                                      st2[:, 6 + par:7 + par])
                        v.tensor_copy(dbg[:, 6 * tt + 2:6 * tt + 3],
                                      st2[:, 8 + par:9 + par])
                        v.tensor_copy(dbg[:, 6 * tt + 3:6 * tt + 4],
                                      st2[:, 10 + par:11 + par])
                        v.tensor_copy(dbg[:, 6 * tt + 4:6 * tt + 5],
                                      st2[:, par:par + 1])
                        v.tensor_copy(dbg[:, 6 * tt + 5:6 * tt + 6],
                                      st2[:, 2 + par:3 + par])
                    for dc in range(8):
                        v.wait_ge(sems["tr"], B["tr"] + 8 * tt + dc + 1)
                        v.tensor_copy(
                            ln1T[:, dc * S + 128 * tt:dc * S + 128 * tt + 128],
                            trp[dc % 2][:, 0:128]).then_inc(sems["trev"], 1)
                for ch in range(8 if P1_STAGES >= 2 else 0):
                    v.wait_ge(sems["wld"], B["wld"] + 16 * (ch + 1))
                    v.tensor_copy(wbuf[:, ch * 2048:ch * 2048 + 2048],
                                  wstg[:, (ch % 2) * 2048:(ch % 2) * 2048 + 2048]
                                  ).then_inc(sems["wc"], 1)
                for ft in range(P1_KN if P1_STAGES >= 2 else 0):
                    v.wait_ge(sems["kmm"], B["kmm"] + ft + 1)
                    for n in range(4):
                        v.tensor_copy(
                            KT[:, ft * S + 512 * n:ft * S + 512 * n + 512],
                            qk[n][:]).then_inc(sems["kev"], 1)
                if P1_STAGES >= 3:
                    v.wait_ge(sems["kmm"], B["kmm"] + TT)
                for ch in range(8 if P1_STAGES >= 3 else 0):
                    gi = 8 + ch
                    v.wait_ge(sems["wld"], B["wld"] + 16 * (gi + 1))
                    v.tensor_copy(wbuf[:, ch * 2048:ch * 2048 + 2048],
                                  wstg[:, (gi % 2) * 2048:(gi % 2) * 2048 + 2048]
                                  ).then_inc(sems["wc"], 1)
                for ft in range(TT if P1_STAGES >= 3 else 0):
                    par = ft % 2
                    v.wait_ge(sems["qmm"], B["qmm"] + ft + 1)
                    if ft >= 2:
                        v.wait_ge(sems["qst"], B["qst"] + 16 * (ft - 1))
                    for n in range(4):
                        v.tensor_copy(
                            qstg[:, par * 2048 + 512 * n:
                                 par * 2048 + 512 * n + 512],
                            qk[n][:]).then_inc(sems["qev"], 1)
                if P1_STAGES >= 4:
                    v.wait_ge(sems["qmm"], B["qmm"] + TT)
                for ch in range(8 if P1_STAGES >= 4 else 0):
                    gi = 16 + ch
                    v.wait_ge(sems["wld"], B["wld"] + 16 * (gi + 1))
                    v.tensor_copy(wbuf[:, ch * 2048:ch * 2048 + 1024],
                                  wstg[:, (gi % 2) * 2048:(gi % 2) * 2048 + 1024]
                                  ).then_inc(sems["wc"], 1)
                if b == 0 and P1_STAGES >= 4:
                    va3 = Vaug[:].rearrange("p (t c) -> p t c", c=65)
                    v.memset(va3[:, :, 64:65], 1.0)
                v.sem_inc(sems["vev"], 1)
                for tt in range(TT if P1_STAGES >= 4 else 0):
                    for vc in range(2):
                        v.wait_ge(sems["vmm"], B["vmm"] + tt + 1)
                        dst = Vaug[:, tt * 1040 + 520 * vc:
                                   tt * 1040 + 520 * vc + 520]
                        dst = dst.rearrange("p (h c) -> p h c", c=65)[:, :, 0:64]
                        src = qk[vc][:].rearrange("p (h c) -> p h c", c=64)
                        v.tensor_copy(dst, src).then_inc(sems["vev"], 1)

            @blk.tensor
            def _(t):
                for tt in range(TT):
                    t.wait_ge(sems["apply"], B["apply"] + tt + 1)
                    for dc in range(8):
                        gi = 8 * tt + dc
                        if gi >= 2:
                            t.wait_ge(sems["trev"], B["trev"] + gi - 1)
                        t.transpose(trp[dc % 2][:, 0:128],
                                    ln1b[:, 128 * dc:128 * dc + 128],
                                    c_idb[:]).then_inc(sems["tr"], 1)
                t.wait_ge(sems["trev"], B["trev"] + 128)
                if P1_STAGES >= 2:
                    t.wait_ge(sems["wc"], B["wc"] + 8)
                for ft in range(P1_KN if P1_STAGES >= 2 else 0):
                    if ft >= 1:
                        t.wait_ge(sems["kev"], B["kev"] + 4 * ft)
                    for dc in range(8):
                        for n in range(4):
                            inst = t.matmul(qk[n][:],
                                            wbuf[:, dc * 2048 + 128 * ft:
                                                 dc * 2048 + 128 * ft + 128],
                                            ln1T[:, dc * S + 512 * n:
                                                 dc * S + 512 * n + 512],
                                            start=(dc == 0), stop=(dc == 7))
                    inst.then_inc(sems["kmm"], 1)
                if P1_STAGES >= 3:
                    t.wait_ge(sems["wc"], B["wc"] + 16)
                    t.wait_ge(sems["kev"], B["kev"] + 4 * TT)
                for ft in range(TT if P1_STAGES >= 3 else 0):
                    if ft >= 1:
                        t.wait_ge(sems["qev"], B["qev"] + 4 * ft)
                    for dc in range(8):
                        for n in range(4):
                            inst = t.matmul(qk[n][:],
                                            wbuf[:, dc * 2048 + 128 * ft:
                                                 dc * 2048 + 128 * ft + 128],
                                            ln1T[:, dc * S + 512 * n:
                                                 dc * S + 512 * n + 512],
                                            start=(dc == 0), stop=(dc == 7))
                    inst.then_inc(sems["qmm"], 1)
                if P1_STAGES >= 4:
                    t.wait_ge(sems["wc"], B["wc"] + 24)
                    t.wait_ge(sems["qev"], B["qev"] + 4 * TT)
                for tt in range(TT if P1_STAGES >= 4 else 0):
                    if tt >= 1:
                        t.wait_ge(sems["vev"], B["vev"] + 2 * tt + 1)
                    for dc in range(8):
                        for vc in range(2):
                            inst = t.matmul(qk[vc][:],
                                            ln1T[:, dc * S + 128 * tt:
                                                 dc * S + 128 * tt + 128],
                                            wbuf[:, dc * 2048 + 512 * vc:
                                                 dc * 2048 + 512 * vc + 512],
                                            start=(dc == 0), stop=(dc == 7))
                    inst.then_inc(sems["vmm"], 1)

            tp = [(f"ln1T_{b}", ln1T, "trev", B["trev"] + 128),
                  (f"KT_{b}", KT, "kev", B["kev"] + 4 * P1_KN),
                  (f"Vaug_{b}", Vaug, "vev", B["vev"] + 2 * TT + 1)]
            if dbg is not None:
                tp.append((f"dbg_{b}", dbg, "mbn", B["mbn"] + TT))
            tp = [x for x in tp if x[0] in taps]
            if tp:
                @blk.scalar
                def _(g):
                    for name, buf, sname, val in tp:
                        dst = tapout(name, list(buf.shape),
                                     F32 if name.startswith("dbg") else BF16)
                        g.wait_ge(sems[sname], val)
                        g.dma_start(out=dst, in_=buf[:]).then_inc(sems["ost"], 16)
                        cnt["ost"] += 16
                    g.wait_ge(sems["ost"], cnt["ost"])
            barrier(blk)
        ns = P1_STAGES
        nw = [0, 0, 8, 16, 24][ns]
        for k, d in [("x", 16 * TT), ("sq", TT), ("var", TT), ("r", TT),
                     ("mbn", TT), ("apply", TT), ("tr", 8 * TT),
                     ("trev", 8 * TT), ("wld", 16 * nw), ("wc", nw),
                     ("kmm", P1_KN if ns >= 2 else 0),
                     ("kev", 4 * P1_KN if ns >= 2 else 0),
                     ("qmm", TT if ns >= 3 else 0),
                     ("qev", 4 * TT if ns >= 3 else 0),
                     ("qst", 16 * TT if ns >= 3 else 0),
                     ("vmm", TT if ns >= 4 else 0),
                     ("vev", (2 * TT + 1) if ns >= 4 else 0)]:
            cnt[k] = B[k] + d

    def emit_p2(b):
        B = defaultdict(int, cnt)
        with nc.Block() as blk, ExitStack() as ps:
            apm = [ps.enter_context(nc.psum_tensor(f"ap{b}_{i}", [128, 1024], F32))
                   for i in range(2)]
            o1 = ps.enter_context(nc.psum_tensor(f"o1_{b}", [65, 512], F32))
            o2 = ps.enter_context(nc.psum_tensor(f"o2_{b}", [65, 512], F32))
            mix = ps.enter_context(nc.psum_tensor(f"mx_{b}", [128, 1024], F32))
            wob = ps.enter_context(nc.sbuf_tensor(f"wob{b}", [128, 8 * D], BF16))
            qtq = ps.enter_context(nc.sbuf_tensor(f"qtq{b}", [128, 2 * NH * 512], BF16))
            pbuf = ps.enter_context(nc.sbuf_tensor(f"pbuf{b}", [128, 2 * 1024], BF16))
            wstg = ps.enter_context(nc.sbuf_tensor(f"wst2{b}", [128, 2 * 1024], F32))
            rt12 = ps.enter_context(nc.sbuf_tensor(f"rt12{b}", [1, 2 * 512], F32))
            rb1b = ps.enter_context(nc.sbuf_tensor(f"rb1b{b}", [1, 512], BF16))
            rb2b = ps.enter_context(nc.sbuf_tensor(f"rb2b{b}", [1, 512], BF16))
            rbs1 = ps.enter_context(nc.sbuf_tensor(f"rbs1{b}", [64, 512], F32))
            rbs2 = ps.enter_context(nc.sbuf_tensor(f"rbs2{b}", [64, 512], F32))
            cb1 = ps.enter_context(nc.sbuf_tensor(f"cb1{b}", [64, 512], F32))
            cb2 = ps.enter_context(nc.sbuf_tensor(f"cb2{b}", [64, 512], F32))
            aoT = ps.enter_context(nc.sbuf_tensor(f"aoT{b}", [128, 8 * 512], BF16))
            tsb = ps.enter_context(nc.sbuf_tensor(f"tsb{b}", [128, 2 * D], F32))
            x2s = ps.enter_context(nc.sbuf_tensor(f"x2s{b}", [128, 2 * D], F32))

            @blk.sync
            def _(sy):
                for ch in range(8):
                    if ch >= 2:
                        sy.wait_ge(sems["wc"], B["wc"] + ch - 1)
                    sy.dma_start(
                        out=wstg[:, (ch % 2) * 1024:(ch % 2) * 1024 + 1024],
                        in_=wo_d[128 * ch:128 * ch + 128, :]
                        ).then_inc(sems["wld"], 16)
                sy.wait_ge(sems["qst"], 16 * TT * (b + 1))
                for qb in range(4):
                    if qb >= 2:
                        sy.wait_ge(sems["mmA"], B["mmA"] + (qb - 1) * 256)
                    for ft in range(TT):
                        sy.dma_start(
                            out=qtq[:, (qb % 2) * 8192 + ft * 512:
                                    (qb % 2) * 8192 + ft * 512 + 512],
                            in_=qt_d[b, ft, :, 512 * qb:512 * qb + 512]
                            ).then_inc(sems["qt"], 16)
                    for tt in range(4 if (P2_STAGES >= 2 and not P2_NOX2) else 0):
                        wt = qb * 4 + tt
                        if wt >= 2:
                            sy.wait_ge(sems["x2c"], B["x2c"] + wt - 1)
                        sy.dma_start(out=xstg[:, (wt % 2) * D:(wt % 2) * D + D],
                                     in_=x_t[b, wt]).then_inc(sems["xr"], 16)
                    for tt in range(4 if (P2_STAGES >= 2 and not P2_NOX2) else 0):
                        wt = qb * 4 + tt
                        sy.wait_ge(sems["x2c"], B["x2c"] + wt + 1)
                        sy.dma_start(out=x2_d[b * TT + wt],
                                     in_=x2s[:, (wt % 2) * D:(wt % 2) * D + D]
                                     ).then_inc(sems["x2st"], 16)
                if qb == 3 and P2_STAGES >= 2 and not P2_NOX2:
                    sy.wait_ge(sems["x2st"], B["x2st"] + 16 * 16)

            @blk.vector
            def _(v):
                for ch in range(8):
                    v.wait_ge(sems["wld"], B["wld"] + 16 * (ch + 1))
                    v.tensor_copy(wob[:, ch * D:ch * D + D],
                                  wstg[:, (ch % 2) * 1024:(ch % 2) * 1024 + 1024]
                                  ).then_inc(sems["wc"], 1)
                for qb in range(4):
                    for h in range(NH):
                        qh = qb * NH + h
                        v.wait_ge(sems["rbe"], B["rbe"] + qh + 1)
                        if qh >= 1 and P2_STAGES >= 2:
                            v.wait_ge(sems["womm"], B["womm"] + 4 * qb)
                        vsync(v, v.tensor_tensor(cb1[:], o1[0:64, :], rbs1[:],
                                                 op=OP.mult))
                        vsync(v, v.tensor_tensor(cb2[:], o2[0:64, :], rbs2[:],
                                                 op=OP.mult))
                        r0 = 64 * (h % 2)
                        c0 = 512 * (h // 2)
                        v.tensor_tensor(aoT[r0:r0 + 64, c0:c0 + 512],
                                        cb1[:], cb2[:],
                                        op=OP.subtract).then_inc(sems["comb"], 1)
                    for tt in range(4 if P2_STAGES >= 2 else 0):
                        wt = qb * 4 + tt
                        par = wt % 2
                        wop = mix if tt % 2 == 0 else apm[0]
                        v.wait_ge(sems["womm"], B["womm"] + wt + 1)
                        vsync(v, v.reduce_sum(st2[:, par:par + 1], wop[:],
                                              axis=AX.X))
                        v.sem_inc(sems["wored"], 1)
                        v.wait_ge(sems["wosq"], B["wosq"] + wt + 1)
                        vsync(v, v.tensor_scalar(st2[:, 4 + par:5 + par],
                                                 st2[:, par:par + 1], 1.0 / D,
                                                 None, op0=OP.mult))
                        v.tensor_scalar(st2[:, 14 + par:15 + par],
                                        st2[:, 2 + par:3 + par], 1.0 / D, None,
                                        op0=OP.mult)
                        vsync(v, v.tensor_tensor(st2[:, 12 + par:13 + par],
                                                 st2[:, 4 + par:5 + par],
                                                 st2[:, 4 + par:5 + par],
                                                 op=OP.mult))
                        v.tensor_tensor(st2[:, 6 + par:7 + par],
                                        st2[:, 14 + par:15 + par],
                                        st2[:, 12 + par:13 + par],
                                        op=OP.subtract).then_inc(sems["wovar"], 1)
                        v.wait_ge(sems["wor"], B["wor"] + wt + 1)
                        v.scalar_tensor_tensor(st2[:, 10 + par:11 + par],
                                               st2[:, 4 + par:5 + par], -1.0,
                                               st2[:, 8 + par:9 + par],
                                               op0=OP.mult, op1=OP.mult
                                               ).then_inc(sems["wombn"], 1)
                        if P2_NOX2:
                            v.sem_inc(sems["x2c"], 1)
                            continue
                        v.wait_ge(sems["woap"], B["woap"] + wt + 1)
                        v.wait_ge(sems["xr"], B["xr"] + 16 * (wt + 1))
                        if wt >= 2:
                            v.wait_ge(sems["x2st"], B["x2st"] + 16 * (wt - 1))
                        vsync(v, v.tensor_tensor(x2s[:, par * D:par * D + D],
                                                 xstg[:, par * D:par * D + D],
                                                 bb[:], op=OP.add))
                        vsync(v, v.tensor_tensor(tsb[:, par * D:par * D + D],
                                                 tsb[:, par * D:par * D + D],
                                                 gb[:], op=OP.mult))
                        v.tensor_tensor(x2s[:, par * D:par * D + D],
                                        x2s[:, par * D:par * D + D],
                                        tsb[:, par * D:par * D + D],
                                        op=OP.add).then_inc(sems["x2c"], 1)

            @blk.scalar
            def _(s):
                for qb in range(4):
                    for h in range(NH):
                        qh = qb * NH + h
                        for kt in range(TT):
                            g = qh * TT + kt
                            s.wait_ge(sems["mmA"], B["mmA"] + g + 1)
                            if g >= 2:
                                s.wait_ge(sems["pv"], B["pv"] + g - 1)
                            s.activation(
                                pbuf[:, (g % 2) * 1024:(g % 2) * 1024 + 1024],
                                apm[g % 2][:], AF.Exp, scale=0.125,
                                bias=c_m8[:, 0:1]).then_inc(sems["exp"], 1)
                        s.wait_ge(sems["pv"], B["pv"] + (qh + 1) * TT)
                        if qh >= 1:
                            s.wait_ge(sems["rbmm"], B["rbmm"] + qh)
                        ssync(s, s.activation(rt12[:, 0:512], o1[64:65, :],
                                              AF.Ln))
                        s.activation(rb1b[:], rt12[:, 0:512], AF.Exp, scale=-1.0)
                        ssync(s, s.activation(rt12[:, 512:1024], o2[64:65, :],
                                              AF.Ln))
                        s.activation(rb2b[:], rt12[:, 512:1024], AF.Exp,
                                     scale=-1.0).then_inc(sems["rb"], 1)
                        s.wait_ge(sems["rbmm"], B["rbmm"] + qh + 1)
                        if qh >= 1:
                            s.wait_ge(sems["comb"], B["comb"] + qh)
                        s.copy(rbs1[:], mix[0:64, 0:512])
                        s.copy(rbs2[:], mix[64:128, 0:512]
                               ).then_inc(sems["rbe"], 1)
                    for tt in range(4 if P2_STAGES >= 2 else 0):
                        wt = qb * 4 + tt
                        par = wt % 2
                        wop = mix if tt % 2 == 0 else apm[0]
                        s.wait_ge(sems["womm"], B["womm"] + wt + 1)
                        s.wait_ge(sems["wored"], B["wored"] + wt + 1)
                        s.activation(sqt[:], wop[:], AF.Square,
                                     accum_out=st2[:, 2 + par:3 + par]
                                     ).then_inc(sems["wosq"], 1)
                        s.wait_ge(sems["wovar"], B["wovar"] + wt + 1)
                        ssync(s, s.activation(st2[:, 12 + par:13 + par],
                                              st2[:, 6 + par:7 + par], AF.Ln,
                                              bias=c_eps[:, 0:1]))
                        s.activation(st2[:, 8 + par:9 + par],
                                     st2[:, 12 + par:13 + par], AF.Exp,
                                     scale=-0.5).then_inc(sems["wor"], 1)
                        s.wait_ge(sems["wombn"], B["wombn"] + wt + 1)
                        if wt >= 2:
                            s.wait_ge(sems["x2c"], B["x2c"] + wt - 1)
                        s.activation(tsb[:, par * D:par * D + D], wop[:],
                                     AF.Identity,
                                     bias=st2[:, 10 + par:11 + par],
                                     scale=st2[:, 8 + par:9 + par]
                                     ).then_inc(sems["woap"], 1)

            @blk.tensor
            def _(t):
                t.wait_ge(sems["wc"], B["wc"] + 8)
                for qb in range(4):
                    t.wait_ge(sems["qt"], B["qt"] + 16 * TT * (qb + 1))
                    if P2_STAGES >= 2:
                        t.wait_ge(sems["wored"], B["wored"] + 4 * qb)
                        t.wait_ge(sems["wosq"], B["wosq"] + 4 * qb)
                        t.wait_ge(sems["woap"], B["woap"] + 4 * qb)
                    for h in range(NH):
                        qh = qb * NH + h
                        qtb = (qb % 2) * 8192 + h * 512

                        def emitA(kt):
                            g = qh * TT + kt
                            t.matmul(apm[g % 2][:, 0:512],
                                     KT[0:64, h * S + 128 * kt:
                                        h * S + 128 * kt + 128],
                                     qtq[0:64, qtb:qtb + 512],
                                     start=True, stop=True)
                            t.matmul(apm[g % 2][:, 512:1024],
                                     KT[64:128, h * S + 128 * kt:
                                        h * S + 128 * kt + 128],
                                     qtq[64:128, qtb:qtb + 512],
                                     start=True, stop=True
                                     ).then_inc(sems["mmA"], 1)

                        def emitPV(kt):
                            g = qh * TT + kt
                            t.wait_ge(sems["exp"], B["exp"] + g + 1)
                            if kt == 0:
                                t.wait_ge(sems["comb"], B["comb"] + qh)
                            vb = kt * 1040 + 65 * h
                            pb = (g % 2) * 1024
                            t.matmul(o1[:], Vaug[:, vb:vb + 65],
                                     pbuf[:, pb:pb + 512],
                                     start=(kt == 0), stop=(kt == 15))
                            t.matmul(o2[:], Vaug[:, vb:vb + 65],
                                     pbuf[:, pb + 512:pb + 1024],
                                     start=(kt == 0), stop=(kt == 15)
                                     ).then_inc(sems["pv"], 1)

                        emitA(0)
                        emitA(1)
                        for kt in range(TT - 1):
                            emitPV(kt)
                            if kt + 2 < TT:
                                emitA(kt + 2)
                        emitPV(TT - 1)
                        t.wait_ge(sems["rb"], B["rb"] + qh + 1)
                        if qh >= 1:
                            t.wait_ge(sems["rbe"], B["rbe"] + qh)
                        t.matmul(mix[0:64, 0:512], c_onb[0:1, 0:64], rb1b[:],
                                 start=True, stop=True, tile_position=(0, 0))
                        t.matmul(mix[64:128, 0:512], lam_b[0:1, 0:64], rb2b[:],
                                 start=True, stop=True, tile_position=(0, 64)
                                 ).then_inc(sems["rbmm"], 1)
                    if P2_STAGES >= 2:
                        t.wait_ge(sems["comb"], B["comb"] + 16 * (qb + 1))
                        t.wait_ge(sems["rbe"], B["rbe"] + 16 * (qb + 1))
                        t.wait_ge(sems["exp"], B["exp"] + 256 * (qb + 1))
                    for tt in range(4 if P2_STAGES >= 2 else 0):
                        wt = qb * 4 + tt
                        wop = mix if tt % 2 == 0 else apm[0]
                        if wt >= 2:
                            t.wait_ge(sems["wored"], B["wored"] + wt - 1)
                            t.wait_ge(sems["wosq"], B["wosq"] + wt - 1)
                            t.wait_ge(sems["woap"], B["woap"] + wt - 1)
                        for dc in range(8):
                            for nh2 in range(2):
                                inst = t.matmul(wop[:, 512 * nh2:512 * nh2 + 512],
                                                aoT[:, dc * 512 + 128 * tt:
                                                    dc * 512 + 128 * tt + 128],
                                                wob[:, dc * D + 512 * nh2:
                                                    dc * D + 512 * nh2 + 512],
                                                start=(dc == 0), stop=(dc == 7))
                        inst.then_inc(sems["womm"], 1)

            if f"aoT_{b}" in taps:
                @blk.gpsimd
                def _(g):
                    dst = tapout(f"aoT_{b}", [128, 8 * 512], BF16)
                    g.wait_ge(sems["comb"], B["comb"] + 64)
                    g.dma_start(out=dst, in_=aoT[:]).then_inc(sems["ost"], 16)
                    cnt["ost"] += 16
                    g.wait_ge(sems["ost"], cnt["ost"])
            barrier(blk)
        s2 = P2_STAGES >= 2
        for k, d in [("wld", 16 * 8), ("wc", 8), ("qt", 16 * TT * 4),
                     ("mmA", 1024), ("exp", 1024), ("pv", 1024),
                     ("rb", 64), ("rbmm", 64), ("rbe", 64), ("comb", 64),
                     ("womm", 16 if s2 else 0), ("wored", 16 if s2 else 0),
                     ("wosq", 16 if s2 else 0), ("wovar", 16 if s2 else 0),
                     ("wor", 16 if s2 else 0), ("wombn", 16 if s2 else 0),
                     ("woap", 16 if s2 else 0),
                     ("xr", 16 * 16 if (s2 and not P2_NOX2) else 0),
                     ("x2c", 16 if s2 else 0),
                     ("x2st", 16 * 16 if (s2 and not P2_NOX2) else 0)]:
            cnt[k] = B[k] + d

    def emit_p3():
        e3 = ExitStack()

        def sb3(name, shape, dt):
            return e3.enter_context(nc.sbuf_tensor(name, list(shape), dt))

        CWE = sb3("CWE", [128, NB * S], BF16)

        B = defaultdict(int, cnt)
        with nc.Block() as blk, ExitStack() as ps:
            trp = [ps.enter_context(nc.psum_tensor(f"t3_{i}", [128, 1024], BF16))
                   for i in range(2)]
            lgp = ps.enter_context(nc.psum_tensor("lgp", [128, 512], F32))
            cwp = ps.enter_context(nc.psum_tensor("cwp", [64, 512], F32))
            def sbp(name, shape, dt):
                return ps.enter_context(nc.sbuf_tensor(name, list(shape), dt))
            rwb = sbp("rwb", [128, 8 * E], BF16)
            rws = sbp("rws", [128, 8 * E], F32)
            xst3 = sbp("xst3", [128, 2 * D], F32)
            lnfb = sbp("lnfb", [128, 2 * D], BF16)
            lnfa = sbp("lnfa", [128, D], BF16)
            sq3 = sbp("sq3", [128, D], F32)
            st3 = sbp("st3", [128, 16], F32)
            lgs = sbp("lgs", [128, E], F32)
            rt3t = sbp("rt3t", [128, 4], F32)
            m8 = sbp("m8", [128, 8], F32)
            elx = sbp("elx", [128, E], F32)

            @blk.sync
            def _(sy):
                sy.dma_start(out=rws[:].rearrange("p (c e) -> p c e", e=E),
                             in_=rw_d.rearrange("(c p) e -> p c e", p=128)
                             ).then_inc(sems["wld"], 16)
                sy.wait_ge(sems["x2st"], 16 * GT)
                for gt in range(GT):
                    if gt >= 2:
                        sy.wait_ge(sems["ln3"], B["ln3"] + gt - 1)
                    sy.dma_start(out=xst3[:, (gt % 2) * D:(gt % 2) * D + D],
                                 in_=x2_d[gt]).then_inc(sems["xr3"], 16)

            @blk.vector
            def _(v):
                v.wait_ge(sems["wld"], B["wld"] + 16)
                v.tensor_copy(rwb[:], rws[:]).then_inc(sems["wc"], 1)
                vsync(v, v.memset(CWE[64:128, :], 0.0))
                v.memset(CWE[64:66, :], 1.0)
                v.sem_inc(sems["cw4"], 1)
                for gt in range(GT):
                    par = gt % 2
                    v.wait_ge(sems["xr3"], B["xr3"] + 16 * (gt + 1))
                    vsync(v, v.reduce_sum(st3[:, par:par + 1],
                                          xst3[:, par * D:par * D + D],
                                          axis=AX.X))
                    v.wait_ge(sems["sq"], B["sq"] + gt + 1)
                    vsync(v, v.tensor_scalar(st3[:, 4 + par:5 + par],
                                             st3[:, par:par + 1],
                                             1.0 / D, None, op0=OP.mult))
                    v.tensor_scalar(st3[:, 14 + par:15 + par],
                                    st3[:, 2 + par:3 + par], 1.0 / D, None,
                                    op0=OP.mult)
                    vsync(v, v.tensor_tensor(st3[:, 12 + par:13 + par],
                                             st3[:, 4 + par:5 + par],
                                             st3[:, 4 + par:5 + par],
                                             op=OP.mult))
                    v.tensor_tensor(st3[:, 6 + par:7 + par],
                                    st3[:, 14 + par:15 + par],
                                    st3[:, 12 + par:13 + par],
                                    op=OP.subtract).then_inc(sems["var"], 1)
                    v.wait_ge(sems["r"], B["r"] + gt + 1)
                    v.scalar_tensor_tensor(st3[:, 10 + par:11 + par],
                                           st3[:, 4 + par:5 + par], -1.0,
                                           st3[:, 8 + par:9 + par],
                                           op0=OP.mult, op1=OP.mult
                                           ).then_inc(sems["mbn"], 1)
                    if gt >= 2:
                        v.wait_ge(sems["lnld"], B["lnld"] + 16 * 8 * (gt - 1))
                        v.wait_ge(sems["rt3"], B["rt3"] + gt - 1)
                    for dc in range(8):
                        v.wait_ge(sems["tr"], B["tr"] + 8 * gt + dc + 1)
                        v.tensor_copy(lnfb[:, par * D + 128 * dc:
                                          par * D + 128 * dc + 128],
                                      trp[dc % 2][:, 0:128]).then_inc(sems["trev"], 1)
                    v.wait_ge(sems["rt3"], B["rt3"] + gt + 1)
                    vsync(v, v.tensor_tensor(lgs[:], lgp[:, 0:E], rbb[:],
                                             op=OP.add))
                    vsync(v, v.max(out=m8[:], in_=lgs[:]))
                    v.tensor_scalar(rt3t[:, 0:1], m8[:, 0:1],
                                    -1.0, None, op0=OP.mult
                                    ).then_inc(sems["cw1"], 1)
                    v.wait_ge(sems["cw2"], B["cw2"] + gt + 1)
                    vsync(v, v.tensor_scalar(lgs[:], lgs[:], m8[:, 7:8], None,
                                             op0=OP.is_ge))
                    vsync(v, v.tensor_tensor(elx[:], elx[:], lgs[:],
                                             op=OP.mult))
                    vsync(v, v.reduce_sum(rt3t[:, 1:2], elx[:], axis=AX.X))
                    vsync(v, v.reciprocal(rt3t[:, 2:3], rt3t[:, 1:2]))
                    v.tensor_scalar(elx[:], elx[:], rt3t[:, 2:3], None,
                                    op0=OP.mult).then_inc(sems["cw3"], 1)
                    v.wait_ge(sems["cw4"], B["cw4"] + gt + 2)
                    v.tensor_copy(CWE[0:64, 128 * gt:128 * gt + 128],
                                  cwp[:, 0:128]).then_inc(sems["cwu"], 1)

            @blk.scalar
            def _(s):
                for gt in range(GT):
                    par = gt % 2
                    s.wait_ge(sems["xr3"], B["xr3"] + 16 * (gt + 1))
                    s.activation(sq3[:], xst3[:, par * D:par * D + D], AF.Square,
                                 accum_out=st3[:, 2 + par:3 + par]
                                 ).then_inc(sems["sq"], 1)
                    s.wait_ge(sems["var"], B["var"] + gt + 1)
                    ssync(s, s.activation(st3[:, 12 + par:13 + par],
                                          st3[:, 6 + par:7 + par], AF.Ln,
                                          bias=c_eps[:, 0:1]))
                    s.activation(st3[:, 8 + par:9 + par],
                                 st3[:, 12 + par:13 + par], AF.Exp,
                                 scale=-0.5).then_inc(sems["r"], 1)
                    s.wait_ge(sems["mbn"], B["mbn"] + gt + 1)
                    if gt >= 1:
                        s.wait_ge(sems["tr"], B["tr"] + 8 * gt)
                    s.activation(lnfa[:], xst3[:, par * D:par * D + D],
                                 AF.Identity, bias=st3[:, 10 + par:11 + par],
                                 scale=st3[:, 8 + par:9 + par]
                                 ).then_inc(sems["ln3"], 1)
                    s.wait_ge(sems["cw1"], B["cw1"] + gt + 1)
                    s.wait_ge(sems["cw4"], B["cw4"] + gt + 1)
                    s.activation(elx[:], lgs[:], AF.Exp,
                                 bias=rt3t[:, 0:1]
                                 ).then_inc(sems["cw2"], 1)
                    s.wait_ge(sems["trev"], B["trev"] + 8 * (gt + 1))
                    for dc in range(8):
                        s.dma_start(out=lnft_d[dc, :, 128 * gt:128 * gt + 128],
                                    in_=lnfb[:, par * D + 128 * dc:
                                             par * D + 128 * dc + 128]
                                    ).then_inc(sems["lnld"], 16)
                s.wait_ge(sems["lnld"], B["lnld"] + 16 * 8 * GT)

            @blk.tensor
            def _(t):
                t.wait_ge(sems["wc"], B["wc"] + 1)
                for gt in range(GT):
                    par = gt % 2
                    t.wait_ge(sems["ln3"], B["ln3"] + gt + 1)
                    for dc in range(8):
                        gi = 8 * gt + dc
                        if gi >= 2:
                            t.wait_ge(sems["trev"], B["trev"] + gi - 1)
                        t.transpose(trp[dc % 2][:, 0:128],
                                    lnfa[:, 128 * dc:128 * dc + 128],
                                    c_idb[:]).then_inc(sems["tr"], 1)
                    t.wait_ge(sems["trev"], B["trev"] + 8 * (gt + 1))
                    if gt >= 2:
                        t.wait_ge(sems["lnld"], B["lnld"] + 16 * 8 * (gt - 1))
                    for dc in range(8):
                        inst = t.matmul(lgp[:, 0:E],
                                        lnfb[:, par * D + 128 * dc:
                                             par * D + 128 * dc + 128],
                                        rwb[:, dc * E:dc * E + E],
                                        start=(dc == 0), stop=(dc == 7))
                    inst.then_inc(sems["rt3"], 1)
                    t.wait_ge(sems["cw3"], B["cw3"] + gt + 1)
                    if gt >= 1:
                        t.wait_ge(sems["cwu"], B["cwu"] + gt)
                    t.transpose(cwp[:, 0:128], elx[:, 0:64], c_idf[:]
                                ).then_inc(sems["cw4"], 1)
            barrier(blk)
        for k, d in [("wld", 16), ("wc", 1), ("xr3", 16 * GT), ("sq", GT),
                     ("var", GT), ("r", GT), ("mbn", GT), ("ln3", GT),
                     ("tr", 8 * GT), ("trev", 8 * GT), ("rt3", GT),
                     ("cw1", GT), ("cw2", GT), ("cw3", GT), ("cw4", GT + 1),
                     ("cwu", GT), ("hz", GT), ("lnld", 16 * 8 * GT)]:
            cnt[k] = B[k] + d

        B = defaultdict(int, cnt)
        with nc.Block() as blk, ExitStack() as ps:
            gps = [ps.enter_context(nc.psum_tensor(f"g3_{m}", [128, 1024], F32))
                   for m in range(2)]
            cwbp = ps.enter_context(nc.psum_tensor("cwbp", [128, 1024], F32))
            w2p = [ps.enter_context(nc.psum_tensor(f"w2p{i}", [128, 512], F32))
                   for i in range(2)]
            def sbm(name, shape, dt):
                return ps.enter_context(nc.sbuf_tensor(name, list(shape), dt))
            lnfc = sbm("lnfc", [128, 8 * 1024], BF16)
            Gg = sbm("Gg", [128, 16 * 1024], BF16)
            w2b = sbm("w2b", [128, 16 * 1024], BF16)
            acc = sbm("acc", [128, 8 * 1024], F32)
            ohb = sbm("ohb", [E, E * 128], BF16)
            w1s = sbm("w1s", [128, 2 * 2048], F32)
            w1b = sbm("w1b", [128, 2 * 2048], BF16)
            w2s = sbm("w2s", [128, 2 * 1024], F32)
            stmp = sbm("stmp", [128, 2 * 1024], BF16)
            x2s3 = sbm("x2s3", [128, 2 * D], F32)
            ostg = sbm("ostg", [128, 2 * D], F32)
            b2st = sbm("b2st", [64, D], F32)
            sb2st = sbm("sb2st", [64, D], F32)
            b2b = sbm("b2b", [128, D], BF16)

            def eidx(p, grp, el):
                return (p * NGRP + grp) * 8 + el

            def epidx(p, grp, mt, nh2):
                return ((p * NGRP + grp) * 8 + mt) * 2 + nh2

            @blk.sync
            def _(sy):
                sy.dma_start(out=ohb[:], in_=oh_d).then_inc(sems["wld"], 16)
                sy.dma_start(out=b2st[:], in_=b2_d).then_inc(sems["wld"], 16)
                sy.dma_start(out=sb2st[0:2, :], in_=sb2_d).then_inc(sems["wld"], 16)
                for p in range(NPASS):
                    if p >= 1:
                        sy.wait_ge(sems["gmm"], B["gmm"] + 160 * p)
                    for dc in range(8):
                        sy.dma_start(out=lnfc[:, dc * 1024:dc * 1024 + 1024],
                                     in_=lnft_d[dc, :, TC * p:TC * p + TC]
                                     ).then_inc(sems["lnf"], 16)
                    for grp in range(NGRP):
                        for el in range(8):
                            ei = eidx(p, grp, el)
                            if ei >= 2:
                                sy.wait_ge(sems["w1c"], B["w1c"] + ei - 1)
                            if grp < 8:
                                e = grp * 8 + el
                                src = w1_d[e].rearrange("(c p) h -> p c h", p=128)
                            else:
                                es2 = grp - 8
                                src = sw1_d[es2].rearrange("(c p) h -> p c h",
                                                           p=128)[:, :,
                                                                  256 * el:
                                                                  256 * el + 256]
                            sy.dma_start(
                                out=w1s[:, (ei % 2) * 2048:(ei % 2) * 2048 + 2048]
                                .rearrange("p (c h) -> p c h", h=HE),
                                in_=src).then_inc(sems["w1ld"], 16)
                        for r in range(16):
                            ri = (p * NGRP + grp) * 16 + r
                            if ri >= 2:
                                sy.wait_ge(sems["w2c"], B["w2c"] + ri - 1)
                            row = grp * 2048 + 128 * r
                            if grp < 8:
                                src = w2_f[row:row + 128, :]
                            else:
                                row = (grp - 8) * 2048 + 128 * r
                                src = sw2_f[row:row + 128, :]
                            sy.dma_start(
                                out=w2s[:, (ri % 2) * 1024:(ri % 2) * 1024 + 1024],
                                in_=src).then_inc(sems["w2ld"], 16)
                    for mt in range(8):
                        oi = p * 8 + mt
                        if oi >= 2:
                            sy.wait_ge(sems["oadd"], B["oadd"] + oi - 1)
                        sy.dma_start(out=x2s3[:, (oi % 2) * D:(oi % 2) * D + D],
                                     in_=x2_d[oi]).then_inc(sems["xr3"], 16)

            @blk.vector
            def _(v):
                v.wait_ge(sems["wld"], B["wld"] + 48)
                vsync(v, v.memset(b2b[:], 0.0))
                v.tensor_copy(b2b[0:64, :], b2st[:])
                v.tensor_copy(b2b[64:66, :], sb2st[0:2, :]).then_inc(sems["wc"], 1)
                for p in range(NPASS):
                    for grp in range(NGRP):
                        for el in range(8):
                            ei = eidx(p, grp, el)
                            v.wait_ge(sems["w1ld"], B["w1ld"] + 16 * (ei + 1))
                            if ei >= 2:
                                v.wait_ge(sems["gmm"], B["gmm"] + 2 * (ei - 1))
                            v.tensor_copy(
                                w1b[:, (ei % 2) * 2048:(ei % 2) * 2048 + 2048],
                                w1s[:, (ei % 2) * 2048:(ei % 2) * 2048 + 2048]
                                ).then_inc(sems["w1c"], 1)
                            if grp < 8:
                                ir = p * 64 + grp * 8 + el
                                for m in range(2):
                                    si = 2 * ei + m
                                    v.wait_ge(sems["silu"], B["silu"] + si + 1)
                                    v.wait_ge(sems["cwb"], B["cwb"] + ir + 1)
                                    if el == 0 and m == 0:
                                        v.wait_ge(sems["w2mm"],
                                                  B["w2mm"] +
                                                  16 * (p * NGRP + grp))
                                    v.tensor_tensor(
                                        Gg[:, (2 * el + m) * 1024:
                                           (2 * el + m) * 1024 + 1024],
                                        stmp[:, m * 1024:m * 1024 + 1024],
                                        cwbp[:], op=OP.mult
                                        ).then_inc(sems["cwu"], 1)
                        for r in range(16):
                            ri = (p * NGRP + grp) * 16 + r
                            v.wait_ge(sems["w2ld"], B["w2ld"] + 16 * (ri + 1))
                            if r == 0:
                                v.wait_ge(sems["w2mm"],
                                          B["w2mm"] + 16 * (p * NGRP + grp))
                            v.tensor_copy(w2b[:, r * 1024:r * 1024 + 1024],
                                          w2s[:, (ri % 2) * 1024:
                                              (ri % 2) * 1024 + 1024]
                                          ).then_inc(sems["w2c"], 1)
                        for mt in range(8):
                            for nh2 in range(2):
                                ep = epidx(p, grp, mt, nh2)
                                v.wait_ge(sems["w2mm"], B["w2mm"] + ep + 1)
                                dst = acc[:, mt * 1024 + 512 * nh2:
                                          mt * 1024 + 512 * nh2 + 512]
                                if grp == 0:
                                    v.tensor_copy(dst, w2p[ep % 2][:]
                                                  ).then_inc(sems["acc"], 1)
                                else:
                                    v.tensor_tensor(dst, dst, w2p[ep % 2][:],
                                                    op=OP.add
                                                    ).then_inc(sems["acc"], 1)
                    for mt in range(8):
                        oi = p * 8 + mt
                        v.wait_ge(sems["acc"], B["acc"] + 160 * (p + 1))
                        v.wait_ge(sems["xr3"], B["xr3"] + 16 * (oi + 1))
                        if oi >= 2:
                            v.wait_ge(sems["ost"], B["ost"] + 16 * (oi - 1))
                        v.tensor_tensor(ostg[:, (oi % 2) * D:(oi % 2) * D + D],
                                        acc[:, mt * 1024:mt * 1024 + 1024],
                                        x2s3[:, (oi % 2) * D:(oi % 2) * D + D],
                                        op=OP.add).then_inc(sems["oadd"], 1)

            @blk.scalar
            def _(s):
                for p in range(NPASS):
                    for grp in range(NGRP):
                        for el in range(8):
                            ei = eidx(p, grp, el)
                            for m in range(2):
                                si = 2 * ei + m
                                s.wait_ge(sems["gmm"], B["gmm"] + si + 1)
                                if grp < 8:
                                    e = grp * 8 + el
                                    bias = b1T[:, 64 * m + e:64 * m + e + 1]
                                    sir = 2 * (p * 64 + grp * 8 + el) + m
                                    if sir >= 2:
                                        s.wait_ge(sems["cwu"],
                                                  B["cwu"] + sir - 1)
                                    s.activation(
                                        stmp[:, m * 1024:m * 1024 + 1024],
                                        gps[m][:], AF.Silu, bias=bias
                                        ).then_inc(sems["silu"], 1)
                                else:
                                    es2 = grp - 8
                                    ht = 2 * el + m
                                    bias = sb1T[:, 2 * ht + es2:2 * ht + es2 + 1]
                                    if el == 0 and m == 0:
                                        s.wait_ge(sems["w2mm"],
                                                  B["w2mm"] +
                                                  16 * (p * NGRP + grp))
                                    s.activation(
                                        Gg[:, (2 * el + m) * 1024:
                                           (2 * el + m) * 1024 + 1024],
                                        gps[m][:], AF.Silu, bias=bias
                                        ).then_inc(sems["silu"], 1)
                    for mt in range(8):
                        oi = p * 8 + mt
                        s.wait_ge(sems["oadd"], B["oadd"] + oi + 1)
                        s.dma_start(out=out_t[oi],
                                    in_=ostg[:, (oi % 2) * D:(oi % 2) * D + D]
                                    ).then_inc(sems["ost"], 16)
                s.wait_ge(sems["ost"], B["ost"] + 16 * GT)

            @blk.tensor
            def _(t):
                t.wait_ge(sems["wc"], B["wc"] + 1)
                for p in range(NPASS):
                    t.wait_ge(sems["lnf"], B["lnf"] + 16 * 8 * (p + 1))
                    for grp in range(NGRP):
                        pg = p * NGRP + grp
                        for el in range(8):
                            ei = eidx(p, grp, el)
                            t.wait_ge(sems["w1c"], B["w1c"] + ei + 1)
                            if grp < 8:
                                e = grp * 8 + el
                                ir = p * 64 + grp * 8 + el
                                if ir >= 1:
                                    t.wait_ge(sems["cwu"], B["cwu"] + 2 * ir)
                                t.matmul(cwbp[:, 0:512],
                                         ohb[:, 128 * e:128 * e + 128],
                                         CWE[0:64, TC * p:TC * p + 512],
                                         start=True, stop=True)
                                t.matmul(cwbp[:, 512:1024],
                                         ohb[:, 128 * e:128 * e + 128],
                                         CWE[0:64, TC * p + 512:TC * p + 1024],
                                         start=True, stop=True
                                         ).then_inc(sems["cwb"], 1)
                            for m in range(2):
                                si = 2 * ei + m
                                if si >= 2:
                                    t.wait_ge(sems["silu"], B["silu"] + si - 1)
                                for dc in range(8):
                                    for n2 in range(2):
                                        inst = t.matmul(
                                            gps[m][:, 512 * n2:512 * n2 + 512],
                                            w1b[:, (ei % 2) * 2048 + HE * dc +
                                                128 * m:
                                                (ei % 2) * 2048 + HE * dc +
                                                128 * m + 128],
                                            lnfc[:, dc * 1024 + 512 * n2:
                                                 dc * 1024 + 512 * n2 + 512],
                                            start=(dc == 0), stop=(dc == 7))
                                inst.then_inc(sems["gmm"], 1)
                        if grp < 8:
                            t.wait_ge(sems["cwu"],
                                      B["cwu"] + 16 * (8 * p + grp + 1))
                        else:
                            t.wait_ge(sems["silu"],
                                      B["silu"] + 16 * (NGRP * p + grp + 1))
                        for mt in range(8):
                            for nh2 in range(2):
                                ep = epidx(p, grp, mt, nh2)
                                if ep >= 2:
                                    t.wait_ge(sems["acc"], B["acc"] + ep - 1)
                                for r in range(16):
                                    t.wait_ge(sems["w2c"],
                                              B["w2c"] + (pg * 16) + r + 1)
                                    inst = t.matmul(
                                        w2p[ep % 2][:],
                                        Gg[:, r * 1024 + 128 * mt:
                                           r * 1024 + 128 * mt + 128],
                                        w2b[:, r * 1024 + 512 * nh2:
                                            r * 1024 + 512 * nh2 + 512],
                                        start=(r == 0),
                                        stop=(r == 15 and grp != NGRP - 1))
                                if grp == NGRP - 1:
                                    inst = t.matmul(
                                        w2p[ep % 2][:],
                                        CWE[:, TC * p + 128 * mt:
                                            TC * p + 128 * mt + 128],
                                        b2b[:, 512 * nh2:512 * nh2 + 512],
                                        start=False, stop=True)
                                inst.then_inc(sems["w2mm"], 1)
            barrier(blk)
        e3.close()

    if "setup_tap" in taps:
        with nc.Block() as blk:
            @blk.scalar
            def _(g):
                base = cnt["ost"]
                for nm, buf in [("gb_t", gb), ("bb_t", bb), ("rbb_t", rbb),
                                ("b1T_t", b1T), ("sb1T_t", sb1T)]:
                    dst = tapout(nm, list(buf.shape), F32)
                    g.dma_start(out=dst, in_=buf[:]).then_inc(sems["ost"], 16)
                    cnt["ost"] += 16
                dst = tapout("lam_t", [1, HD], BF16)
                g.dma_start(out=dst, in_=lam_b[:]).then_inc(sems["ost"], 16)
                cnt["ost"] += 16
                g.wait_ge(sems["ost"], cnt["ost"])
            barrier(blk)

    if "p1" in phases:
        emit_p1(0)
        if "p2" in phases:
            emit_p2(0)
        emit_p1(1)
        if "p2" in phases:
            emit_p2(1)
    e12.close()
    if "p3" in phases:
        emit_p3()
    eg.close()
    return nc, tap_out


def _consts():
    bf = ml_dtypes.bfloat16
    ident = np.eye(128, dtype=np.float32)
    ones = np.ones((1, 128), dtype=np.float32)
    oh = np.zeros((E, E * 128), dtype=np.float32)
    for e in range(E):
        oh[e, 128 * e:128 * e + 128] = 1.0
    return {
        "id_bf": ident.astype(bf), "id_f32": ident,
        "ones_f32": ones, "ones_bf": ones.astype(bf),
        "oh_bf": oh.astype(bf),
    }


def _weights(inputs):
    w = {
        "wq": inputs["L0_attn_Wq"], "wk": inputs["L0_attn_Wk"],
        "wv": inputs["L0_attn_Wv"], "wo": inputs["L0_attn_Wo"],
        "lq1": inputs["L0_attn_lambda_q1"].reshape(1, HD),
        "lk1": inputs["L0_attn_lambda_k1"].reshape(1, HD),
        "lq2": inputs["L0_attn_lambda_q2"].reshape(1, HD),
        "lk2": inputs["L0_attn_lambda_k2"].reshape(1, HD),
        "gam": inputs["L0_attn_gamma"].reshape(1, D),
        "bet": inputs["L0_attn_beta"].reshape(1, D),
        "rw": inputs["L0_ffn_router_w"],
        "rb": inputs["L0_ffn_router_b"].reshape(1, E),
        "w1": inputs["L0_ffn_w1"], "b1": inputs["L0_ffn_b1"],
        "w2": inputs["L0_ffn_w2"], "b2": inputs["L0_ffn_b2"],
        "sw1": inputs["L0_ffn_sw1"], "sb1": inputs["L0_ffn_sb1"],
        "sw2": inputs["L0_ffn_sw2"], "sb2": inputs["L0_ffn_sb2"],
    }
    return {k: np.ascontiguousarray(v, dtype=np.float32) for k, v in w.items()}


def kernel(**inputs):
    x = np.ascontiguousarray(inputs["L0_attn_x"], dtype=np.float32)
    consts = _consts()
    weights = _weights(inputs)
    nc, _ = build_nc()
    in_maps = []
    for c in range(8):
        m = dict(weights)
        m.update(consts)
        m["x_sh"] = np.ascontiguousarray(x[2 * c:2 * c + 2])
        in_maps.append(m)
    res = run_bass_kernel_spmd(nc, in_maps, core_ids=list(range(8)))
    return np.concatenate([r["out"] for r in res.results], axis=0)


# version 02
# speedup vs baseline: 1.1457x; 1.1457x over previous
import math
import numpy as np
import ml_dtypes
import concourse.bass as bass
import concourse.mybir as mybir
from concourse.alu_op_type import AluOpType as OP
from concourse.bass_utils import run_bass_kernel_spmd
from contextlib import ExitStack

F32 = mybir.dt.float32
BF16 = mybir.dt.bfloat16
AF = mybir.ActivationFunctionType
AX = mybir.AxisListType

NB, S, D = 2, 2048, 1024
NH, HD = 16, 64
E, HE = 64, 256
ES, HS = 2, 2048
TT = S // 128
GT = NB * TT
LAMBDA_INIT = 0.8 - 0.6 * math.exp(-0.3 * 11.0)
LN_EPS = 1e-5
NPASS = 4
TC = (NB * S) // NPASS
NGRP = 10

PHASES = ("p1", "p2", "p3")
TAPS = ()
P1_STAGES = 4
P1_RAW = False
P1_KN = 16
P2_STAGES = 2
P2_NOX2 = False


def build_nc(phases=PHASES, taps=TAPS):
    nc = bass.Bass("TRN2", target_bir_lowering=False,
                   dynamic_dma_scratch_size=2048,
                   detect_race_conditions=False)
    eg = ExitStack()
    tap_out = {}

    def din(name, shape, dt=F32):
        return nc.dram_tensor(name, list(shape), dt, kind="ExternalInput").ap()

    x_in = din("x_sh", [NB, S, D])
    wq_d = din("wq", [D, 2 * D]); wk_d = din("wk", [D, 2 * D])
    wv_d = din("wv", [D, D]); wo_d = din("wo", [D, D])
    lq1 = din("lq1", [1, HD]); lk1 = din("lk1", [1, HD])
    lq2 = din("lq2", [1, HD]); lk2 = din("lk2", [1, HD])
    gam = din("gam", [1, D]); bet = din("bet", [1, D])
    rw_d = din("rw", [D, E]); rb_d = din("rb", [1, E])
    w1_d = din("w1", [E, D, HE]); b1_d = din("b1", [E, HE])
    w2_d = din("w2", [E, HE, D]); b2_d = din("b2", [E, D])
    sw1_d = din("sw1", [ES, D, HS]); sb1_d = din("sb1", [ES, HS])
    sw2_d = din("sw2", [ES, HS, D]); sb2_d = din("sb2", [ES, D])
    id_bf = din("id_bf", [128, 128], BF16)
    id_f = din("id_f32", [128, 128])
    ones_f = din("ones_f32", [1, 128])
    ones_b = din("ones_bf", [1, 128], BF16)
    oh_d = din("oh_bf", [E, E * 128], BF16)

    out = nc.dram_tensor("out", [NB, S, D], F32, kind="ExternalOutput").ap()
    out_t = out.rearrange("b (t p) d -> (b t) p d", p=128)

    def dscr(name, shape, dt):
        kind = "ExternalOutput" if name in taps else "Internal"
        return nc.dram_tensor(name, list(shape), dt, kind=kind).ap()

    qt_d = dscr("qt_d", [NB, TT, 128, S], BF16)
    x2_d = dscr("x2_d", [GT, 128, D], F32)
    lnft_d = dscr("lnft_d", [8, 128, NB * S], BF16)

    x_t = x_in.rearrange("b (t p) d -> b t p d", p=128)
    w2_f = w2_d.rearrange("e h d -> (e h) d")
    sw2_f = sw2_d.rearrange("e h d -> (e h) d")

    def tapout(name, shape, dt):
        ap = nc.dram_tensor(name, list(shape), dt, kind="ExternalOutput").ap()
        tap_out[name] = name
        return ap

    def sbg(name, shape, dt=F32):
        return eg.enter_context(nc.sbuf_tensor(name, list(shape), dt))

    c_idb = sbg("c_idb", [128, 128], BF16)
    c_idf = sbg("c_idf", [128, 128], F32)
    c_onf = sbg("c_onf", [1, 128], F32)
    c_onb = sbg("c_onb", [1, 128], BF16)
    c_eps = sbg("c_eps", [128, 1], F32)
    c_m8 = sbg("c_m8", [128, 1], F32)
    lamv = sbg("lamv", [1, 1], F32)
    lam_b = sbg("lam_b", [1, HD], BF16)
    gb = sbg("gb", [128, D], F32)
    bb = sbg("bb", [128, D], F32)
    rbb = sbg("rbb", [128, E], F32)
    b1T = sbg("b1T", [128, 128], F32)
    sb1T = sbg("sb1T", [128, 32], F32)
    lam64 = sbg("lam64", [64, 1], F32)

    from collections import defaultdict
    cnt = defaultdict(int)
    sems = {}

    def S_(name):
        if name not in sems:
            sems[name] = eg.enter_context(nc.semaphore(name))
        return sems[name]

    for n in ("bar ld misc "
              "x sq var r mbn apply tr trev "
              "wld wc kmm kev qmm qev qst vmm vev "
              "qt mmA exp pv rb rbmm rbe comb womm wored wosq wovar wor "
              "wombn woap xr x2c x2st "
              "rt3 cw1 cw2 cw3 cw4 lnst lnld ln3 xr3 "
              "lnf w1ld w1c gmm silu cwb cwu w2ld w2c w2mm acc oadd ost hz hz2 hz3 oev"
              ).split():
        S_(n)

    bar_n = [0]
    hz2n = [0]
    hz3n = [0]

    def ssync(s, inst):
        hz2n[0] += 1
        inst.then_inc(sems["hz2"], 1)
        s.wait_ge(sems["hz2"], hz2n[0])

    def vsync(v, inst):
        hz3n[0] += 1
        inst.then_inc(sems["hz3"], 1)
        v.wait_ge(sems["hz3"], hz3n[0])

    def barrier(blk):
        bar_n[0] += 5
        tgt = bar_n[0]
        for en in ("tensor", "vector", "scalar", "gpsimd", "sync"):
            def mk(eng):
                eng.nop().then_inc(sems["bar"], 1)
                eng.wait_ge(sems["bar"], tgt)
            getattr(blk, en)(mk)

    with nc.Block() as blk, ExitStack() as ps:
        pA = ps.enter_context(nc.psum_tensor("pA", [128, 512], F32))
        pB = ps.enter_context(nc.psum_tensor("pB", [128, 512], F32))
        pC = ps.enter_context(nc.psum_tensor("pC", [128, 512], F32))
        smv = ps.enter_context(nc.sbuf_tensor("smv", [1, 8], F32))
        lrow = ps.enter_context(nc.sbuf_tensor("lrow", [1, 4 * HD], F32))
        srow = ps.enter_context(nc.sbuf_tensor("srow", [1, D], F32))
        b1s = ps.enter_context(nc.sbuf_tensor("b1s", [64, HE], F32))
        sb1s = ps.enter_context(nc.sbuf_tensor("sb1s", [2, HS], F32))

        @blk.sync
        def _(sy):
            loads = [(c_idb[:], id_bf), (c_idf[:], id_f), (c_onf[:], ones_f),
                     (c_onb[:], ones_b), (b1s[:], b1_d),
                     (sb1s[:], sb1_d),
                     (lrow[:, 0:HD], lq1), (lrow[:, HD:2 * HD], lk1),
                     (lrow[:, 2 * HD:3 * HD], lq2), (lrow[:, 3 * HD:], lk2)]
            for dst, src in loads:
                sy.dma_start(out=dst, in_=src).then_inc(sems["ld"], 16)
            sy.dma_start(out=srow[:, 0:D], in_=gam).then_inc(sems["ld"], 16)
        n_ld0 = 11

        @blk.vector
        def _(v):
            v.wait_ge(sems["ld"], 16 * n_ld0)
            v.memset(c_eps[:], LN_EPS)
            v.memset(c_m8[:], -8.0)
            vsync(v, v.tensor_tensor(lrow[:, 0:HD], lrow[:, 0:HD],
                                     lrow[:, HD:2 * HD], op=OP.mult))
            vsync(v, v.tensor_tensor(lrow[:, 2 * HD:3 * HD],
                                     lrow[:, 2 * HD:3 * HD],
                                     lrow[:, 3 * HD:], op=OP.mult))
            v.reduce_sum(smv[:, 0:1], lrow[:, 0:HD], axis=AX.X)
            v.reduce_sum(smv[:, 1:2], lrow[:, 2 * HD:3 * HD],
                         axis=AX.X).then_inc(sems["misc"], 1)
            v.wait_ge(sems["misc"], 2)
            vsync(v, v.tensor_tensor(smv[:, 4:5], smv[:, 2:3], smv[:, 3:4],
                                     op=OP.subtract))
            vsync(v, v.tensor_scalar(lamv[:], smv[:, 4:5], LAMBDA_INIT, None,
                                     op0=OP.add))
            v.sem_inc(sems["misc"], 1)

        @blk.scalar
        def _(s):
            s.wait_ge(sems["misc"], 1)
            s.activation(smv[:, 2:3], smv[:, 0:1], AF.Exp)
            s.activation(smv[:, 3:4], smv[:, 1:2],
                         AF.Exp).then_inc(sems["misc"], 1)

        @blk.tensor
        def _(t):
            t.wait_ge(sems["misc"], 3)
            t.wait_ge(sems["ld"], 16 * n_ld0)
            for i in range(23):
                pt = pA if i % 2 == 0 else pB
                if i >= 2:
                    t.wait_ge(sems["misc"], 3 + i - 1)
                if i < 5:
                    t.wait_ge(sems["ld"], 16 * (n_ld0 + min(i // 2, 2)))
                    if i < 2:
                        rhs = srow[:, i * 512:i * 512 + 512]
                    elif i < 4:
                        rhs = srow[:, (i - 2) * 512:(i - 2) * 512 + 512]
                    else:
                        rhs = srow[:, 0:E]
                    t.matmul(pt[:, 0:rhs.shape[1]], c_onf[:], rhs,
                             start=True, stop=True).then_inc(sems["mbn"], 1)
                elif i < 7:
                    m = i - 5
                    if m == 0:
                        t.matmul(pC[0:64, 0:1], c_onf[0:1, 0:64], lamv[:],
                                 start=True, stop=True)
                    t.transpose(pt[:, 0:64], b1s[0:64, 128 * m:128 * m + 128],
                                c_idf[0:64, 0:64]).then_inc(sems["mbn"], 1)
                else:
                    j = i - 7
                    t.transpose(pt[:, 0:2], sb1s[0:2, 128 * j:128 * j + 128],
                                c_idf[0:2, 0:2]).then_inc(sems["mbn"], 1)

        @blk.vector
        def _(v):
            for i in range(23):
                pt = pA if i % 2 == 0 else pB
                v.wait_ge(sems["mbn"], i + 1)
                if i < 2:
                    v.tensor_copy(gb[:, i * 512:i * 512 + 512], pt[:, 0:512])
                elif i < 4:
                    v.tensor_copy(bb[:, (i - 2) * 512:(i - 2) * 512 + 512],
                                  pt[:, 0:512])
                elif i == 4:
                    v.tensor_copy(rbb[:], pt[:, 0:E])
                elif i < 7:
                    if i == 5:
                        v.tensor_copy(lam64[:], pC[0:64, 0:1])
                    v.tensor_copy(b1T[:, 64 * (i - 5):64 * (i - 5) + 64],
                                  pt[:, 0:64])
                else:
                    j = i - 7
                    v.tensor_copy(sb1T[:, 2 * j:2 * j + 2], pt[:, 0:2])
                v.sem_inc(sems["misc"], 1)

        @blk.sync
        def _(sy):
            sy.wait_ge(sems["misc"], 5)
            sy.dma_start(out=srow[:, 0:D], in_=bet).then_inc(sems["ld"], 16)
            sy.wait_ge(sems["misc"], 7)
            sy.dma_start(out=srow[:, 0:E], in_=rb_d).then_inc(sems["ld"], 16)
        barrier(blk)
    cnt["ld"] = 16 * (n_ld0 + 2)
    cnt["mbn"] = 23
    cnt["misc"] = 26

    e12 = ExitStack()
    if "p1" in phases:
        def sb12(name, shape, dt):
            return e12.enter_context(nc.sbuf_tensor(name, list(shape), dt))
        KT = sb12("KT", [128, TT * S], BF16)
        Vaug = sb12("Vaug", [128, TT * 1040], BF16)
        xstg = sb12("xstg", [128, 2 * D], F32)
        sqt = sb12("sqt", [128, D], F32)
        st2 = sb12("st2", [128, 16], F32)
        sdum = sb12("sdum", [128, 1], F32)

    def emit_p1(b):
        B = defaultdict(int, cnt)
        with nc.Block() as blk, ExitStack() as ps:
            dbg = (ps.enter_context(nc.sbuf_tensor(f"dbg{b}", [128, 96], F32))
                   if f"dbg_{b}" in taps else None)
            qk = [ps.enter_context(nc.psum_tensor(f"qk{b}_{i}", [128, 512], F32))
                  for i in range(4)]
            trp = [ps.enter_context(nc.psum_tensor(f"tp{b}_{i}", [128, 1024], BF16))
                   for i in range(2)]
            ln1T = ps.enter_context(nc.sbuf_tensor(f"ln1T{b}", [128, 8 * S], BF16))
            wbuf = ps.enter_context(nc.sbuf_tensor(f"wbuf{b}", [128, 8 * 2048], BF16))
            wstg = ps.enter_context(nc.sbuf_tensor(f"wstg{b}", [128, 2 * 2048], F32))
            ln1b = ps.enter_context(nc.sbuf_tensor(f"ln1b{b}", [128, D], BF16))
            qstg = ps.enter_context(nc.sbuf_tensor(f"qstg{b}", [128, 2 * 2048], BF16))

            @blk.sync
            def _(sy):
                for tt in range(TT):
                    if tt >= 2:
                        sy.wait_ge(sems["apply"], B["apply"] + tt - 1)
                    sy.dma_start(out=xstg[:, (tt % 2) * D:(tt % 2) * D + D],
                                 in_=x_t[b, tt]).then_inc(sems["x"], 16)
                for wi, (wap, ncol) in enumerate([(wk_d, 2048), (wq_d, 2048),
                                                  (wv_d, 1024)][:P1_STAGES - 1]):
                    for ch in range(8):
                        gi = wi * 8 + ch
                        if gi >= 2:
                            sy.wait_ge(sems["wc"], B["wc"] + gi - 1)
                        sy.dma_start(
                            out=wstg[:, (gi % 2) * 2048:(gi % 2) * 2048 + ncol],
                            in_=wap[128 * ch:128 * ch + 128, :]
                        ).then_inc(sems["wld"], 16)

            @blk.scalar
            def _(s):
                for tt in range(TT):
                    par = tt % 2
                    s.wait_ge(sems["x"], B["x"] + 16 * (tt + 1))
                    s.activation(sqt[:], xstg[:, par * D:par * D + D], AF.Square,
                                 accum_out=st2[:, 2 + par:3 + par]
                                 ).then_inc(sems["sq"], 1)
                    s.wait_ge(sems["var"], B["var"] + tt + 1)
                    ssync(s, s.activation(st2[:, 12 + par:13 + par],
                                          st2[:, 6 + par:7 + par], AF.Ln,
                                          bias=c_eps[:, 0:1]))
                    s.activation(st2[:, 8 + par:9 + par],
                                 st2[:, 12 + par:13 + par], AF.Exp,
                                 scale=-0.5).then_inc(sems["r"], 1)
                    s.wait_ge(sems["mbn"], B["mbn"] + tt + 1)
                    if tt >= 1:
                        s.wait_ge(sems["tr"], B["tr"] + 8 * tt)
                    if P1_RAW:
                        s.activation(ln1b[:], xstg[:, par * D:par * D + D],
                                     AF.Identity, bias=c_eps[:, 0:1],
                                     scale=1.0).then_inc(sems["apply"], 1)
                    else:
                        s.activation(ln1b[:], xstg[:, par * D:par * D + D],
                                     AF.Identity, bias=st2[:, 10 + par:11 + par],
                                     scale=st2[:, 8 + par:9 + par]
                                     ).then_inc(sems["apply"], 1)
                for ft in range(TT if P1_STAGES >= 3 else 0):
                    par = ft % 2
                    s.wait_ge(sems["qev"], B["qev"] + 4 * (ft + 1))
                    s.dma_start(out=qt_d[b, ft],
                                in_=qstg[:, par * 2048:par * 2048 + 2048]
                                ).then_inc(sems["qst"], 16)
                if P1_STAGES >= 3:
                    s.wait_ge(sems["qst"], B["qst"] + 16 * TT)

            @blk.vector
            def _(v):
                for tt in range(TT):
                    par = tt % 2
                    v.wait_ge(sems["x"], B["x"] + 16 * (tt + 1))
                    vsync(v, v.reduce_sum(st2[:, par:par + 1],
                                          xstg[:, par * D:par * D + D],
                                          axis=AX.X))
                    v.wait_ge(sems["sq"], B["sq"] + tt + 1)
                    vsync(v, v.tensor_scalar(st2[:, 4 + par:5 + par],
                                             st2[:, par:par + 1],
                                             1.0 / D, None, op0=OP.mult))
                    v.tensor_scalar(st2[:, 14 + par:15 + par],
                                    st2[:, 2 + par:3 + par], 1.0 / D, None,
                                    op0=OP.mult)
                    vsync(v, v.tensor_tensor(st2[:, 12 + par:13 + par],
                                             st2[:, 4 + par:5 + par],
                                             st2[:, 4 + par:5 + par],
                                             op=OP.mult))
                    v.tensor_tensor(st2[:, 6 + par:7 + par],
                                    st2[:, 14 + par:15 + par],
                                    st2[:, 12 + par:13 + par],
                                    op=OP.subtract).then_inc(sems["var"], 1)
                    v.wait_ge(sems["r"], B["r"] + tt + 1)
                    v.scalar_tensor_tensor(st2[:, 10 + par:11 + par],
                                           st2[:, 4 + par:5 + par], -1.0,
                                           st2[:, 8 + par:9 + par],
                                           op0=OP.mult, op1=OP.mult
                                           ).then_inc(sems["mbn"], 1)
                    if dbg is not None:
                        v.tensor_copy(dbg[:, 6 * tt:6 * tt + 1],
                                      st2[:, 4 + par:5 + par])
                        v.tensor_copy(dbg[:, 6 * tt + 1:6 * tt + 2],
                                      st2[:, 6 + par:7 + par])
                        v.tensor_copy(dbg[:, 6 * tt + 2:6 * tt + 3],
                                      st2[:, 8 + par:9 + par])
                        v.tensor_copy(dbg[:, 6 * tt + 3:6 * tt + 4],
                                      st2[:, 10 + par:11 + par])
                        v.tensor_copy(dbg[:, 6 * tt + 4:6 * tt + 5],
                                      st2[:, par:par + 1])
                        v.tensor_copy(dbg[:, 6 * tt + 5:6 * tt + 6],
                                      st2[:, 2 + par:3 + par])
                    for dc in range(8):
                        v.wait_ge(sems["tr"], B["tr"] + 8 * tt + dc + 1)
                        v.tensor_copy(
                            ln1T[:, dc * S + 128 * tt:dc * S + 128 * tt + 128],
                            trp[dc % 2][:, 0:128]).then_inc(sems["trev"], 1)
                for ch in range(8 if P1_STAGES >= 2 else 0):
                    v.wait_ge(sems["wld"], B["wld"] + 16 * (ch + 1))
                    v.tensor_copy(wbuf[:, ch * 2048:ch * 2048 + 2048],
                                  wstg[:, (ch % 2) * 2048:(ch % 2) * 2048 + 2048]
                                  ).then_inc(sems["wc"], 1)
                for ft in range(P1_KN if P1_STAGES >= 2 else 0):
                    v.wait_ge(sems["kmm"], B["kmm"] + ft + 1)
                    for n in range(4):
                        v.tensor_copy(
                            KT[:, ft * S + 512 * n:ft * S + 512 * n + 512],
                            qk[n][:]).then_inc(sems["kev"], 1)
                if P1_STAGES >= 3:
                    v.wait_ge(sems["kmm"], B["kmm"] + TT)
                for ch in range(8 if P1_STAGES >= 3 else 0):
                    gi = 8 + ch
                    v.wait_ge(sems["wld"], B["wld"] + 16 * (gi + 1))
                    v.tensor_copy(wbuf[:, ch * 2048:ch * 2048 + 2048],
                                  wstg[:, (gi % 2) * 2048:(gi % 2) * 2048 + 2048]
                                  ).then_inc(sems["wc"], 1)
                for ft in range(TT if P1_STAGES >= 3 else 0):
                    par = ft % 2
                    v.wait_ge(sems["qmm"], B["qmm"] + ft + 1)
                    if ft >= 2:
                        v.wait_ge(sems["qst"], B["qst"] + 16 * (ft - 1))
                    for n in range(4):
                        v.tensor_copy(
                            qstg[:, par * 2048 + 512 * n:
                                 par * 2048 + 512 * n + 512],
                            qk[n][:]).then_inc(sems["qev"], 1)
                if P1_STAGES >= 4:
                    v.wait_ge(sems["qmm"], B["qmm"] + TT)
                for ch in range(8 if P1_STAGES >= 4 else 0):
                    gi = 16 + ch
                    v.wait_ge(sems["wld"], B["wld"] + 16 * (gi + 1))
                    v.tensor_copy(wbuf[:, ch * 2048:ch * 2048 + 1024],
                                  wstg[:, (gi % 2) * 2048:(gi % 2) * 2048 + 1024]
                                  ).then_inc(sems["wc"], 1)
                if b == 0 and P1_STAGES >= 4:
                    va3 = Vaug[:].rearrange("p (t c) -> p t c", c=65)
                    v.memset(va3[:, :, 64:65], 1.0)
                v.sem_inc(sems["vev"], 1)
                for tt in range(TT if P1_STAGES >= 4 else 0):
                    for vc in range(2):
                        v.wait_ge(sems["vmm"], B["vmm"] + tt + 1)
                        dst = Vaug[:, tt * 1040 + 520 * vc:
                                   tt * 1040 + 520 * vc + 520]
                        dst = dst.rearrange("p (h c) -> p h c", c=65)[:, :, 0:64]
                        src = qk[vc][:].rearrange("p (h c) -> p h c", c=64)
                        v.tensor_copy(dst, src).then_inc(sems["vev"], 1)

            @blk.tensor
            def _(t):
                for tt in range(TT):
                    t.wait_ge(sems["apply"], B["apply"] + tt + 1)
                    for dc in range(8):
                        gi = 8 * tt + dc
                        if gi >= 2:
                            t.wait_ge(sems["trev"], B["trev"] + gi - 1)
                        t.transpose(trp[dc % 2][:, 0:128],
                                    ln1b[:, 128 * dc:128 * dc + 128],
                                    c_idb[:]).then_inc(sems["tr"], 1)
                t.wait_ge(sems["trev"], B["trev"] + 128)
                if P1_STAGES >= 2:
                    t.wait_ge(sems["wc"], B["wc"] + 8)
                for ft in range(P1_KN if P1_STAGES >= 2 else 0):
                    if ft >= 1:
                        t.wait_ge(sems["kev"], B["kev"] + 4 * ft)
                    for dc in range(8):
                        for n in range(4):
                            inst = t.matmul(qk[n][:],
                                            wbuf[:, dc * 2048 + 128 * ft:
                                                 dc * 2048 + 128 * ft + 128],
                                            ln1T[:, dc * S + 512 * n:
                                                 dc * S + 512 * n + 512],
                                            start=(dc == 0), stop=(dc == 7))
                    inst.then_inc(sems["kmm"], 1)
                if P1_STAGES >= 3:
                    t.wait_ge(sems["wc"], B["wc"] + 16)
                    t.wait_ge(sems["kev"], B["kev"] + 4 * TT)
                for ft in range(TT if P1_STAGES >= 3 else 0):
                    if ft >= 1:
                        t.wait_ge(sems["qev"], B["qev"] + 4 * ft)
                    for dc in range(8):
                        for n in range(4):
                            inst = t.matmul(qk[n][:],
                                            wbuf[:, dc * 2048 + 128 * ft:
                                                 dc * 2048 + 128 * ft + 128],
                                            ln1T[:, dc * S + 512 * n:
                                                 dc * S + 512 * n + 512],
                                            start=(dc == 0), stop=(dc == 7))
                    inst.then_inc(sems["qmm"], 1)
                if P1_STAGES >= 4:
                    t.wait_ge(sems["wc"], B["wc"] + 24)
                    t.wait_ge(sems["qev"], B["qev"] + 4 * TT)
                for tt in range(TT if P1_STAGES >= 4 else 0):
                    if tt >= 1:
                        t.wait_ge(sems["vev"], B["vev"] + 2 * tt + 1)
                    for dc in range(8):
                        for vc in range(2):
                            inst = t.matmul(qk[vc][:],
                                            ln1T[:, dc * S + 128 * tt:
                                                 dc * S + 128 * tt + 128],
                                            wbuf[:, dc * 2048 + 512 * vc:
                                                 dc * 2048 + 512 * vc + 512],
                                            start=(dc == 0), stop=(dc == 7))
                    inst.then_inc(sems["vmm"], 1)

            tp = [(f"ln1T_{b}", ln1T, "trev", B["trev"] + 128),
                  (f"KT_{b}", KT, "kev", B["kev"] + 4 * P1_KN),
                  (f"Vaug_{b}", Vaug, "vev", B["vev"] + 2 * TT + 1)]
            if dbg is not None:
                tp.append((f"dbg_{b}", dbg, "mbn", B["mbn"] + TT))
            tp = [x for x in tp if x[0] in taps]
            if tp:
                @blk.scalar
                def _(g):
                    for name, buf, sname, val in tp:
                        dst = tapout(name, list(buf.shape),
                                     F32 if name.startswith("dbg") else BF16)
                        g.wait_ge(sems[sname], val)
                        g.dma_start(out=dst, in_=buf[:]).then_inc(sems["ost"], 16)
                        cnt["ost"] += 16
                    g.wait_ge(sems["ost"], cnt["ost"])
            barrier(blk)
        ns = P1_STAGES
        nw = [0, 0, 8, 16, 24][ns]
        for k, d in [("x", 16 * TT), ("sq", TT), ("var", TT), ("r", TT),
                     ("mbn", TT), ("apply", TT), ("tr", 8 * TT),
                     ("trev", 8 * TT), ("wld", 16 * nw), ("wc", nw),
                     ("kmm", P1_KN if ns >= 2 else 0),
                     ("kev", 4 * P1_KN if ns >= 2 else 0),
                     ("qmm", TT if ns >= 3 else 0),
                     ("qev", 4 * TT if ns >= 3 else 0),
                     ("qst", 16 * TT if ns >= 3 else 0),
                     ("vmm", TT if ns >= 4 else 0),
                     ("vev", (2 * TT + 1) if ns >= 4 else 0)]:
            cnt[k] = B[k] + d

    def emit_p2(b):
        B = defaultdict(int, cnt)
        with nc.Block() as blk, ExitStack() as ps:
            apm = [ps.enter_context(nc.psum_tensor(f"ap{b}_{i}", [128, 1024], F32))
                   for i in range(2)]
            o12 = ps.enter_context(nc.psum_tensor(f"o12_{b}", [65, 1024], F32))
            mix = ps.enter_context(nc.psum_tensor(f"mx_{b}", [128, 1024], F32))
            wob = ps.enter_context(nc.sbuf_tensor(f"wob{b}", [128, 8 * D], BF16))
            qtq = ps.enter_context(nc.sbuf_tensor(f"qtq{b}", [128, 2 * NH * 512], BF16))
            pbuf = ps.enter_context(nc.sbuf_tensor(f"pbuf{b}", [128, 2 * 1024], BF16))
            wstg = ps.enter_context(nc.sbuf_tensor(f"wst2{b}", [128, 1024], F32))
            rt12 = ps.enter_context(nc.sbuf_tensor(f"rt12{b}", [1, 1024], F32))
            rb12 = ps.enter_context(nc.sbuf_tensor(f"rb12{b}", [1, 2 * 1024], BF16))
            rbs1 = ps.enter_context(nc.sbuf_tensor(f"rbs1{b}", [64, 512], F32))
            rbs2 = ps.enter_context(nc.sbuf_tensor(f"rbs2{b}", [64, 512], F32))
            cb1 = ps.enter_context(nc.sbuf_tensor(f"cb1{b}", [64, 512], F32))
            cb2 = ps.enter_context(nc.sbuf_tensor(f"cb2{b}", [64, 512], F32))
            aoT = ps.enter_context(nc.sbuf_tensor(f"aoT{b}", [128, 8 * 512], BF16))
            tsb = ps.enter_context(nc.sbuf_tensor(f"tsb{b}", [128, D], F32))
            x2s = ps.enter_context(nc.sbuf_tensor(f"x2s{b}", [128, 2 * D], F32))
            oev = ps.enter_context(nc.sbuf_tensor(f"oev{b}", [65, 2 * 1024], F32))

            @blk.sync
            def _(sy):
                for ch in range(8):
                    if ch >= 1:
                        sy.wait_ge(sems["wc"], B["wc"] + ch)
                    sy.dma_start(
                        out=wstg[:, 0:1024],
                        in_=wo_d[128 * ch:128 * ch + 128, :]
                        ).then_inc(sems["wld"], 16)
                sy.wait_ge(sems["qst"], 16 * TT * (b + 1))
                for qb in range(4):
                    if qb >= 2:
                        sy.wait_ge(sems["mmA"], B["mmA"] + (qb - 1) * 256)
                    for ft in range(TT):
                        sy.dma_start(
                            out=qtq[:, (qb % 2) * 8192 + ft * 512:
                                    (qb % 2) * 8192 + ft * 512 + 512],
                            in_=qt_d[b, ft, :, 512 * qb:512 * qb + 512]
                            ).then_inc(sems["qt"], 16)
                    for tt in range(4 if (P2_STAGES >= 2 and not P2_NOX2) else 0):
                        wt = qb * 4 + tt
                        if wt >= 2:
                            sy.wait_ge(sems["x2c"], B["x2c"] + wt - 1)
                        sy.dma_start(out=xstg[:, (wt % 2) * D:(wt % 2) * D + D],
                                     in_=x_t[b, wt]).then_inc(sems["xr"], 16)
                    for tt in range(4 if (P2_STAGES >= 2 and not P2_NOX2) else 0):
                        wt = qb * 4 + tt
                        sy.wait_ge(sems["x2c"], B["x2c"] + wt + 1)
                        sy.dma_start(out=x2_d[b * TT + wt],
                                     in_=x2s[:, (wt % 2) * D:(wt % 2) * D + D]
                                     ).then_inc(sems["x2st"], 16)
                if qb == 3 and P2_STAGES >= 2 and not P2_NOX2:
                    sy.wait_ge(sems["x2st"], B["x2st"] + 16 * 16)

            @blk.vector
            def _(v):
                for ch in range(8):
                    v.wait_ge(sems["wld"], B["wld"] + 16 * (ch + 1))
                    v.tensor_copy(wob[:, ch * D:ch * D + D],
                                  wstg[:, 0:1024]).then_inc(sems["wc"], 1)
                def vslot(qb2, hh):
                    if hh < NH:
                        qh2 = qb2 * NH + hh
                        v.wait_ge(sems["pv"], B["pv"] + (qh2 + 1) * TT)
                        if qh2 >= 2:
                            v.wait_ge(sems["comb"], B["comb"] + qh2 - 1)
                        vsync(v, v.tensor_copy(
                            oev[:, (qh2 % 2) * 1024:(qh2 % 2) * 1024 + 1024],
                            o12[:]))
                        v.sem_inc(sems["oev"], 1)
                    hp = hh - 1
                    if hp < 0:
                        return
                    qh2 = qb2 * NH + hp
                    par = qh2 % 2
                    v.wait_ge(sems["rbmm"], B["rbmm"] + qh2 + 1)
                    if qh2 >= 1:
                        v.wait_ge(sems["comb"], B["comb"] + qh2)
                    v.tensor_copy(rbs1[:], mix[0:64, 0:512])
                    vsync(v, v.tensor_copy(rbs2[:], mix[64:128, 0:512]))
                    v.sem_inc(sems["rbe"], 1)
                    if P2_STAGES >= 2 and qh2 >= 1:
                        v.wait_ge(sems["womm"], B["womm"] + 4 * qb2)
                    vsync(v, v.tensor_tensor(cb1[:],
                                             oev[0:64, par * 1024:par * 1024 + 512],
                                             rbs1[:], op=OP.mult))
                    vsync(v, v.scalar_tensor_tensor(
                        cb2[:], oev[0:64, par * 1024 + 512:par * 1024 + 1024],
                        lam64[:, 0:1], rbs2[:], op0=OP.mult, op1=OP.mult))
                    r0 = 64 * (hp % 2)
                    c0 = 512 * (hp // 2)
                    v.tensor_tensor(aoT[r0:r0 + 64, c0:c0 + 512],
                                    cb1[:], cb2[:],
                                    op=OP.subtract).then_inc(sems["comb"], 1)

                for qb in range(4):
                    for h in range(NH):
                        vslot(qb, h)
                    vslot(qb, NH)
                    for tt in range(4 if P2_STAGES >= 2 else 0):
                        wt = qb * 4 + tt
                        par = wt % 2
                        wop = mix if tt % 2 == 0 else apm[0]
                        v.wait_ge(sems["womm"], B["womm"] + wt + 1)
                        vsync(v, v.reduce_sum(st2[:, par:par + 1], wop[:],
                                              axis=AX.X))
                        v.sem_inc(sems["wored"], 1)
                        v.wait_ge(sems["wosq"], B["wosq"] + wt + 1)
                        vsync(v, v.tensor_scalar(st2[:, 4 + par:5 + par],
                                                 st2[:, par:par + 1], 1.0 / D,
                                                 None, op0=OP.mult))
                        v.tensor_scalar(st2[:, 14 + par:15 + par],
                                        st2[:, 2 + par:3 + par], 1.0 / D, None,
                                        op0=OP.mult)
                        vsync(v, v.tensor_tensor(st2[:, 12 + par:13 + par],
                                                 st2[:, 4 + par:5 + par],
                                                 st2[:, 4 + par:5 + par],
                                                 op=OP.mult))
                        v.tensor_tensor(st2[:, 6 + par:7 + par],
                                        st2[:, 14 + par:15 + par],
                                        st2[:, 12 + par:13 + par],
                                        op=OP.subtract).then_inc(sems["wovar"], 1)
                        v.wait_ge(sems["wor"], B["wor"] + wt + 1)
                        v.scalar_tensor_tensor(st2[:, 10 + par:11 + par],
                                               st2[:, 4 + par:5 + par], -1.0,
                                               st2[:, 8 + par:9 + par],
                                               op0=OP.mult, op1=OP.mult
                                               ).then_inc(sems["wombn"], 1)
                        if P2_NOX2:
                            v.sem_inc(sems["x2c"], 1)
                            continue
                        v.wait_ge(sems["woap"], B["woap"] + wt + 1)
                        v.wait_ge(sems["xr"], B["xr"] + 16 * (wt + 1))
                        if wt >= 2:
                            v.wait_ge(sems["x2st"], B["x2st"] + 16 * (wt - 1))
                        vsync(v, v.tensor_tensor(x2s[:, par * D:par * D + D],
                                                 xstg[:, par * D:par * D + D],
                                                 bb[:], op=OP.add))
                        vsync(v, v.tensor_tensor(tsb[:], tsb[:],
                                                 gb[:], op=OP.mult))
                        v.tensor_tensor(x2s[:, par * D:par * D + D],
                                        x2s[:, par * D:par * D + D],
                                        tsb[:],
                                        op=OP.add).then_inc(sems["x2c"], 1)

            @blk.scalar
            def _(s):
                def rchain(qh2):
                    if qh2 < 0:
                        return
                    par = qh2 % 2
                    s.wait_ge(sems["oev"], B["oev"] + qh2 + 1)
                    if qh2 >= 2:
                        s.wait_ge(sems["rbmm"], B["rbmm"] + qh2 - 1)
                    ssync(s, s.activation(rt12[:, 0:1024],
                                          oev[64:65, par * 1024:par * 1024 + 1024],
                                          AF.Ln))
                    s.activation(rb12[:, par * 1024:par * 1024 + 1024],
                                 rt12[:, 0:1024], AF.Exp,
                                 scale=-1.0).then_inc(sems["rb"], 1)

                for qb in range(4):
                    for h in range(NH):
                        qh = qb * NH + h
                        for kt in range(TT):
                            g = qh * TT + kt
                            s.wait_ge(sems["mmA"], B["mmA"] + g + 1)
                            if g >= 2:
                                s.wait_ge(sems["pv"], B["pv"] + g - 1)
                            s.activation(
                                pbuf[:, (g % 2) * 1024:(g % 2) * 1024 + 1024],
                                apm[g % 2][:], AF.Exp, scale=0.125,
                                bias=c_m8[:, 0:1]).then_inc(sems["exp"], 1)
                            if kt == 1 and h > 0:
                                rchain(qh - 1)
                    rchain(qb * NH + NH - 1)
                    for tt in range(4 if P2_STAGES >= 2 else 0):
                        wt = qb * 4 + tt
                        par = wt % 2
                        wop = mix if tt % 2 == 0 else apm[0]
                        s.wait_ge(sems["womm"], B["womm"] + wt + 1)
                        s.wait_ge(sems["wored"], B["wored"] + wt + 1)
                        s.activation(sqt[:], wop[:], AF.Square,
                                     accum_out=st2[:, 2 + par:3 + par]
                                     ).then_inc(sems["wosq"], 1)
                        s.wait_ge(sems["wovar"], B["wovar"] + wt + 1)
                        ssync(s, s.activation(st2[:, 12 + par:13 + par],
                                              st2[:, 6 + par:7 + par], AF.Ln,
                                              bias=c_eps[:, 0:1]))
                        s.activation(st2[:, 8 + par:9 + par],
                                     st2[:, 12 + par:13 + par], AF.Exp,
                                     scale=-0.5).then_inc(sems["wor"], 1)
                        s.wait_ge(sems["wombn"], B["wombn"] + wt + 1)
                        if wt >= 1:
                            s.wait_ge(sems["x2c"], B["x2c"] + wt)
                        s.activation(tsb[:], wop[:],
                                     AF.Identity,
                                     bias=st2[:, 10 + par:11 + par],
                                     scale=st2[:, 8 + par:9 + par]
                                     ).then_inc(sems["woap"], 1)

            @blk.tensor
            def _(t):
                t.wait_ge(sems["wc"], B["wc"] + 8)
                for qb in range(4):
                    t.wait_ge(sems["qt"], B["qt"] + 16 * TT * (qb + 1))
                    if P2_STAGES >= 2:
                        t.wait_ge(sems["wored"], B["wored"] + 4 * qb)
                        t.wait_ge(sems["wosq"], B["wosq"] + 4 * qb)
                        t.wait_ge(sems["woap"], B["woap"] + 4 * qb)
                    def emit_rbmm(hh):
                        if hh < 0:
                            return
                        qhh = qb * NH + hh
                        t.wait_ge(sems["rb"], B["rb"] + qhh + 1)
                        if qhh >= 1:
                            t.wait_ge(sems["rbe"], B["rbe"] + qhh)
                        rbb2 = (qhh % 2) * 1024
                        t.matmul(mix[0:64, 0:512], c_onb[0:1, 0:64],
                                 rb12[0:1, rbb2:rbb2 + 512],
                                 start=True, stop=True, tile_position=(0, 0))
                        t.matmul(mix[64:128, 0:512], c_onb[0:1, 0:64],
                                 rb12[0:1, rbb2 + 512:rbb2 + 1024],
                                 start=True, stop=True, tile_position=(0, 64)
                                 ).then_inc(sems["rbmm"], 1)

                    for h in range(NH):
                        qh = qb * NH + h
                        qtb = (qb % 2) * 8192 + h * 512

                        def emitA(kt):
                            g = qh * TT + kt
                            t.matmul(apm[g % 2][:, 0:512],
                                     KT[0:64, h * S + 128 * kt:
                                        h * S + 128 * kt + 128],
                                     qtq[0:64, qtb:qtb + 512],
                                     start=True, stop=True)
                            t.matmul(apm[g % 2][:, 512:1024],
                                     KT[64:128, h * S + 128 * kt:
                                        h * S + 128 * kt + 128],
                                     qtq[64:128, qtb:qtb + 512],
                                     start=True, stop=True
                                     ).then_inc(sems["mmA"], 1)

                        def emitPV(kt):
                            g = qh * TT + kt
                            t.wait_ge(sems["exp"], B["exp"] + g + 1)
                            if kt == 0:
                                t.wait_ge(sems["oev"], B["oev"] + qh)
                            vb = kt * 1040 + 65 * h
                            pb = (g % 2) * 1024
                            t.matmul(o12[:, 0:512], Vaug[:, vb:vb + 65],
                                     pbuf[:, pb:pb + 512],
                                     start=(kt == 0), stop=(kt == 15))
                            t.matmul(o12[:, 512:1024], Vaug[:, vb:vb + 65],
                                     pbuf[:, pb + 512:pb + 1024],
                                     start=(kt == 0), stop=(kt == 15)
                                     ).then_inc(sems["pv"], 1)

                        emitA(0)
                        emitA(1)
                        for kt in range(TT - 1):
                            emitPV(kt)
                            if kt == 3:
                                emit_rbmm(h - 1)
                            if kt + 2 < TT:
                                emitA(kt + 2)
                        emitPV(TT - 1)
                    emit_rbmm(NH - 1)
                    if P2_STAGES >= 2:
                        t.wait_ge(sems["comb"], B["comb"] + 16 * (qb + 1))
                        t.wait_ge(sems["rbe"], B["rbe"] + 16 * (qb + 1))
                        t.wait_ge(sems["exp"], B["exp"] + 256 * (qb + 1))
                    for tt in range(4 if P2_STAGES >= 2 else 0):
                        wt = qb * 4 + tt
                        wop = mix if tt % 2 == 0 else apm[0]
                        if wt >= 2:
                            t.wait_ge(sems["wored"], B["wored"] + wt - 1)
                            t.wait_ge(sems["wosq"], B["wosq"] + wt - 1)
                            t.wait_ge(sems["woap"], B["woap"] + wt - 1)
                        for dc in range(8):
                            for nh2 in range(2):
                                inst = t.matmul(wop[:, 512 * nh2:512 * nh2 + 512],
                                                aoT[:, dc * 512 + 128 * tt:
                                                    dc * 512 + 128 * tt + 128],
                                                wob[:, dc * D + 512 * nh2:
                                                    dc * D + 512 * nh2 + 512],
                                                start=(dc == 0), stop=(dc == 7))
                        inst.then_inc(sems["womm"], 1)

            if f"aoT_{b}" in taps:
                @blk.gpsimd
                def _(g):
                    dst = tapout(f"aoT_{b}", [128, 8 * 512], BF16)
                    g.wait_ge(sems["comb"], B["comb"] + 64)
                    g.dma_start(out=dst, in_=aoT[:]).then_inc(sems["ost"], 16)
                    cnt["ost"] += 16
                    g.wait_ge(sems["ost"], cnt["ost"])
            barrier(blk)
        s2 = P2_STAGES >= 2
        for k, d in [("wld", 16 * 8), ("wc", 8), ("qt", 16 * TT * 4),
                     ("mmA", 1024), ("exp", 1024), ("pv", 1024),
                     ("rb", 64), ("rbmm", 64), ("rbe", 64), ("comb", 64),
                     ("oev", 64),
                     ("womm", 16 if s2 else 0), ("wored", 16 if s2 else 0),
                     ("wosq", 16 if s2 else 0), ("wovar", 16 if s2 else 0),
                     ("wor", 16 if s2 else 0), ("wombn", 16 if s2 else 0),
                     ("woap", 16 if s2 else 0),
                     ("xr", 16 * 16 if (s2 and not P2_NOX2) else 0),
                     ("x2c", 16 if s2 else 0),
                     ("x2st", 16 * 16 if (s2 and not P2_NOX2) else 0)]:
            cnt[k] = B[k] + d

    def emit_p3():
        e3 = ExitStack()

        def sb3(name, shape, dt):
            return e3.enter_context(nc.sbuf_tensor(name, list(shape), dt))

        CWE = sb3("CWE", [128, NB * S], BF16)

        B = defaultdict(int, cnt)
        with nc.Block() as blk, ExitStack() as ps:
            trp = [ps.enter_context(nc.psum_tensor(f"t3_{i}", [128, 1024], BF16))
                   for i in range(2)]
            lgp = ps.enter_context(nc.psum_tensor("lgp", [128, 512], F32))
            cwp = ps.enter_context(nc.psum_tensor("cwp", [64, 512], F32))
            def sbp(name, shape, dt):
                return ps.enter_context(nc.sbuf_tensor(name, list(shape), dt))
            rwb = sbp("rwb", [128, 8 * E], BF16)
            rws = sbp("rws", [128, 8 * E], F32)
            xst3 = sbp("xst3", [128, 2 * D], F32)
            lnfb = sbp("lnfb", [128, 2 * D], BF16)
            lnfa = sbp("lnfa", [128, D], BF16)
            sq3 = sbp("sq3", [128, D], F32)
            st3 = sbp("st3", [128, 16], F32)
            lgs = sbp("lgs", [128, E], F32)
            rt3t = sbp("rt3t", [128, 4], F32)
            m8 = sbp("m8", [128, 8], F32)
            elx = sbp("elx", [128, E], F32)

            @blk.sync
            def _(sy):
                sy.dma_start(out=rws[:].rearrange("p (c e) -> p c e", e=E),
                             in_=rw_d.rearrange("(c p) e -> p c e", p=128)
                             ).then_inc(sems["wld"], 16)
                sy.wait_ge(sems["x2st"], 16 * GT)
                for gt in range(GT):
                    if gt >= 2:
                        sy.wait_ge(sems["ln3"], B["ln3"] + gt - 1)
                    sy.dma_start(out=xst3[:, (gt % 2) * D:(gt % 2) * D + D],
                                 in_=x2_d[gt]).then_inc(sems["xr3"], 16)

            @blk.vector
            def _(v):
                v.wait_ge(sems["wld"], B["wld"] + 16)
                v.tensor_copy(rwb[:], rws[:]).then_inc(sems["wc"], 1)
                vsync(v, v.memset(CWE[64:128, :], 0.0))
                v.memset(CWE[64:66, :], 1.0)
                v.sem_inc(sems["cw4"], 1)
                for gt in range(GT):
                    par = gt % 2
                    v.wait_ge(sems["xr3"], B["xr3"] + 16 * (gt + 1))
                    vsync(v, v.reduce_sum(st3[:, par:par + 1],
                                          xst3[:, par * D:par * D + D],
                                          axis=AX.X))
                    v.wait_ge(sems["sq"], B["sq"] + gt + 1)
                    vsync(v, v.tensor_scalar(st3[:, 4 + par:5 + par],
                                             st3[:, par:par + 1],
                                             1.0 / D, None, op0=OP.mult))
                    v.tensor_scalar(st3[:, 14 + par:15 + par],
                                    st3[:, 2 + par:3 + par], 1.0 / D, None,
                                    op0=OP.mult)
                    vsync(v, v.tensor_tensor(st3[:, 12 + par:13 + par],
                                             st3[:, 4 + par:5 + par],
                                             st3[:, 4 + par:5 + par],
                                             op=OP.mult))
                    v.tensor_tensor(st3[:, 6 + par:7 + par],
                                    st3[:, 14 + par:15 + par],
                                    st3[:, 12 + par:13 + par],
                                    op=OP.subtract).then_inc(sems["var"], 1)
                    v.wait_ge(sems["r"], B["r"] + gt + 1)
                    v.scalar_tensor_tensor(st3[:, 10 + par:11 + par],
                                           st3[:, 4 + par:5 + par], -1.0,
                                           st3[:, 8 + par:9 + par],
                                           op0=OP.mult, op1=OP.mult
                                           ).then_inc(sems["mbn"], 1)
                    if gt >= 2:
                        v.wait_ge(sems["lnld"], B["lnld"] + 16 * 8 * (gt - 1))
                        v.wait_ge(sems["rt3"], B["rt3"] + gt - 1)
                    for dc in range(8):
                        v.wait_ge(sems["tr"], B["tr"] + 8 * gt + dc + 1)
                        v.tensor_copy(lnfb[:, par * D + 128 * dc:
                                          par * D + 128 * dc + 128],
                                      trp[dc % 2][:, 0:128]).then_inc(sems["trev"], 1)
                    v.wait_ge(sems["rt3"], B["rt3"] + gt + 1)
                    vsync(v, v.tensor_tensor(lgs[:], lgp[:, 0:E], rbb[:],
                                             op=OP.add))
                    vsync(v, v.max(out=m8[:], in_=lgs[:]))
                    v.tensor_scalar(rt3t[:, 0:1], m8[:, 0:1],
                                    -1.0, None, op0=OP.mult
                                    ).then_inc(sems["cw1"], 1)
                    v.wait_ge(sems["cw2"], B["cw2"] + gt + 1)
                    vsync(v, v.tensor_scalar(lgs[:], lgs[:], m8[:, 7:8], None,
                                             op0=OP.is_ge))
                    vsync(v, v.tensor_tensor(elx[:], elx[:], lgs[:],
                                             op=OP.mult))
                    vsync(v, v.reduce_sum(rt3t[:, 1:2], elx[:], axis=AX.X))
                    vsync(v, v.reciprocal(rt3t[:, 2:3], rt3t[:, 1:2]))
                    v.tensor_scalar(elx[:], elx[:], rt3t[:, 2:3], None,
                                    op0=OP.mult).then_inc(sems["cw3"], 1)
                    v.wait_ge(sems["cw4"], B["cw4"] + gt + 2)
                    v.tensor_copy(CWE[0:64, 128 * gt:128 * gt + 128],
                                  cwp[:, 0:128]).then_inc(sems["cwu"], 1)

            @blk.scalar
            def _(s):
                for gt in range(GT):
                    par = gt % 2
                    s.wait_ge(sems["xr3"], B["xr3"] + 16 * (gt + 1))
                    s.activation(sq3[:], xst3[:, par * D:par * D + D], AF.Square,
                                 accum_out=st3[:, 2 + par:3 + par]
                                 ).then_inc(sems["sq"], 1)
                    s.wait_ge(sems["var"], B["var"] + gt + 1)
                    ssync(s, s.activation(st3[:, 12 + par:13 + par],
                                          st3[:, 6 + par:7 + par], AF.Ln,
                                          bias=c_eps[:, 0:1]))
                    s.activation(st3[:, 8 + par:9 + par],
                                 st3[:, 12 + par:13 + par], AF.Exp,
                                 scale=-0.5).then_inc(sems["r"], 1)
                    s.wait_ge(sems["mbn"], B["mbn"] + gt + 1)
                    if gt >= 1:
                        s.wait_ge(sems["tr"], B["tr"] + 8 * gt)
                    s.activation(lnfa[:], xst3[:, par * D:par * D + D],
                                 AF.Identity, bias=st3[:, 10 + par:11 + par],
                                 scale=st3[:, 8 + par:9 + par]
                                 ).then_inc(sems["ln3"], 1)
                    s.wait_ge(sems["cw1"], B["cw1"] + gt + 1)
                    s.wait_ge(sems["cw4"], B["cw4"] + gt + 1)
                    s.activation(elx[:], lgs[:], AF.Exp,
                                 bias=rt3t[:, 0:1]
                                 ).then_inc(sems["cw2"], 1)
                    s.wait_ge(sems["trev"], B["trev"] + 8 * (gt + 1))
                    for dc in range(8):
                        s.dma_start(out=lnft_d[dc, :, 128 * gt:128 * gt + 128],
                                    in_=lnfb[:, par * D + 128 * dc:
                                             par * D + 128 * dc + 128]
                                    ).then_inc(sems["lnld"], 16)
                s.wait_ge(sems["lnld"], B["lnld"] + 16 * 8 * GT)

            @blk.tensor
            def _(t):
                t.wait_ge(sems["wc"], B["wc"] + 1)
                for gt in range(GT):
                    par = gt % 2
                    t.wait_ge(sems["ln3"], B["ln3"] + gt + 1)
                    for dc in range(8):
                        gi = 8 * gt + dc
                        if gi >= 2:
                            t.wait_ge(sems["trev"], B["trev"] + gi - 1)
                        t.transpose(trp[dc % 2][:, 0:128],
                                    lnfa[:, 128 * dc:128 * dc + 128],
                                    c_idb[:]).then_inc(sems["tr"], 1)
                    t.wait_ge(sems["trev"], B["trev"] + 8 * (gt + 1))
                    if gt >= 2:
                        t.wait_ge(sems["lnld"], B["lnld"] + 16 * 8 * (gt - 1))
                    for dc in range(8):
                        inst = t.matmul(lgp[:, 0:E],
                                        lnfb[:, par * D + 128 * dc:
                                             par * D + 128 * dc + 128],
                                        rwb[:, dc * E:dc * E + E],
                                        start=(dc == 0), stop=(dc == 7))
                    inst.then_inc(sems["rt3"], 1)
                    t.wait_ge(sems["cw3"], B["cw3"] + gt + 1)
                    if gt >= 1:
                        t.wait_ge(sems["cwu"], B["cwu"] + gt)
                    t.transpose(cwp[:, 0:128], elx[:, 0:64], c_idf[:]
                                ).then_inc(sems["cw4"], 1)
            barrier(blk)
        for k, d in [("wld", 16), ("wc", 1), ("xr3", 16 * GT), ("sq", GT),
                     ("var", GT), ("r", GT), ("mbn", GT), ("ln3", GT),
                     ("tr", 8 * GT), ("trev", 8 * GT), ("rt3", GT),
                     ("cw1", GT), ("cw2", GT), ("cw3", GT), ("cw4", GT + 1),
                     ("cwu", GT), ("hz", GT), ("lnld", 16 * 8 * GT)]:
            cnt[k] = B[k] + d

        B = defaultdict(int, cnt)
        with nc.Block() as blk, ExitStack() as ps:
            gps = [ps.enter_context(nc.psum_tensor(f"g3_{m}", [128, 1024], F32))
                   for m in range(2)]
            cwbp = ps.enter_context(nc.psum_tensor("cwbp", [128, 1024], F32))
            w2p = [ps.enter_context(nc.psum_tensor(f"w2p{i}", [128, 512], F32))
                   for i in range(2)]
            def sbm(name, shape, dt):
                return ps.enter_context(nc.sbuf_tensor(name, list(shape), dt))
            lnfc = sbm("lnfc", [128, 8 * 1024], BF16)
            Gg = sbm("Gg", [128, 16 * 1024], BF16)
            w2b = sbm("w2b", [128, 16 * 1024], BF16)
            acc = sbm("acc", [128, 8 * 1024], F32)
            ohb = sbm("ohb", [E, E * 128], BF16)
            w1s = sbm("w1s", [128, 2 * 2048], F32)
            w1b = sbm("w1b", [128, 4 * 2048], BF16)
            w2s = sbm("w2s", [128, 2 * 1024], F32)
            stmp = sbm("stmp", [128, 2 * 1024], BF16)
            x2s3 = sbm("x2s3", [128, 2 * D], F32)
            ostg = sbm("ostg", [128, 2 * D], F32)
            b2st = sbm("b2st", [64, D], F32)
            sb2st = sbm("sb2st", [64, D], F32)
            b2b = sbm("b2b", [128, D], BF16)

            def eidx(p, grp, el):
                return (p * NGRP + grp) * 8 + el

            def epidx(p, grp, mt, nh2):
                return ((p * NGRP + grp) * 8 + mt) * 2 + nh2

            @blk.sync
            def _(sy):
                sy.dma_start(out=ohb[:], in_=oh_d).then_inc(sems["wld"], 16)
                sy.dma_start(out=b2st[:], in_=b2_d).then_inc(sems["wld"], 16)
                sy.dma_start(out=sb2st[0:2, :], in_=sb2_d).then_inc(sems["wld"], 16)
                def w1load(pg2, el2):
                    ei = pg2 * 8 + el2
                    p2, grp2 = divmod(pg2, NGRP)
                    if ei >= 2:
                        sy.wait_ge(sems["w1c"], B["w1c"] + ei - 1)
                    if grp2 < 8:
                        e = grp2 * 8 + el2
                        srca = w1_d[e].rearrange("(c p) h -> p c h", p=128)
                    else:
                        es2 = grp2 - 8
                        srca = sw1_d[es2].rearrange("(c p) h -> p c h",
                                                    p=128)[:, :,
                                                           256 * el2:
                                                           256 * el2 + 256]
                    sy.dma_start(
                        out=w1s[:, (ei % 2) * 2048:(ei % 2) * 2048 + 2048]
                        .rearrange("p (c h) -> p c h", h=HE),
                        in_=srca).then_inc(sems["w1ld"], 16)

                def w2load(pg2, r):
                    ri = pg2 * 16 + r
                    p2, grp2 = divmod(pg2, NGRP)
                    if ri >= 2:
                        sy.wait_ge(sems["w2c"], B["w2c"] + ri - 1)
                    if grp2 < 8:
                        srca = w2_f[grp2 * 2048 + 128 * r:
                                    grp2 * 2048 + 128 * r + 128, :]
                    else:
                        srca = sw2_f[(grp2 - 8) * 2048 + 128 * r:
                                     (grp2 - 8) * 2048 + 128 * r + 128, :]
                    sy.dma_start(
                        out=w2s[:, (ri % 2) * 1024:(ri % 2) * 1024 + 1024],
                        in_=srca).then_inc(sems["w2ld"], 16)

                w1load(0, 0)
                w1load(0, 1)
                for p in range(NPASS):
                    if p >= 1:
                        sy.wait_ge(sems["gmm"], B["gmm"] + 160 * p)
                    for dc in range(8):
                        sy.dma_start(out=lnfc[:, dc * 1024:dc * 1024 + 1024],
                                     in_=lnft_d[dc, :, TC * p:TC * p + TC]
                                     ).then_inc(sems["lnf"], 16)
                    for grp in range(NGRP):
                        pg = p * NGRP + grp
                        for el in range(8):
                            w2load(pg, 2 * el)
                            w2load(pg, 2 * el + 1)
                            if el < 6:
                                w1load(pg, el + 2)
                        if pg + 1 < NPASS * NGRP:
                            w1load(pg + 1, 0)
                            w1load(pg + 1, 1)
                    for mt in range(8):
                        oi = p * 8 + mt
                        if oi >= 2:
                            sy.wait_ge(sems["oadd"], B["oadd"] + oi - 1)
                        sy.dma_start(out=x2s3[:, (oi % 2) * D:(oi % 2) * D + D],
                                     in_=x2_d[oi]).then_inc(sems["xr3"], 16)

            @blk.vector
            def _(v):
                v.wait_ge(sems["wld"], B["wld"] + 48)
                vsync(v, v.memset(b2b[:], 0.0))
                v.tensor_copy(b2b[0:64, :], b2st[:])
                v.tensor_copy(b2b[64:66, :], sb2st[0:2, :]).then_inc(sems["wc"], 1)

                def w1cast(pg2, el2):
                    ei = pg2 * 8 + el2
                    v.wait_ge(sems["w1ld"], B["w1ld"] + 16 * (ei + 1))
                    if ei >= 4:
                        v.wait_ge(sems["gmm"], B["gmm"] + 2 * (ei - 3))
                    v.tensor_copy(
                        w1b[:, (ei % 4) * 2048:(ei % 4) * 2048 + 2048],
                        w1s[:, (ei % 2) * 2048:(ei % 2) * 2048 + 2048]
                        ).then_inc(sems["w1c"], 1)

                def cwm(p2, grp2, el2):
                    ei = (p2 * NGRP + grp2) * 8 + el2
                    ir = p2 * 64 + grp2 * 8 + el2
                    for m in range(2):
                        si = 2 * ei + m
                        v.wait_ge(sems["silu"], B["silu"] + si + 1)
                        v.wait_ge(sems["cwb"], B["cwb"] + ir + 1)
                        if el2 == 0 and m == 0:
                            v.wait_ge(sems["w2mm"],
                                      B["w2mm"] + 16 * (p2 * NGRP + grp2))
                        v.tensor_tensor(
                            Gg[:, (2 * el2 + m) * 1024:
                               (2 * el2 + m) * 1024 + 1024],
                            stmp[:, m * 1024:m * 1024 + 1024],
                            cwbp[:], op=OP.mult).then_inc(sems["cwu"], 1)

                def w2cast(p2, grp2, r):
                    ri = (p2 * NGRP + grp2) * 16 + r
                    v.wait_ge(sems["w2ld"], B["w2ld"] + 16 * (ri + 1))
                    if r == 0:
                        v.wait_ge(sems["w2mm"],
                                  B["w2mm"] + 16 * (p2 * NGRP + grp2))
                    v.tensor_copy(w2b[:, r * 1024:r * 1024 + 1024],
                                  w2s[:, (ri % 2) * 1024:(ri % 2) * 1024 + 1024]
                                  ).then_inc(sems["w2c"], 1)

                w1cast(0, 0)
                w1cast(0, 1)
                for p in range(NPASS):
                    for grp in range(NGRP):
                        pg = p * NGRP + grp
                        for el in range(8):
                            w2cast(p, grp, 2 * el)
                            w2cast(p, grp, 2 * el + 1)
                            if el >= 1 and grp < 8:
                                cwm(p, grp, el - 1)
                            if el < 6:
                                w1cast(pg, el + 2)
                        if grp < 8:
                            cwm(p, grp, 7)
                        if pg + 1 < NPASS * NGRP:
                            w1cast(pg + 1, 0)
                            w1cast(pg + 1, 1)
                        for mt in range(8):
                            for nh2 in range(2):
                                ep = epidx(p, grp, mt, nh2)
                                v.wait_ge(sems["w2mm"], B["w2mm"] + ep + 1)
                                dst = acc[:, mt * 1024 + 512 * nh2:
                                          mt * 1024 + 512 * nh2 + 512]
                                if grp == 0:
                                    v.tensor_copy(dst, w2p[ep % 2][:]
                                                  ).then_inc(sems["acc"], 1)
                                else:
                                    v.tensor_tensor(dst, dst, w2p[ep % 2][:],
                                                    op=OP.add
                                                    ).then_inc(sems["acc"], 1)
                    for mt in range(8):
                        oi = p * 8 + mt
                        v.wait_ge(sems["acc"], B["acc"] + 160 * (p + 1))
                        v.wait_ge(sems["xr3"], B["xr3"] + 16 * (oi + 1))
                        if oi >= 2:
                            v.wait_ge(sems["ost"], B["ost"] + 16 * (oi - 1))
                        v.tensor_tensor(ostg[:, (oi % 2) * D:(oi % 2) * D + D],
                                        acc[:, mt * 1024:mt * 1024 + 1024],
                                        x2s3[:, (oi % 2) * D:(oi % 2) * D + D],
                                        op=OP.add).then_inc(sems["oadd"], 1)

            @blk.scalar
            def _(s):
                for p in range(NPASS):
                    for grp in range(NGRP):
                        for el in range(8):
                            ei = eidx(p, grp, el)
                            for m in range(2):
                                si = 2 * ei + m
                                s.wait_ge(sems["gmm"], B["gmm"] + si + 1)
                                if grp < 8:
                                    e = grp * 8 + el
                                    bias = b1T[:, 64 * m + e:64 * m + e + 1]
                                    sir = 2 * (p * 64 + grp * 8 + el) + m
                                    if sir >= 2:
                                        s.wait_ge(sems["cwu"],
                                                  B["cwu"] + sir - 1)
                                    s.activation(
                                        stmp[:, m * 1024:m * 1024 + 1024],
                                        gps[m][:], AF.Silu, bias=bias
                                        ).then_inc(sems["silu"], 1)
                                else:
                                    es2 = grp - 8
                                    ht = 2 * el + m
                                    bias = sb1T[:, 2 * ht + es2:2 * ht + es2 + 1]
                                    if el == 0 and m == 0:
                                        s.wait_ge(sems["w2mm"],
                                                  B["w2mm"] +
                                                  16 * (p * NGRP + grp))
                                    s.activation(
                                        Gg[:, (2 * el + m) * 1024:
                                           (2 * el + m) * 1024 + 1024],
                                        gps[m][:], AF.Silu, bias=bias
                                        ).then_inc(sems["silu"], 1)
                    for mt in range(8):
                        oi = p * 8 + mt
                        s.wait_ge(sems["oadd"], B["oadd"] + oi + 1)
                        s.dma_start(out=out_t[oi],
                                    in_=ostg[:, (oi % 2) * D:(oi % 2) * D + D]
                                    ).then_inc(sems["ost"], 16)
                s.wait_ge(sems["ost"], B["ost"] + 16 * GT)

            @blk.tensor
            def _(t):
                t.wait_ge(sems["wc"], B["wc"] + 1)
                for p in range(NPASS):
                    t.wait_ge(sems["lnf"], B["lnf"] + 16 * 8 * (p + 1))
                    for grp in range(NGRP):
                        pg = p * NGRP + grp
                        for el in range(8):
                            ei = eidx(p, grp, el)
                            t.wait_ge(sems["w1c"], B["w1c"] + ei + 1)
                            if grp < 8:
                                e = grp * 8 + el
                                ir = p * 64 + grp * 8 + el
                                if ir >= 1:
                                    t.wait_ge(sems["cwu"], B["cwu"] + 2 * ir)
                                t.matmul(cwbp[:, 0:512],
                                         ohb[:, 128 * e:128 * e + 128],
                                         CWE[0:64, TC * p:TC * p + 512],
                                         start=True, stop=True)
                                t.matmul(cwbp[:, 512:1024],
                                         ohb[:, 128 * e:128 * e + 128],
                                         CWE[0:64, TC * p + 512:TC * p + 1024],
                                         start=True, stop=True
                                         ).then_inc(sems["cwb"], 1)
                            for m in range(2):
                                si = 2 * ei + m
                                if si >= 2:
                                    t.wait_ge(sems["silu"], B["silu"] + si - 1)
                                for dc in range(8):
                                    for n2 in range(2):
                                        inst = t.matmul(
                                            gps[m][:, 512 * n2:512 * n2 + 512],
                                            w1b[:, (ei % 4) * 2048 + HE * dc +
                                                128 * m:
                                                (ei % 4) * 2048 + HE * dc +
                                                128 * m + 128],
                                            lnfc[:, dc * 1024 + 512 * n2:
                                                 dc * 1024 + 512 * n2 + 512],
                                            start=(dc == 0), stop=(dc == 7))
                                inst.then_inc(sems["gmm"], 1)
                        if grp < 8:
                            t.wait_ge(sems["cwu"],
                                      B["cwu"] + 16 * (8 * p + grp + 1))
                        else:
                            t.wait_ge(sems["silu"],
                                      B["silu"] + 16 * (NGRP * p + grp + 1))
                        for mt in range(8):
                            for nh2 in range(2):
                                ep = epidx(p, grp, mt, nh2)
                                if ep >= 2:
                                    t.wait_ge(sems["acc"], B["acc"] + ep - 1)
                                for r in range(16):
                                    t.wait_ge(sems["w2c"],
                                              B["w2c"] + (pg * 16) + r + 1)
                                    inst = t.matmul(
                                        w2p[ep % 2][:],
                                        Gg[:, r * 1024 + 128 * mt:
                                           r * 1024 + 128 * mt + 128],
                                        w2b[:, r * 1024 + 512 * nh2:
                                            r * 1024 + 512 * nh2 + 512],
                                        start=(r == 0),
                                        stop=(r == 15 and grp != NGRP - 1))
                                if grp == NGRP - 1:
                                    inst = t.matmul(
                                        w2p[ep % 2][:],
                                        CWE[:, TC * p + 128 * mt:
                                            TC * p + 128 * mt + 128],
                                        b2b[:, 512 * nh2:512 * nh2 + 512],
                                        start=False, stop=True)
                                inst.then_inc(sems["w2mm"], 1)
            barrier(blk)
        e3.close()

    if "setup_tap" in taps:
        with nc.Block() as blk:
            @blk.scalar
            def _(g):
                base = cnt["ost"]
                for nm, buf in [("gb_t", gb), ("bb_t", bb), ("rbb_t", rbb),
                                ("b1T_t", b1T), ("sb1T_t", sb1T)]:
                    dst = tapout(nm, list(buf.shape), F32)
                    g.dma_start(out=dst, in_=buf[:]).then_inc(sems["ost"], 16)
                    cnt["ost"] += 16
                dst = tapout("lam_t", [1, HD], BF16)
                g.dma_start(out=dst, in_=lam_b[:]).then_inc(sems["ost"], 16)
                cnt["ost"] += 16
                g.wait_ge(sems["ost"], cnt["ost"])
            barrier(blk)

    if "p1" in phases:
        emit_p1(0)
        if "p2" in phases:
            emit_p2(0)
        emit_p1(1)
        if "p2" in phases:
            emit_p2(1)
    e12.close()
    if "p3" in phases:
        emit_p3()
    eg.close()
    return nc, tap_out


def _consts():
    bf = ml_dtypes.bfloat16
    ident = np.eye(128, dtype=np.float32)
    ones = np.ones((1, 128), dtype=np.float32)
    oh = np.zeros((E, E * 128), dtype=np.float32)
    for e in range(E):
        oh[e, 128 * e:128 * e + 128] = 1.0
    return {
        "id_bf": ident.astype(bf), "id_f32": ident,
        "ones_f32": ones, "ones_bf": ones.astype(bf),
        "oh_bf": oh.astype(bf),
    }


def _weights(inputs):
    w = {
        "wq": inputs["L0_attn_Wq"], "wk": inputs["L0_attn_Wk"],
        "wv": inputs["L0_attn_Wv"], "wo": inputs["L0_attn_Wo"],
        "lq1": inputs["L0_attn_lambda_q1"].reshape(1, HD),
        "lk1": inputs["L0_attn_lambda_k1"].reshape(1, HD),
        "lq2": inputs["L0_attn_lambda_q2"].reshape(1, HD),
        "lk2": inputs["L0_attn_lambda_k2"].reshape(1, HD),
        "gam": inputs["L0_attn_gamma"].reshape(1, D),
        "bet": inputs["L0_attn_beta"].reshape(1, D),
        "rw": inputs["L0_ffn_router_w"],
        "rb": inputs["L0_ffn_router_b"].reshape(1, E),
        "w1": inputs["L0_ffn_w1"], "b1": inputs["L0_ffn_b1"],
        "w2": inputs["L0_ffn_w2"], "b2": inputs["L0_ffn_b2"],
        "sw1": inputs["L0_ffn_sw1"], "sb1": inputs["L0_ffn_sb1"],
        "sw2": inputs["L0_ffn_sw2"], "sb2": inputs["L0_ffn_sb2"],
    }
    return {k: np.ascontiguousarray(v, dtype=np.float32) for k, v in w.items()}


def kernel(**inputs):
    x = np.ascontiguousarray(inputs["L0_attn_x"], dtype=np.float32)
    consts = _consts()
    weights = _weights(inputs)
    nc, _ = build_nc()
    in_maps = []
    for c in range(8):
        m = dict(weights)
        m.update(consts)
        m["x_sh"] = np.ascontiguousarray(x[2 * c:2 * c + 2])
        in_maps.append(m)
    res = run_bass_kernel_spmd(nc, in_maps, core_ids=list(range(8)))
    return np.concatenate([r["out"] for r in res.results], axis=0)


# version 04 (final)
# speedup vs baseline: 1.2667x; 1.1056x over previous
import math
import numpy as np
import ml_dtypes
import concourse.bass as bass
import concourse.mybir as mybir
from concourse.alu_op_type import AluOpType as OP
from concourse.bass_utils import run_bass_kernel_spmd
from contextlib import ExitStack

F32 = mybir.dt.float32
BF16 = mybir.dt.bfloat16
AF = mybir.ActivationFunctionType
AX = mybir.AxisListType

NB, S, D = 2, 2048, 1024
NH, HD = 16, 64
E, HE = 64, 256
ES, HS = 2, 2048
TT = S // 128
GT = NB * TT
LAMBDA_INIT = 0.8 - 0.6 * math.exp(-0.3 * 11.0)
LN_EPS = 1e-5
NPASS = 4
TC = (NB * S) // NPASS
NGRP = 10

PHASES = ("p1", "p2", "p3")
TAPS = ()
P1_STAGES = 4
P1_RAW = False
P1_KN = 16
P2_STAGES = 2
P2_NOX2 = False


def build_nc(phases=PHASES, taps=TAPS):
    nc = bass.Bass("TRN2", target_bir_lowering=False,
                   dynamic_dma_scratch_size=2048,
                   detect_race_conditions=False)
    eg = ExitStack()
    tap_out = {}

    def din(name, shape, dt=F32):
        return nc.dram_tensor(name, list(shape), dt, kind="ExternalInput").ap()

    x_in = din("x_sh", [NB, S, D])
    wq_d = din("wq", [D, 2 * D]); wk_d = din("wk", [D, 2 * D])
    wv_d = din("wv", [D, D]); wo_d = din("wo", [D, D])
    lq1 = din("lq1", [1, HD]); lk1 = din("lk1", [1, HD])
    lq2 = din("lq2", [1, HD]); lk2 = din("lk2", [1, HD])
    gam = din("gam", [1, D]); bet = din("bet", [1, D])
    rw_d = din("rw", [D, E]); rb_d = din("rb", [1, E])
    w1_d = din("w1", [E, D, HE]); b1_d = din("b1", [E, HE])
    w2_d = din("w2", [E, HE, D]); b2_d = din("b2", [E, D])
    sw1_d = din("sw1", [ES, D, HS]); sb1_d = din("sb1", [ES, HS])
    sw2_d = din("sw2", [ES, HS, D]); sb2_d = din("sb2", [ES, D])
    id_bf = din("id_bf", [128, 128], BF16)
    id_f = din("id_f32", [128, 128])
    ones_f = din("ones_f32", [1, 128])
    ones_b = din("ones_bf", [1, 128], BF16)
    oh_d = din("oh_bf", [E, E * 128], BF16)

    out = nc.dram_tensor("out", [NB, S, D], F32, kind="ExternalOutput").ap()
    out_t = out.rearrange("b (t p) d -> (b t) p d", p=128)

    def dscr(name, shape, dt):
        kind = "ExternalOutput" if name in taps else "Internal"
        return nc.dram_tensor(name, list(shape), dt, kind=kind).ap()

    qt_d = dscr("qt_d", [NB, TT, 128, S], BF16)
    x2_d = dscr("x2_d", [GT, 128, D], F32)
    lnft_d = dscr("lnft_d", [8, 128, NB * S], BF16)

    x_t = x_in.rearrange("b (t p) d -> b t p d", p=128)
    w2_f = w2_d.rearrange("e h d -> (e h) d")
    sw2_f = sw2_d.rearrange("e h d -> (e h) d")

    def tapout(name, shape, dt):
        ap = nc.dram_tensor(name, list(shape), dt, kind="ExternalOutput").ap()
        tap_out[name] = name
        return ap

    def sbg(name, shape, dt=F32):
        return eg.enter_context(nc.sbuf_tensor(name, list(shape), dt))

    c_idb = sbg("c_idb", [128, 128], BF16)
    c_idf = sbg("c_idf", [128, 128], F32)
    c_onf = sbg("c_onf", [1, 128], F32)
    c_onb = sbg("c_onb", [1, 128], BF16)
    c_eps = sbg("c_eps", [128, 1], F32)
    c_m8 = sbg("c_m8", [128, 1], F32)
    lamv = sbg("lamv", [1, 1], F32)
    lam_b = sbg("lam_b", [1, HD], BF16)
    gb = sbg("gb", [128, D], F32)
    bb = sbg("bb", [128, D], F32)
    rbb = sbg("rbb", [128, E], F32)
    b1T = sbg("b1T", [128, 128], F32)
    sb1T = sbg("sb1T", [128, 32], F32)
    lam64 = sbg("lam64", [64, 1], F32)

    from collections import defaultdict
    cnt = defaultdict(int)
    sems = {}

    def S_(name):
        if name not in sems:
            sems[name] = eg.enter_context(nc.semaphore(name))
        return sems[name]

    for n in ("bar ld misc "
              "x sq var r mbn apply tr trev "
              "wld wc kmm kev qmm qev qst vmm vev "
              "qt mmA exp pv rb rbmm rbe comb womm wored wosq wovar wor "
              "wombn woap xr x2c x2st "
              "rt3 cw1 cw2 cw3 cw4 lnst lnld ln3 xr3 "
              "lnf w1ld w1c gmm silu cwb cwu w2ld w2c w2mm acc oadd ost hz hz2 hz3 oev"
              ).split():
        S_(n)

    bar_n = [0]
    hz2n = [0]
    hz3n = [0]

    def ssync(s, inst):
        hz2n[0] += 1
        inst.then_inc(sems["hz2"], 1)
        s.wait_ge(sems["hz2"], hz2n[0])

    def vsync(v, inst):
        hz3n[0] += 1
        inst.then_inc(sems["hz3"], 1)
        v.wait_ge(sems["hz3"], hz3n[0])

    def barrier(blk):
        bar_n[0] += 5
        tgt = bar_n[0]
        for en in ("tensor", "vector", "scalar", "gpsimd", "sync"):
            def mk(eng):
                eng.nop().then_inc(sems["bar"], 1)
                eng.wait_ge(sems["bar"], tgt)
            getattr(blk, en)(mk)

    with nc.Block() as blk, ExitStack() as ps:
        pA = ps.enter_context(nc.psum_tensor("pA", [128, 512], F32))
        pB = ps.enter_context(nc.psum_tensor("pB", [128, 512], F32))
        pC = ps.enter_context(nc.psum_tensor("pC", [128, 512], F32))
        smv = ps.enter_context(nc.sbuf_tensor("smv", [1, 8], F32))
        lrow = ps.enter_context(nc.sbuf_tensor("lrow", [1, 4 * HD], F32))
        srow = ps.enter_context(nc.sbuf_tensor("srow", [1, D], F32))
        b1s = ps.enter_context(nc.sbuf_tensor("b1s", [64, HE], F32))
        sb1s = ps.enter_context(nc.sbuf_tensor("sb1s", [2, HS], F32))

        @blk.sync
        def _(sy):
            loads = [(c_idb[:], id_bf), (c_idf[:], id_f), (c_onf[:], ones_f),
                     (c_onb[:], ones_b), (b1s[:], b1_d),
                     (sb1s[:], sb1_d),
                     (lrow[:, 0:HD], lq1), (lrow[:, HD:2 * HD], lk1),
                     (lrow[:, 2 * HD:3 * HD], lq2), (lrow[:, 3 * HD:], lk2)]
            for dst, src in loads:
                sy.dma_start(out=dst, in_=src).then_inc(sems["ld"], 16)
            sy.dma_start(out=srow[:, 0:D], in_=gam).then_inc(sems["ld"], 16)
        n_ld0 = 11

        @blk.vector
        def _(v):
            v.wait_ge(sems["ld"], 16 * n_ld0)
            v.memset(c_eps[:], LN_EPS)
            v.memset(c_m8[:], -8.0)
            vsync(v, v.tensor_tensor(lrow[:, 0:HD], lrow[:, 0:HD],
                                     lrow[:, HD:2 * HD], op=OP.mult))
            vsync(v, v.tensor_tensor(lrow[:, 2 * HD:3 * HD],
                                     lrow[:, 2 * HD:3 * HD],
                                     lrow[:, 3 * HD:], op=OP.mult))
            v.reduce_sum(smv[:, 0:1], lrow[:, 0:HD], axis=AX.X)
            v.reduce_sum(smv[:, 1:2], lrow[:, 2 * HD:3 * HD],
                         axis=AX.X).then_inc(sems["misc"], 1)
            v.wait_ge(sems["misc"], 2)
            vsync(v, v.tensor_tensor(smv[:, 4:5], smv[:, 2:3], smv[:, 3:4],
                                     op=OP.subtract))
            vsync(v, v.tensor_scalar(lamv[:], smv[:, 4:5], LAMBDA_INIT, None,
                                     op0=OP.add))
            v.sem_inc(sems["misc"], 1)

        @blk.scalar
        def _(s):
            s.wait_ge(sems["misc"], 1)
            s.activation(smv[:, 2:3], smv[:, 0:1], AF.Exp)
            s.activation(smv[:, 3:4], smv[:, 1:2],
                         AF.Exp).then_inc(sems["misc"], 1)

        @blk.tensor
        def _(t):
            t.wait_ge(sems["misc"], 3)
            t.wait_ge(sems["ld"], 16 * n_ld0)
            for i in range(23):
                pt = pA if i % 2 == 0 else pB
                if i >= 2:
                    t.wait_ge(sems["misc"], 3 + i - 1)
                if i < 5:
                    t.wait_ge(sems["ld"], 16 * (n_ld0 + min(i // 2, 2)))
                    if i < 2:
                        rhs = srow[:, i * 512:i * 512 + 512]
                    elif i < 4:
                        rhs = srow[:, (i - 2) * 512:(i - 2) * 512 + 512]
                    else:
                        rhs = srow[:, 0:E]
                    t.matmul(pt[:, 0:rhs.shape[1]], c_onf[:], rhs,
                             start=True, stop=True).then_inc(sems["mbn"], 1)
                elif i < 7:
                    m = i - 5
                    if m == 0:
                        t.matmul(pC[0:64, 0:1], c_onf[0:1, 0:64], lamv[:],
                                 start=True, stop=True)
                    t.transpose(pt[:, 0:64], b1s[0:64, 128 * m:128 * m + 128],
                                c_idf[0:64, 0:64]).then_inc(sems["mbn"], 1)
                else:
                    j = i - 7
                    t.transpose(pt[:, 0:2], sb1s[0:2, 128 * j:128 * j + 128],
                                c_idf[0:2, 0:2]).then_inc(sems["mbn"], 1)

        @blk.vector
        def _(v):
            for i in range(23):
                pt = pA if i % 2 == 0 else pB
                v.wait_ge(sems["mbn"], i + 1)
                if i < 2:
                    v.tensor_copy(gb[:, i * 512:i * 512 + 512], pt[:, 0:512])
                elif i < 4:
                    v.tensor_copy(bb[:, (i - 2) * 512:(i - 2) * 512 + 512],
                                  pt[:, 0:512])
                elif i == 4:
                    v.tensor_copy(rbb[:], pt[:, 0:E])
                elif i < 7:
                    if i == 5:
                        v.tensor_copy(lam64[:], pC[0:64, 0:1])
                    v.tensor_copy(b1T[:, 64 * (i - 5):64 * (i - 5) + 64],
                                  pt[:, 0:64])
                else:
                    j = i - 7
                    v.tensor_copy(sb1T[:, 2 * j:2 * j + 2], pt[:, 0:2])
                v.sem_inc(sems["misc"], 1)

        @blk.sync
        def _(sy):
            sy.wait_ge(sems["misc"], 5)
            sy.dma_start(out=srow[:, 0:D], in_=bet).then_inc(sems["ld"], 16)
            sy.wait_ge(sems["misc"], 7)
            sy.dma_start(out=srow[:, 0:E], in_=rb_d).then_inc(sems["ld"], 16)
        barrier(blk)
    cnt["ld"] = 16 * (n_ld0 + 2)
    cnt["mbn"] = 23
    cnt["misc"] = 26

    e12 = ExitStack()
    if "p1" in phases:
        def sb12(name, shape, dt):
            return e12.enter_context(nc.sbuf_tensor(name, list(shape), dt))
        KT = sb12("KT", [128, TT * S], BF16)
        Vaug = sb12("Vaug", [128, TT * 1040], BF16)
        xstg = sb12("xstg", [128, 2 * D], F32)
        sqt = sb12("sqt", [128, D], F32)
        st2 = sb12("st2", [128, 16], F32)
        sdum = sb12("sdum", [128, 1], F32)

    def emit_p1(b):
        B = defaultdict(int, cnt)
        with nc.Block() as blk, ExitStack() as ps:
            dbg = (ps.enter_context(nc.sbuf_tensor(f"dbg{b}", [128, 96], F32))
                   if f"dbg_{b}" in taps else None)
            qk = [ps.enter_context(nc.psum_tensor(f"qk{b}_{i}", [128, 512], F32))
                  for i in range(4)]
            trp = [ps.enter_context(nc.psum_tensor(f"tp{b}_{i}", [128, 1024], BF16))
                   for i in range(2)]
            ln1T = ps.enter_context(nc.sbuf_tensor(f"ln1T{b}", [128, 8 * S], BF16))
            wbuf = ps.enter_context(nc.sbuf_tensor(f"wbuf{b}", [128, 8 * 2048], BF16))
            wstg = ps.enter_context(nc.sbuf_tensor(f"wstg{b}", [128, 2 * 2048], F32))
            ln1b = ps.enter_context(nc.sbuf_tensor(f"ln1b{b}", [128, D], BF16))
            qstg = ps.enter_context(nc.sbuf_tensor(f"qstg{b}", [128, 2 * 2048], BF16))

            @blk.sync
            def _(sy):
                for tt in range(TT):
                    if tt >= 2:
                        sy.wait_ge(sems["apply"], B["apply"] + tt - 1)
                    sy.dma_start(out=xstg[:, (tt % 2) * D:(tt % 2) * D + D],
                                 in_=x_t[b, tt]).then_inc(sems["x"], 16)
                for wi, (wap, ncol) in enumerate([(wk_d, 2048), (wq_d, 2048),
                                                  (wv_d, 1024)][:P1_STAGES - 1]):
                    for ch in range(8):
                        gi = wi * 8 + ch
                        if gi >= 2:
                            sy.wait_ge(sems["wc"], B["wc"] + gi - 1)
                        sy.dma_start(
                            out=wstg[:, (gi % 2) * 2048:(gi % 2) * 2048 + ncol],
                            in_=wap[128 * ch:128 * ch + 128, :]
                        ).then_inc(sems["wld"], 16)

            @blk.scalar
            def _(s):
                for tt in range(TT):
                    par = tt % 2
                    s.wait_ge(sems["x"], B["x"] + 16 * (tt + 1))
                    s.activation(sqt[:], xstg[:, par * D:par * D + D], AF.Square,
                                 accum_out=st2[:, 2 + par:3 + par]
                                 ).then_inc(sems["sq"], 1)
                    s.wait_ge(sems["var"], B["var"] + tt + 1)
                    ssync(s, s.activation(st2[:, 12 + par:13 + par],
                                          st2[:, 6 + par:7 + par], AF.Ln,
                                          bias=c_eps[:, 0:1]))
                    s.activation(st2[:, 8 + par:9 + par],
                                 st2[:, 12 + par:13 + par], AF.Exp,
                                 scale=-0.5).then_inc(sems["r"], 1)
                    s.wait_ge(sems["mbn"], B["mbn"] + tt + 1)
                    if tt >= 1:
                        s.wait_ge(sems["tr"], B["tr"] + 8 * tt)
                    if P1_RAW:
                        s.activation(ln1b[:], xstg[:, par * D:par * D + D],
                                     AF.Identity, bias=c_eps[:, 0:1],
                                     scale=1.0).then_inc(sems["apply"], 1)
                    else:
                        s.activation(ln1b[:], xstg[:, par * D:par * D + D],
                                     AF.Identity, bias=st2[:, 10 + par:11 + par],
                                     scale=st2[:, 8 + par:9 + par]
                                     ).then_inc(sems["apply"], 1)
                for ft in range(TT if P1_STAGES >= 3 else 0):
                    par = ft % 2
                    s.wait_ge(sems["qev"], B["qev"] + 4 * (ft + 1))
                    s.dma_start(out=qt_d[b, ft],
                                in_=qstg[:, par * 2048:par * 2048 + 2048]
                                ).then_inc(sems["qst"], 16)
                if P1_STAGES >= 3:
                    s.wait_ge(sems["qst"], B["qst"] + 16 * TT)

            @blk.vector
            def _(v):
                for tt in range(TT):
                    par = tt % 2
                    v.wait_ge(sems["x"], B["x"] + 16 * (tt + 1))
                    vsync(v, v.reduce_sum(st2[:, par:par + 1],
                                          xstg[:, par * D:par * D + D],
                                          axis=AX.X))
                    v.wait_ge(sems["sq"], B["sq"] + tt + 1)
                    vsync(v, v.tensor_scalar(st2[:, 4 + par:5 + par],
                                             st2[:, par:par + 1],
                                             1.0 / D, None, op0=OP.mult))
                    v.tensor_scalar(st2[:, 14 + par:15 + par],
                                    st2[:, 2 + par:3 + par], 1.0 / D, None,
                                    op0=OP.mult)
                    vsync(v, v.tensor_tensor(st2[:, 12 + par:13 + par],
                                             st2[:, 4 + par:5 + par],
                                             st2[:, 4 + par:5 + par],
                                             op=OP.mult))
                    v.tensor_tensor(st2[:, 6 + par:7 + par],
                                    st2[:, 14 + par:15 + par],
                                    st2[:, 12 + par:13 + par],
                                    op=OP.subtract).then_inc(sems["var"], 1)
                    v.wait_ge(sems["r"], B["r"] + tt + 1)
                    v.scalar_tensor_tensor(st2[:, 10 + par:11 + par],
                                           st2[:, 4 + par:5 + par], -1.0,
                                           st2[:, 8 + par:9 + par],
                                           op0=OP.mult, op1=OP.mult
                                           ).then_inc(sems["mbn"], 1)
                    if dbg is not None:
                        v.tensor_copy(dbg[:, 6 * tt:6 * tt + 1],
                                      st2[:, 4 + par:5 + par])
                        v.tensor_copy(dbg[:, 6 * tt + 1:6 * tt + 2],
                                      st2[:, 6 + par:7 + par])
                        v.tensor_copy(dbg[:, 6 * tt + 2:6 * tt + 3],
                                      st2[:, 8 + par:9 + par])
                        v.tensor_copy(dbg[:, 6 * tt + 3:6 * tt + 4],
                                      st2[:, 10 + par:11 + par])
                        v.tensor_copy(dbg[:, 6 * tt + 4:6 * tt + 5],
                                      st2[:, par:par + 1])
                        v.tensor_copy(dbg[:, 6 * tt + 5:6 * tt + 6],
                                      st2[:, 2 + par:3 + par])
                    for dc in range(8):
                        v.wait_ge(sems["tr"], B["tr"] + 8 * tt + dc + 1)
                        v.tensor_copy(
                            ln1T[:, dc * S + 128 * tt:dc * S + 128 * tt + 128],
                            trp[dc % 2][:, 0:128]).then_inc(sems["trev"], 1)
                for ch in range(8 if P1_STAGES >= 2 else 0):
                    v.wait_ge(sems["wld"], B["wld"] + 16 * (ch + 1))
                    v.tensor_copy(wbuf[:, ch * 2048:ch * 2048 + 2048],
                                  wstg[:, (ch % 2) * 2048:(ch % 2) * 2048 + 2048]
                                  ).then_inc(sems["wc"], 1)
                for u in range(4 * P1_KN if P1_STAGES >= 2 else 0):
                    n, ft = divmod(u, TT)
                    v.wait_ge(sems["kmm"], B["kmm"] + u + 1)
                    v.tensor_copy(
                        KT[:, ft * S + 512 * n:ft * S + 512 * n + 512],
                        qk[u % 4][:]).then_inc(sems["kev"], 1)
                if P1_STAGES >= 3:
                    v.wait_ge(sems["kmm"], B["kmm"] + 4 * P1_KN)
                for ch in range(8 if P1_STAGES >= 3 else 0):
                    gi = 8 + ch
                    v.wait_ge(sems["wld"], B["wld"] + 16 * (gi + 1))
                    v.tensor_copy(wbuf[:, ch * 2048:ch * 2048 + 2048],
                                  wstg[:, (gi % 2) * 2048:(gi % 2) * 2048 + 2048]
                                  ).then_inc(sems["wc"], 1)
                for ft in range(TT if P1_STAGES >= 3 else 0):
                    par = ft % 2
                    v.wait_ge(sems["qmm"], B["qmm"] + ft + 1)
                    if ft >= 2:
                        v.wait_ge(sems["qst"], B["qst"] + 16 * (ft - 1))
                    for n in range(4):
                        v.tensor_copy(
                            qstg[:, par * 2048 + 512 * n:
                                 par * 2048 + 512 * n + 512],
                            qk[n][:]).then_inc(sems["qev"], 1)
                if P1_STAGES >= 4:
                    v.wait_ge(sems["qmm"], B["qmm"] + TT)
                for ch in range(8 if P1_STAGES >= 4 else 0):
                    gi = 16 + ch
                    v.wait_ge(sems["wld"], B["wld"] + 16 * (gi + 1))
                    v.tensor_copy(wbuf[:, ch * 2048:ch * 2048 + 1024],
                                  wstg[:, (gi % 2) * 2048:(gi % 2) * 2048 + 1024]
                                  ).then_inc(sems["wc"], 1)
                if b == 0 and P1_STAGES >= 4:
                    va3 = Vaug[:].rearrange("p (t c) -> p t c", c=65)
                    v.memset(va3[:, :, 64:65], 1.0)
                v.sem_inc(sems["vev"], 1)
                for tt in range(TT if P1_STAGES >= 4 else 0):
                    for vc in range(2):
                        v.wait_ge(sems["vmm"], B["vmm"] + tt + 1)
                        dst = Vaug[:, tt * 1040 + 520 * vc:
                                   tt * 1040 + 520 * vc + 520]
                        dst = dst.rearrange("p (h c) -> p h c", c=65)[:, :, 0:64]
                        src = qk[vc][:].rearrange("p (h c) -> p h c", c=64)
                        v.tensor_copy(dst, src).then_inc(sems["vev"], 1)

            @blk.tensor
            def _(t):
                for tt in range(TT):
                    t.wait_ge(sems["apply"], B["apply"] + tt + 1)
                    for dc in range(8):
                        gi = 8 * tt + dc
                        if gi >= 2:
                            t.wait_ge(sems["trev"], B["trev"] + gi - 1)
                        t.transpose(trp[dc % 2][:, 0:128],
                                    ln1b[:, 128 * dc:128 * dc + 128],
                                    c_idb[:]).then_inc(sems["tr"], 1)
                if P1_STAGES >= 2:
                    t.wait_ge(sems["wc"], B["wc"] + 8)
                for u in range(4 * P1_KN if P1_STAGES >= 2 else 0):
                    n, ft = divmod(u, TT)
                    if ft == 0:
                        t.wait_ge(sems["trev"], B["trev"] + 32 * (n + 1))
                    if u >= 4:
                        t.wait_ge(sems["kev"], B["kev"] + u - 3)
                    for dc in range(8):
                        inst = t.matmul(qk[u % 4][:],
                                        wbuf[:, dc * 2048 + 128 * ft:
                                             dc * 2048 + 128 * ft + 128],
                                        ln1T[:, dc * S + 512 * n:
                                             dc * S + 512 * n + 512],
                                        start=(dc == 0), stop=(dc == 7))
                    inst.then_inc(sems["kmm"], 1)
                if P1_STAGES >= 3:
                    t.wait_ge(sems["wc"], B["wc"] + 16)
                    t.wait_ge(sems["kev"], B["kev"] + 4 * TT)
                for ft in range(TT if P1_STAGES >= 3 else 0):
                    if ft >= 1:
                        t.wait_ge(sems["qev"], B["qev"] + 4 * ft)
                    for dc in range(8):
                        for n in range(4):
                            inst = t.matmul(qk[n][:],
                                            wbuf[:, dc * 2048 + 128 * ft:
                                                 dc * 2048 + 128 * ft + 128],
                                            ln1T[:, dc * S + 512 * n:
                                                 dc * S + 512 * n + 512],
                                            start=(dc == 0), stop=(dc == 7))
                    inst.then_inc(sems["qmm"], 1)
                if P1_STAGES >= 4:
                    t.wait_ge(sems["wc"], B["wc"] + 24)
                    t.wait_ge(sems["qev"], B["qev"] + 4 * TT)
                for tt in range(TT if P1_STAGES >= 4 else 0):
                    if tt >= 1:
                        t.wait_ge(sems["vev"], B["vev"] + 2 * tt + 1)
                    for dc in range(8):
                        for vc in range(2):
                            inst = t.matmul(qk[vc][:],
                                            ln1T[:, dc * S + 128 * tt:
                                                 dc * S + 128 * tt + 128],
                                            wbuf[:, dc * 2048 + 512 * vc:
                                                 dc * 2048 + 512 * vc + 512],
                                            start=(dc == 0), stop=(dc == 7))
                    inst.then_inc(sems["vmm"], 1)

            tp = [(f"ln1T_{b}", ln1T, "trev", B["trev"] + 128),
                  (f"KT_{b}", KT, "kev", B["kev"] + 4 * P1_KN),
                  (f"Vaug_{b}", Vaug, "vev", B["vev"] + 2 * TT + 1)]
            if dbg is not None:
                tp.append((f"dbg_{b}", dbg, "mbn", B["mbn"] + TT))
            tp = [x for x in tp if x[0] in taps]
            if tp:
                @blk.scalar
                def _(g):
                    for name, buf, sname, val in tp:
                        dst = tapout(name, list(buf.shape),
                                     F32 if name.startswith("dbg") else BF16)
                        g.wait_ge(sems[sname], val)
                        g.dma_start(out=dst, in_=buf[:]).then_inc(sems["ost"], 16)
                        cnt["ost"] += 16
                    g.wait_ge(sems["ost"], cnt["ost"])
            barrier(blk)
        ns = P1_STAGES
        nw = [0, 0, 8, 16, 24][ns]
        for k, d in [("x", 16 * TT), ("sq", TT), ("var", TT), ("r", TT),
                     ("mbn", TT), ("apply", TT), ("tr", 8 * TT),
                     ("trev", 8 * TT), ("wld", 16 * nw), ("wc", nw),
                     ("kmm", 4 * P1_KN if ns >= 2 else 0),
                     ("kev", 4 * P1_KN if ns >= 2 else 0),
                     ("qmm", TT if ns >= 3 else 0),
                     ("qev", 4 * TT if ns >= 3 else 0),
                     ("qst", 16 * TT if ns >= 3 else 0),
                     ("vmm", TT if ns >= 4 else 0),
                     ("vev", (2 * TT + 1) if ns >= 4 else 0)]:
            cnt[k] = B[k] + d

    def emit_p2(b):
        B = defaultdict(int, cnt)
        with nc.Block() as blk, ExitStack() as ps:
            apm = [ps.enter_context(nc.psum_tensor(f"ap{b}_{i}", [128, 1024], F32))
                   for i in range(2)]
            o12 = ps.enter_context(nc.psum_tensor(f"o12_{b}", [65, 1024], F32))
            mix = ps.enter_context(nc.psum_tensor(f"mx_{b}", [128, 1024], F32))
            wob = ps.enter_context(nc.sbuf_tensor(f"wob{b}", [128, 8 * D], BF16))
            qtq = ps.enter_context(nc.sbuf_tensor(f"qtq{b}", [128, 2 * NH * 512], BF16))
            pbuf = ps.enter_context(nc.sbuf_tensor(f"pbuf{b}", [128, 2 * 1024], BF16))
            wstg = ps.enter_context(nc.sbuf_tensor(f"wst2{b}", [128, 1024], F32))
            rt12 = ps.enter_context(nc.sbuf_tensor(f"rt12{b}", [1, 1024], F32))
            rb12 = ps.enter_context(nc.sbuf_tensor(f"rb12{b}", [1, 2 * 1024], BF16))
            rbs1 = ps.enter_context(nc.sbuf_tensor(f"rbs1{b}", [64, 512], F32))
            rbs2 = ps.enter_context(nc.sbuf_tensor(f"rbs2{b}", [64, 512], F32))
            cb1 = ps.enter_context(nc.sbuf_tensor(f"cb1{b}", [64, 512], F32))
            cb2 = ps.enter_context(nc.sbuf_tensor(f"cb2{b}", [64, 512], F32))
            aoT = ps.enter_context(nc.sbuf_tensor(f"aoT{b}", [128, 8 * 512], BF16))
            tsb = ps.enter_context(nc.sbuf_tensor(f"tsb{b}", [128, D], F32))
            x2s = ps.enter_context(nc.sbuf_tensor(f"x2s{b}", [128, 2 * D], F32))
            oev = ps.enter_context(nc.sbuf_tensor(f"oev{b}", [65, 2 * 1024], F32))

            @blk.sync
            def _(sy):
                for ch in range(8):
                    if ch >= 1:
                        sy.wait_ge(sems["wc"], B["wc"] + ch)
                    sy.dma_start(
                        out=wstg[:, 0:1024],
                        in_=wo_d[128 * ch:128 * ch + 128, :]
                        ).then_inc(sems["wld"], 16)
                sy.wait_ge(sems["qst"], 16 * TT * (b + 1))

                def qtload(qb2):
                    if qb2 >= 4:
                        return
                    if qb2 >= 2:
                        sy.wait_ge(sems["mmA"], B["mmA"] + (qb2 - 1) * 256)
                    for ft in range(TT):
                        sy.dma_start(
                            out=qtq[:, (qb2 % 2) * 8192 + ft * 512:
                                    (qb2 % 2) * 8192 + ft * 512 + 512],
                            in_=qt_d[b, ft, :, 512 * qb2:512 * qb2 + 512]
                            ).then_inc(sems["qt"], 16)

                qtload(0)
                for qb in range(4):
                    qtload(qb + 1)
                    for tt in range(4 if (P2_STAGES >= 2 and not P2_NOX2) else 0):
                        wt = qb * 4 + tt
                        if wt >= 2:
                            sy.wait_ge(sems["x2c"], B["x2c"] + wt - 1)
                        sy.dma_start(out=xstg[:, (wt % 2) * D:(wt % 2) * D + D],
                                     in_=x_t[b, wt]).then_inc(sems["xr"], 16)
                    for tt in range(4 if (P2_STAGES >= 2 and not P2_NOX2) else 0):
                        wt = qb * 4 + tt
                        sy.wait_ge(sems["x2c"], B["x2c"] + wt + 1)
                        sy.dma_start(out=x2_d[b * TT + wt],
                                     in_=x2s[:, (wt % 2) * D:(wt % 2) * D + D]
                                     ).then_inc(sems["x2st"], 16)
                if P2_STAGES >= 2 and not P2_NOX2:
                    sy.wait_ge(sems["x2st"], B["x2st"] + 16 * 16)

            @blk.vector
            def _(v):
                for ch in range(8):
                    v.wait_ge(sems["wld"], B["wld"] + 16 * (ch + 1))
                    v.tensor_copy(wob[:, ch * D:ch * D + D],
                                  wstg[:, 0:1024]).then_inc(sems["wc"], 1)
                def vslot(qb2, hh):
                    if hh < NH:
                        qh2 = qb2 * NH + hh
                        v.wait_ge(sems["pv"], B["pv"] + (qh2 + 1) * TT)
                        if qh2 >= 2:
                            v.wait_ge(sems["comb"], B["comb"] + qh2 - 1)
                        vsync(v, v.tensor_copy(
                            oev[:, (qh2 % 2) * 1024:(qh2 % 2) * 1024 + 1024],
                            o12[:]))
                        v.sem_inc(sems["oev"], 1)
                    hp = hh - 1
                    if hp < 0:
                        return
                    qh2 = qb2 * NH + hp
                    par = qh2 % 2
                    v.wait_ge(sems["rbmm"], B["rbmm"] + qh2 + 1)
                    if qh2 >= 1:
                        v.wait_ge(sems["comb"], B["comb"] + qh2)
                    v.tensor_copy(rbs1[:], mix[0:64, 0:512])
                    vsync(v, v.tensor_copy(rbs2[:], mix[64:128, 0:512]))
                    v.sem_inc(sems["rbe"], 1)
                    if P2_STAGES >= 2 and qh2 >= 1:
                        v.wait_ge(sems["womm"], B["womm"] + 4 * qb2)
                    vsync(v, v.tensor_tensor(cb1[:],
                                             oev[0:64, par * 1024:par * 1024 + 512],
                                             rbs1[:], op=OP.mult))
                    vsync(v, v.scalar_tensor_tensor(
                        cb2[:], oev[0:64, par * 1024 + 512:par * 1024 + 1024],
                        lam64[:, 0:1], rbs2[:], op0=OP.mult, op1=OP.mult))
                    r0 = 64 * (hp % 2)
                    c0 = 512 * (hp // 2)
                    v.tensor_tensor(aoT[r0:r0 + 64, c0:c0 + 512],
                                    cb1[:], cb2[:],
                                    op=OP.subtract).then_inc(sems["comb"], 1)

                for qb in range(4):
                    for h in range(NH):
                        vslot(qb, h)
                    vslot(qb, NH)
                    for tt in range(4 if P2_STAGES >= 2 else 0):
                        wt = qb * 4 + tt
                        par = wt % 2
                        wop = [mix, apm[0], apm[1]][tt % 3]
                        v.wait_ge(sems["womm"], B["womm"] + wt + 1)
                        vsync(v, v.reduce_sum(st2[:, par:par + 1], wop[:],
                                              axis=AX.X))
                        v.sem_inc(sems["wored"], 1)
                        v.wait_ge(sems["wosq"], B["wosq"] + wt + 1)
                        vsync(v, v.tensor_scalar(st2[:, 4 + par:5 + par],
                                                 st2[:, par:par + 1], 1.0 / D,
                                                 None, op0=OP.mult))
                        v.tensor_scalar(st2[:, 14 + par:15 + par],
                                        st2[:, 2 + par:3 + par], 1.0 / D, None,
                                        op0=OP.mult)
                        vsync(v, v.tensor_tensor(st2[:, 12 + par:13 + par],
                                                 st2[:, 4 + par:5 + par],
                                                 st2[:, 4 + par:5 + par],
                                                 op=OP.mult))
                        v.tensor_tensor(st2[:, 6 + par:7 + par],
                                        st2[:, 14 + par:15 + par],
                                        st2[:, 12 + par:13 + par],
                                        op=OP.subtract).then_inc(sems["wovar"], 1)
                        v.wait_ge(sems["wor"], B["wor"] + wt + 1)
                        v.scalar_tensor_tensor(st2[:, 10 + par:11 + par],
                                               st2[:, 4 + par:5 + par], -1.0,
                                               st2[:, 8 + par:9 + par],
                                               op0=OP.mult, op1=OP.mult
                                               ).then_inc(sems["wombn"], 1)
                        if P2_NOX2:
                            v.sem_inc(sems["x2c"], 1)
                            continue
                        v.wait_ge(sems["woap"], B["woap"] + wt + 1)
                        v.wait_ge(sems["xr"], B["xr"] + 16 * (wt + 1))
                        if wt >= 2:
                            v.wait_ge(sems["x2st"], B["x2st"] + 16 * (wt - 1))
                        vsync(v, v.tensor_tensor(x2s[:, par * D:par * D + D],
                                                 xstg[:, par * D:par * D + D],
                                                 bb[:], op=OP.add))
                        vsync(v, v.tensor_tensor(tsb[:], tsb[:],
                                                 gb[:], op=OP.mult))
                        v.tensor_tensor(x2s[:, par * D:par * D + D],
                                        x2s[:, par * D:par * D + D],
                                        tsb[:],
                                        op=OP.add).then_inc(sems["x2c"], 1)

            @blk.scalar
            def _(s):
                def rchain(qh2):
                    if qh2 < 0:
                        return
                    par = qh2 % 2
                    s.wait_ge(sems["oev"], B["oev"] + qh2 + 1)
                    if qh2 >= 2:
                        s.wait_ge(sems["rbmm"], B["rbmm"] + qh2 - 1)
                    ssync(s, s.activation(rt12[:, 0:1024],
                                          oev[64:65, par * 1024:par * 1024 + 1024],
                                          AF.Ln))
                    s.activation(rb12[:, par * 1024:par * 1024 + 1024],
                                 rt12[:, 0:1024], AF.Exp,
                                 scale=-1.0).then_inc(sems["rb"], 1)

                for qb in range(4):
                    for h in range(NH):
                        qh = qb * NH + h
                        for kt in range(TT):
                            g = qh * TT + kt
                            s.wait_ge(sems["mmA"], B["mmA"] + g + 1)
                            if g >= 2:
                                s.wait_ge(sems["pv"], B["pv"] + g - 1)
                            s.activation(
                                pbuf[:, (g % 2) * 1024:(g % 2) * 1024 + 1024],
                                apm[g % 2][:], AF.Exp, scale=0.125,
                                bias=c_m8[:, 0:1]).then_inc(sems["exp"], 1)
                            if kt == 1 and h > 0:
                                rchain(qh - 1)
                    rchain(qb * NH + NH - 1)
                    for tt in range(4 if P2_STAGES >= 2 else 0):
                        wt = qb * 4 + tt
                        par = wt % 2
                        wop = [mix, apm[0], apm[1]][tt % 3]
                        s.wait_ge(sems["womm"], B["womm"] + wt + 1)
                        s.wait_ge(sems["wored"], B["wored"] + wt + 1)
                        s.activation(sqt[:], wop[:], AF.Square,
                                     accum_out=st2[:, 2 + par:3 + par]
                                     ).then_inc(sems["wosq"], 1)
                        s.wait_ge(sems["wovar"], B["wovar"] + wt + 1)
                        ssync(s, s.activation(st2[:, 12 + par:13 + par],
                                              st2[:, 6 + par:7 + par], AF.Ln,
                                              bias=c_eps[:, 0:1]))
                        s.activation(st2[:, 8 + par:9 + par],
                                     st2[:, 12 + par:13 + par], AF.Exp,
                                     scale=-0.5).then_inc(sems["wor"], 1)
                        s.wait_ge(sems["wombn"], B["wombn"] + wt + 1)
                        if wt >= 1:
                            s.wait_ge(sems["x2c"], B["x2c"] + wt)
                        s.activation(tsb[:], wop[:],
                                     AF.Identity,
                                     bias=st2[:, 10 + par:11 + par],
                                     scale=st2[:, 8 + par:9 + par]
                                     ).then_inc(sems["woap"], 1)

            @blk.tensor
            def _(t):
                t.wait_ge(sems["wc"], B["wc"] + 8)
                for qb in range(4):
                    t.wait_ge(sems["qt"], B["qt"] + 16 * TT * (qb + 1))
                    if P2_STAGES >= 2:
                        t.wait_ge(sems["wored"], B["wored"] + 4 * qb)
                        t.wait_ge(sems["wosq"], B["wosq"] + 4 * qb)
                        t.wait_ge(sems["woap"], B["woap"] + 4 * qb)
                    def emit_rbmm(hh):
                        if hh < 0:
                            return
                        qhh = qb * NH + hh
                        t.wait_ge(sems["rb"], B["rb"] + qhh + 1)
                        if qhh >= 1:
                            t.wait_ge(sems["rbe"], B["rbe"] + qhh)
                        rbb2 = (qhh % 2) * 1024
                        t.matmul(mix[0:64, 0:512], c_onb[0:1, 0:64],
                                 rb12[0:1, rbb2:rbb2 + 512],
                                 start=True, stop=True, tile_position=(0, 0))
                        t.matmul(mix[64:128, 0:512], c_onb[0:1, 0:64],
                                 rb12[0:1, rbb2 + 512:rbb2 + 1024],
                                 start=True, stop=True, tile_position=(0, 64)
                                 ).then_inc(sems["rbmm"], 1)

                    for h in range(NH):
                        qh = qb * NH + h
                        qtb = (qb % 2) * 8192 + h * 512

                        def emitA(kt):
                            g = qh * TT + kt
                            t.matmul(apm[g % 2][:, 0:512],
                                     KT[0:64, h * S + 128 * kt:
                                        h * S + 128 * kt + 128],
                                     qtq[0:64, qtb:qtb + 512],
                                     start=True, stop=True)
                            t.matmul(apm[g % 2][:, 512:1024],
                                     KT[64:128, h * S + 128 * kt:
                                        h * S + 128 * kt + 128],
                                     qtq[64:128, qtb:qtb + 512],
                                     start=True, stop=True
                                     ).then_inc(sems["mmA"], 1)

                        def emitPV(kt):
                            g = qh * TT + kt
                            t.wait_ge(sems["exp"], B["exp"] + g + 1)
                            if kt == 0:
                                t.wait_ge(sems["oev"], B["oev"] + qh)
                            vb = kt * 1040 + 65 * h
                            pb = (g % 2) * 1024
                            t.matmul(o12[:, 0:512], Vaug[:, vb:vb + 65],
                                     pbuf[:, pb:pb + 512],
                                     start=(kt == 0), stop=(kt == 15))
                            t.matmul(o12[:, 512:1024], Vaug[:, vb:vb + 65],
                                     pbuf[:, pb + 512:pb + 1024],
                                     start=(kt == 0), stop=(kt == 15)
                                     ).then_inc(sems["pv"], 1)

                        emitA(0)
                        emitA(1)
                        for kt in range(TT - 1):
                            emitPV(kt)
                            if kt == 3:
                                emit_rbmm(h - 1)
                            if kt + 2 < TT:
                                emitA(kt + 2)
                        emitPV(TT - 1)
                    emit_rbmm(NH - 1)
                    if P2_STAGES >= 2:
                        t.wait_ge(sems["comb"], B["comb"] + 16 * (qb + 1))
                        t.wait_ge(sems["rbe"], B["rbe"] + 16 * (qb + 1))
                        t.wait_ge(sems["exp"], B["exp"] + 256 * (qb + 1))
                    for tt in range(4 if P2_STAGES >= 2 else 0):
                        wt = qb * 4 + tt
                        wop = [mix, apm[0], apm[1]][tt % 3]
                        if wt >= 3:
                            t.wait_ge(sems["wored"], B["wored"] + wt - 2)
                            t.wait_ge(sems["wosq"], B["wosq"] + wt - 2)
                            t.wait_ge(sems["woap"], B["woap"] + wt - 2)
                        for dc in range(8):
                            for nh2 in range(2):
                                inst = t.matmul(wop[:, 512 * nh2:512 * nh2 + 512],
                                                aoT[:, dc * 512 + 128 * tt:
                                                    dc * 512 + 128 * tt + 128],
                                                wob[:, dc * D + 512 * nh2:
                                                    dc * D + 512 * nh2 + 512],
                                                start=(dc == 0), stop=(dc == 7))
                        inst.then_inc(sems["womm"], 1)

            if f"aoT_{b}" in taps:
                @blk.gpsimd
                def _(g):
                    dst = tapout(f"aoT_{b}", [128, 8 * 512], BF16)
                    g.wait_ge(sems["comb"], B["comb"] + 64)
                    g.dma_start(out=dst, in_=aoT[:]).then_inc(sems["ost"], 16)
                    cnt["ost"] += 16
                    g.wait_ge(sems["ost"], cnt["ost"])
            barrier(blk)
        s2 = P2_STAGES >= 2
        for k, d in [("wld", 16 * 8), ("wc", 8), ("qt", 16 * TT * 4),
                     ("mmA", 1024), ("exp", 1024), ("pv", 1024),
                     ("rb", 64), ("rbmm", 64), ("rbe", 64), ("comb", 64),
                     ("oev", 64),
                     ("womm", 16 if s2 else 0), ("wored", 16 if s2 else 0),
                     ("wosq", 16 if s2 else 0), ("wovar", 16 if s2 else 0),
                     ("wor", 16 if s2 else 0), ("wombn", 16 if s2 else 0),
                     ("woap", 16 if s2 else 0),
                     ("xr", 16 * 16 if (s2 and not P2_NOX2) else 0),
                     ("x2c", 16 if s2 else 0),
                     ("x2st", 16 * 16 if (s2 and not P2_NOX2) else 0)]:
            cnt[k] = B[k] + d

    def emit_p3():
        e3 = ExitStack()

        def sb3(name, shape, dt):
            return e3.enter_context(nc.sbuf_tensor(name, list(shape), dt))

        CWE = sb3("CWE", [128, NB * S], BF16)

        B = defaultdict(int, cnt)
        with nc.Block() as blk, ExitStack() as ps:
            trp = [ps.enter_context(nc.psum_tensor(f"t3_{i}", [128, 1024], BF16))
                   for i in range(2)]
            lgp = ps.enter_context(nc.psum_tensor("lgp", [128, 512], F32))
            cwp = ps.enter_context(nc.psum_tensor("cwp", [64, 512], F32))
            def sbp(name, shape, dt):
                return ps.enter_context(nc.sbuf_tensor(name, list(shape), dt))
            rwb = sbp("rwb", [128, 8 * E], BF16)
            rws = sbp("rws", [128, 8 * E], F32)
            xst3 = sbp("xst3", [128, 2 * D], F32)
            lnfb = sbp("lnfb", [128, 2 * D], BF16)
            lnfa = sbp("lnfa", [128, D], BF16)
            sq3 = sbp("sq3", [128, D], F32)
            st3 = sbp("st3", [128, 16], F32)
            lgs = sbp("lgs", [128, E], F32)
            rt3t = sbp("rt3t", [128, 4], F32)
            m8 = sbp("m8", [128, 8], F32)
            elx = sbp("elx", [128, E], F32)

            @blk.sync
            def _(sy):
                sy.dma_start(out=rws[:].rearrange("p (c e) -> p c e", e=E),
                             in_=rw_d.rearrange("(c p) e -> p c e", p=128)
                             ).then_inc(sems["wld"], 16)
                sy.wait_ge(sems["x2st"], 16 * GT)
                for gt in range(GT):
                    if gt >= 2:
                        sy.wait_ge(sems["ln3"], B["ln3"] + gt - 1)
                    sy.dma_start(out=xst3[:, (gt % 2) * D:(gt % 2) * D + D],
                                 in_=x2_d[gt]).then_inc(sems["xr3"], 16)

            @blk.vector
            def _(v):
                v.wait_ge(sems["wld"], B["wld"] + 16)
                v.tensor_copy(rwb[:], rws[:]).then_inc(sems["wc"], 1)
                vsync(v, v.memset(CWE[64:128, :], 0.0))
                v.memset(CWE[64:66, :], 1.0)
                v.sem_inc(sems["cw4"], 1)
                for gt in range(GT):
                    par = gt % 2
                    v.wait_ge(sems["xr3"], B["xr3"] + 16 * (gt + 1))
                    vsync(v, v.reduce_sum(st3[:, par:par + 1],
                                          xst3[:, par * D:par * D + D],
                                          axis=AX.X))
                    v.wait_ge(sems["sq"], B["sq"] + gt + 1)
                    vsync(v, v.tensor_scalar(st3[:, 4 + par:5 + par],
                                             st3[:, par:par + 1],
                                             1.0 / D, None, op0=OP.mult))
                    v.tensor_scalar(st3[:, 14 + par:15 + par],
                                    st3[:, 2 + par:3 + par], 1.0 / D, None,
                                    op0=OP.mult)
                    vsync(v, v.tensor_tensor(st3[:, 12 + par:13 + par],
                                             st3[:, 4 + par:5 + par],
                                             st3[:, 4 + par:5 + par],
                                             op=OP.mult))
                    v.tensor_tensor(st3[:, 6 + par:7 + par],
                                    st3[:, 14 + par:15 + par],
                                    st3[:, 12 + par:13 + par],
                                    op=OP.subtract).then_inc(sems["var"], 1)
                    v.wait_ge(sems["r"], B["r"] + gt + 1)
                    v.scalar_tensor_tensor(st3[:, 10 + par:11 + par],
                                           st3[:, 4 + par:5 + par], -1.0,
                                           st3[:, 8 + par:9 + par],
                                           op0=OP.mult, op1=OP.mult
                                           ).then_inc(sems["mbn"], 1)
                    if gt >= 2:
                        v.wait_ge(sems["lnld"], B["lnld"] + 16 * 8 * (gt - 1))
                        v.wait_ge(sems["rt3"], B["rt3"] + gt - 1)
                    for dc in range(8):
                        v.wait_ge(sems["tr"], B["tr"] + 8 * gt + dc + 1)
                        v.tensor_copy(lnfb[:, par * D + 128 * dc:
                                          par * D + 128 * dc + 128],
                                      trp[dc % 2][:, 0:128]).then_inc(sems["trev"], 1)
                    v.wait_ge(sems["rt3"], B["rt3"] + gt + 1)
                    vsync(v, v.tensor_tensor(lgs[:], lgp[:, 0:E], rbb[:],
                                             op=OP.add))
                    vsync(v, v.max(out=m8[:], in_=lgs[:]))
                    v.tensor_scalar(rt3t[:, 0:1], m8[:, 0:1],
                                    -1.0, None, op0=OP.mult
                                    ).then_inc(sems["cw1"], 1)
                    v.wait_ge(sems["cw2"], B["cw2"] + gt + 1)
                    vsync(v, v.tensor_scalar(lgs[:], lgs[:], m8[:, 7:8], None,
                                             op0=OP.is_ge))
                    vsync(v, v.tensor_tensor(elx[:], elx[:], lgs[:],
                                             op=OP.mult))
                    vsync(v, v.reduce_sum(rt3t[:, 1:2], elx[:], axis=AX.X))
                    vsync(v, v.reciprocal(rt3t[:, 2:3], rt3t[:, 1:2]))
                    v.tensor_scalar(elx[:], elx[:], rt3t[:, 2:3], None,
                                    op0=OP.mult).then_inc(sems["cw3"], 1)
                    v.wait_ge(sems["cw4"], B["cw4"] + gt + 2)
                    v.tensor_copy(CWE[0:64, 128 * gt:128 * gt + 128],
                                  cwp[:, 0:128]).then_inc(sems["cwu"], 1)

            @blk.scalar
            def _(s):
                for gt in range(GT):
                    par = gt % 2
                    s.wait_ge(sems["xr3"], B["xr3"] + 16 * (gt + 1))
                    s.activation(sq3[:], xst3[:, par * D:par * D + D], AF.Square,
                                 accum_out=st3[:, 2 + par:3 + par]
                                 ).then_inc(sems["sq"], 1)
                    s.wait_ge(sems["var"], B["var"] + gt + 1)
                    ssync(s, s.activation(st3[:, 12 + par:13 + par],
                                          st3[:, 6 + par:7 + par], AF.Ln,
                                          bias=c_eps[:, 0:1]))
                    s.activation(st3[:, 8 + par:9 + par],
                                 st3[:, 12 + par:13 + par], AF.Exp,
                                 scale=-0.5).then_inc(sems["r"], 1)
                    s.wait_ge(sems["mbn"], B["mbn"] + gt + 1)
                    if gt >= 1:
                        s.wait_ge(sems["tr"], B["tr"] + 8 * gt)
                    s.activation(lnfa[:], xst3[:, par * D:par * D + D],
                                 AF.Identity, bias=st3[:, 10 + par:11 + par],
                                 scale=st3[:, 8 + par:9 + par]
                                 ).then_inc(sems["ln3"], 1)
                    s.wait_ge(sems["cw1"], B["cw1"] + gt + 1)
                    s.wait_ge(sems["cw4"], B["cw4"] + gt + 1)
                    s.activation(elx[:], lgs[:], AF.Exp,
                                 bias=rt3t[:, 0:1]
                                 ).then_inc(sems["cw2"], 1)
                    s.wait_ge(sems["trev"], B["trev"] + 8 * (gt + 1))
                    for dc in range(8):
                        s.dma_start(out=lnft_d[dc, :, 128 * gt:128 * gt + 128],
                                    in_=lnfb[:, par * D + 128 * dc:
                                             par * D + 128 * dc + 128]
                                    ).then_inc(sems["lnld"], 16)
                s.wait_ge(sems["lnld"], B["lnld"] + 16 * 8 * GT)

            @blk.tensor
            def _(t):
                t.wait_ge(sems["wc"], B["wc"] + 1)
                for gt in range(GT):
                    par = gt % 2
                    t.wait_ge(sems["ln3"], B["ln3"] + gt + 1)
                    for dc in range(8):
                        gi = 8 * gt + dc
                        if gi >= 2:
                            t.wait_ge(sems["trev"], B["trev"] + gi - 1)
                        t.transpose(trp[dc % 2][:, 0:128],
                                    lnfa[:, 128 * dc:128 * dc + 128],
                                    c_idb[:]).then_inc(sems["tr"], 1)
                    t.wait_ge(sems["trev"], B["trev"] + 8 * (gt + 1))
                    if gt >= 2:
                        t.wait_ge(sems["lnld"], B["lnld"] + 16 * 8 * (gt - 1))
                    for dc in range(8):
                        inst = t.matmul(lgp[:, 0:E],
                                        lnfb[:, par * D + 128 * dc:
                                             par * D + 128 * dc + 128],
                                        rwb[:, dc * E:dc * E + E],
                                        start=(dc == 0), stop=(dc == 7))
                    inst.then_inc(sems["rt3"], 1)
                    t.wait_ge(sems["cw3"], B["cw3"] + gt + 1)
                    if gt >= 1:
                        t.wait_ge(sems["cwu"], B["cwu"] + gt)
                    t.transpose(cwp[:, 0:128], elx[:, 0:64], c_idf[:]
                                ).then_inc(sems["cw4"], 1)
            barrier(blk)
        for k, d in [("wld", 16), ("wc", 1), ("xr3", 16 * GT), ("sq", GT),
                     ("var", GT), ("r", GT), ("mbn", GT), ("ln3", GT),
                     ("tr", 8 * GT), ("trev", 8 * GT), ("rt3", GT),
                     ("cw1", GT), ("cw2", GT), ("cw3", GT), ("cw4", GT + 1),
                     ("cwu", GT), ("hz", GT), ("lnld", 16 * 8 * GT)]:
            cnt[k] = B[k] + d

        B = defaultdict(int, cnt)
        with nc.Block() as blk, ExitStack() as ps:
            gps = [ps.enter_context(nc.psum_tensor(f"g3_{m}", [128, 1024], F32))
                   for m in range(2)]
            cwbp = ps.enter_context(nc.psum_tensor("cwbp", [128, 1024], F32))
            w2p = [ps.enter_context(nc.psum_tensor(f"w2p{i}", [128, 512], F32))
                   for i in range(2)]
            def sbm(name, shape, dt):
                return ps.enter_context(nc.sbuf_tensor(name, list(shape), dt))
            lnfc = sbm("lnfc", [128, 8 * 1024], BF16)
            Gg = sbm("Gg", [128, 16 * 1024], BF16)
            w2b = sbm("w2b", [128, 16 * 1024], BF16)
            acc = sbm("acc", [128, 8 * 1024], F32)
            ohb = sbm("ohb", [E, E * 128], BF16)
            w1s = sbm("w1s", [128, 2 * 2048], F32)
            w1b = sbm("w1b", [128, 4 * 2048], BF16)
            w2s = sbm("w2s", [128, 2 * 1024], F32)
            stmp = sbm("stmp", [128, 2 * 1024], BF16)
            x2s3 = sbm("x2s3", [128, 2 * D], F32)
            ostg = sbm("ostg", [128, 2 * D], F32)
            b2st = sbm("b2st", [64, D], F32)
            sb2st = sbm("sb2st", [64, D], F32)
            b2b = sbm("b2b", [128, D], BF16)

            def eidx(p, grp, el):
                return (p * NGRP + grp) * 8 + el

            def epidx(p, grp, mt, nh2):
                return ((p * NGRP + grp) * 8 + mt) * 2 + nh2

            @blk.sync
            def _(sy):
                sy.dma_start(out=ohb[:], in_=oh_d).then_inc(sems["wld"], 16)
                sy.dma_start(out=b2st[:], in_=b2_d).then_inc(sems["wld"], 16)
                sy.dma_start(out=sb2st[0:2, :], in_=sb2_d).then_inc(sems["wld"], 16)
                def w1load(pg2, el2):
                    ei = pg2 * 8 + el2
                    p2, grp2 = divmod(pg2, NGRP)
                    if ei >= 2:
                        sy.wait_ge(sems["w1c"], B["w1c"] + ei - 1)
                    if grp2 < 8:
                        e = grp2 * 8 + el2
                        srca = w1_d[e].rearrange("(c p) h -> p c h", p=128)
                    else:
                        es2 = grp2 - 8
                        srca = sw1_d[es2].rearrange("(c p) h -> p c h",
                                                    p=128)[:, :,
                                                           256 * el2:
                                                           256 * el2 + 256]
                    sy.dma_start(
                        out=w1s[:, (ei % 2) * 2048:(ei % 2) * 2048 + 2048]
                        .rearrange("p (c h) -> p c h", h=HE),
                        in_=srca).then_inc(sems["w1ld"], 16)

                def w2load(pg2, r):
                    ri = pg2 * 16 + r
                    p2, grp2 = divmod(pg2, NGRP)
                    if ri >= 2:
                        sy.wait_ge(sems["w2c"], B["w2c"] + ri - 1)
                    if grp2 < 8:
                        srca = w2_f[grp2 * 2048 + 128 * r:
                                    grp2 * 2048 + 128 * r + 128, :]
                    else:
                        srca = sw2_f[(grp2 - 8) * 2048 + 128 * r:
                                     (grp2 - 8) * 2048 + 128 * r + 128, :]
                    sy.dma_start(
                        out=w2s[:, (ri % 2) * 1024:(ri % 2) * 1024 + 1024],
                        in_=srca).then_inc(sems["w2ld"], 16)

                w1load(0, 0)
                w1load(0, 1)
                for dc in range(8):
                    sy.dma_start(out=lnfc[:, dc * 1024:dc * 1024 + 1024],
                                 in_=lnft_d[dc, :, 0:TC]
                                 ).then_inc(sems["lnf"], 16)
                for p in range(NPASS):
                    for grp in range(NGRP):
                        pg = p * NGRP + grp
                        for el in range(8):
                            w2load(pg, 2 * el)
                            w2load(pg, 2 * el + 1)
                            if el < 6:
                                w1load(pg, el + 2)
                        if pg + 1 < NPASS * NGRP:
                            w1load(pg + 1, 0)
                            w1load(pg + 1, 1)
                    if p + 1 < NPASS:
                        sy.wait_ge(sems["gmm"], B["gmm"] + 160 * (p + 1))
                        for dc in range(8):
                            sy.dma_start(out=lnfc[:, dc * 1024:dc * 1024 + 1024],
                                         in_=lnft_d[dc, :, TC * (p + 1):
                                                    TC * (p + 1) + TC]
                                         ).then_inc(sems["lnf"], 16)
                    for mt in range(8):
                        oi = p * 8 + mt
                        if oi >= 2:
                            sy.wait_ge(sems["oadd"], B["oadd"] + oi - 1)
                        sy.dma_start(out=x2s3[:, (oi % 2) * D:(oi % 2) * D + D],
                                     in_=x2_d[oi]).then_inc(sems["xr3"], 16)

            @blk.vector
            def _(v):
                v.wait_ge(sems["wld"], B["wld"] + 48)
                vsync(v, v.memset(b2b[:], 0.0))
                v.tensor_copy(b2b[0:64, :], b2st[:])
                v.tensor_copy(b2b[64:66, :], sb2st[0:2, :]).then_inc(sems["wc"], 1)

                def w1cast(pg2, el2):
                    ei = pg2 * 8 + el2
                    v.wait_ge(sems["w1ld"], B["w1ld"] + 16 * (ei + 1))
                    if ei >= 4:
                        v.wait_ge(sems["gmm"], B["gmm"] + 2 * (ei - 3))
                    v.tensor_copy(
                        w1b[:, (ei % 4) * 2048:(ei % 4) * 2048 + 2048],
                        w1s[:, (ei % 2) * 2048:(ei % 2) * 2048 + 2048]
                        ).then_inc(sems["w1c"], 1)

                def cwm(p2, grp2, el2):
                    ei = (p2 * NGRP + grp2) * 8 + el2
                    ir = p2 * 64 + grp2 * 8 + el2
                    for m in range(2):
                        si = 2 * ei + m
                        v.wait_ge(sems["silu"], B["silu"] + si + 1)
                        v.wait_ge(sems["cwb"], B["cwb"] + ir + 1)
                        if el2 == 0 and m == 0:
                            v.wait_ge(sems["w2mm"],
                                      B["w2mm"] + 16 * (p2 * NGRP + grp2))
                        v.tensor_tensor(
                            Gg[:, (2 * el2 + m) * 1024:
                               (2 * el2 + m) * 1024 + 1024],
                            stmp[:, m * 1024:m * 1024 + 1024],
                            cwbp[:], op=OP.mult).then_inc(sems["cwu"], 1)

                def w2cast(p2, grp2, r):
                    ri = (p2 * NGRP + grp2) * 16 + r
                    v.wait_ge(sems["w2ld"], B["w2ld"] + 16 * (ri + 1))
                    if r == 0:
                        v.wait_ge(sems["w2mm"],
                                  B["w2mm"] + 16 * (p2 * NGRP + grp2))
                    v.tensor_copy(w2b[:, r * 1024:r * 1024 + 1024],
                                  w2s[:, (ri % 2) * 1024:(ri % 2) * 1024 + 1024]
                                  ).then_inc(sems["w2c"], 1)

                w1cast(0, 0)
                w1cast(0, 1)
                for p in range(NPASS):
                    for grp in range(NGRP):
                        pg = p * NGRP + grp
                        for el in range(8):
                            w2cast(p, grp, 2 * el)
                            w2cast(p, grp, 2 * el + 1)
                            if el >= 1 and grp < 8:
                                cwm(p, grp, el - 1)
                            if el < 6:
                                w1cast(pg, el + 2)
                        if grp < 8:
                            cwm(p, grp, 7)
                        if pg + 1 < NPASS * NGRP:
                            w1cast(pg + 1, 0)
                            w1cast(pg + 1, 1)
                        for mt in range(8):
                            for nh2 in range(2):
                                ep = epidx(p, grp, mt, nh2)
                                v.wait_ge(sems["w2mm"], B["w2mm"] + ep + 1)
                                dst = acc[:, mt * 1024 + 512 * nh2:
                                          mt * 1024 + 512 * nh2 + 512]
                                if grp == 0:
                                    v.tensor_copy(dst, w2p[ep % 2][:]
                                                  ).then_inc(sems["acc"], 1)
                                else:
                                    v.tensor_tensor(dst, dst, w2p[ep % 2][:],
                                                    op=OP.add
                                                    ).then_inc(sems["acc"], 1)
                    for mt in range(8):
                        oi = p * 8 + mt
                        v.wait_ge(sems["acc"], B["acc"] + 160 * (p + 1))
                        v.wait_ge(sems["xr3"], B["xr3"] + 16 * (oi + 1))
                        if oi >= 2:
                            v.wait_ge(sems["ost"], B["ost"] + 16 * (oi - 1))
                        v.tensor_tensor(ostg[:, (oi % 2) * D:(oi % 2) * D + D],
                                        acc[:, mt * 1024:mt * 1024 + 1024],
                                        x2s3[:, (oi % 2) * D:(oi % 2) * D + D],
                                        op=OP.add).then_inc(sems["oadd"], 1)

            @blk.scalar
            def _(s):
                for p in range(NPASS):
                    for grp in range(NGRP):
                        for el in range(8):
                            ei = eidx(p, grp, el)
                            for m in range(2):
                                si = 2 * ei + m
                                s.wait_ge(sems["gmm"], B["gmm"] + si + 1)
                                if grp < 8:
                                    e = grp * 8 + el
                                    bias = b1T[:, 64 * m + e:64 * m + e + 1]
                                    sir = 2 * (p * 64 + grp * 8 + el) + m
                                    if sir >= 2:
                                        s.wait_ge(sems["cwu"],
                                                  B["cwu"] + sir - 1)
                                    s.activation(
                                        stmp[:, m * 1024:m * 1024 + 1024],
                                        gps[m][:], AF.Silu, bias=bias
                                        ).then_inc(sems["silu"], 1)
                                else:
                                    es2 = grp - 8
                                    ht = 2 * el + m
                                    bias = sb1T[:, 2 * ht + es2:2 * ht + es2 + 1]
                                    if el == 0 and m == 0:
                                        s.wait_ge(sems["w2mm"],
                                                  B["w2mm"] +
                                                  16 * (p * NGRP + grp))
                                    s.activation(
                                        Gg[:, (2 * el + m) * 1024:
                                           (2 * el + m) * 1024 + 1024],
                                        gps[m][:], AF.Silu, bias=bias
                                        ).then_inc(sems["silu"], 1)
                    for mt in range(8):
                        oi = p * 8 + mt
                        s.wait_ge(sems["oadd"], B["oadd"] + oi + 1)
                        s.dma_start(out=out_t[oi],
                                    in_=ostg[:, (oi % 2) * D:(oi % 2) * D + D]
                                    ).then_inc(sems["ost"], 16)
                s.wait_ge(sems["ost"], B["ost"] + 16 * GT)

            @blk.tensor
            def _(t):
                t.wait_ge(sems["wc"], B["wc"] + 1)
                for p in range(NPASS):
                    t.wait_ge(sems["lnf"], B["lnf"] + 16 * 8 * (p + 1))
                    for grp in range(NGRP):
                        pg = p * NGRP + grp
                        for el in range(8):
                            ei = eidx(p, grp, el)
                            t.wait_ge(sems["w1c"], B["w1c"] + ei + 1)
                            for m in range(2):
                                si = 2 * ei + m
                                if si >= 2:
                                    t.wait_ge(sems["silu"], B["silu"] + si - 1)
                                for dc in range(8):
                                    for n2 in range(2):
                                        inst = t.matmul(
                                            gps[m][:, 512 * n2:512 * n2 + 512],
                                            w1b[:, (ei % 4) * 2048 + HE * dc +
                                                128 * m:
                                                (ei % 4) * 2048 + HE * dc +
                                                128 * m + 128],
                                            lnfc[:, dc * 1024 + 512 * n2:
                                                 dc * 1024 + 512 * n2 + 512],
                                            start=(dc == 0), stop=(dc == 7))
                                inst.then_inc(sems["gmm"], 1)
                                if m == 0 and grp < 8:
                                    e = grp * 8 + el
                                    ir = p * 64 + grp * 8 + el
                                    if ir >= 1:
                                        t.wait_ge(sems["cwu"],
                                                  B["cwu"] + 2 * ir)
                                    t.matmul(cwbp[:, 0:512],
                                             ohb[:, 128 * e:128 * e + 128],
                                             CWE[0:64, TC * p:TC * p + 512],
                                             start=True, stop=True)
                                    t.matmul(cwbp[:, 512:1024],
                                             ohb[:, 128 * e:128 * e + 128],
                                             CWE[0:64, TC * p + 512:
                                                 TC * p + 1024],
                                             start=True, stop=True
                                             ).then_inc(sems["cwb"], 1)
                        if grp < 8:
                            t.wait_ge(sems["cwu"],
                                      B["cwu"] + 16 * (8 * p + grp + 1))
                        else:
                            t.wait_ge(sems["silu"],
                                      B["silu"] + 16 * (NGRP * p + grp + 1))
                        for mt in range(8):
                            for nh2 in range(2):
                                ep = epidx(p, grp, mt, nh2)
                                if ep >= 2:
                                    t.wait_ge(sems["acc"], B["acc"] + ep - 1)
                                for r in range(16):
                                    t.wait_ge(sems["w2c"],
                                              B["w2c"] + (pg * 16) + r + 1)
                                    inst = t.matmul(
                                        w2p[ep % 2][:],
                                        Gg[:, r * 1024 + 128 * mt:
                                           r * 1024 + 128 * mt + 128],
                                        w2b[:, r * 1024 + 512 * nh2:
                                            r * 1024 + 512 * nh2 + 512],
                                        start=(r == 0),
                                        stop=(r == 15 and grp != NGRP - 1))
                                if grp == NGRP - 1:
                                    inst = t.matmul(
                                        w2p[ep % 2][:],
                                        CWE[:, TC * p + 128 * mt:
                                            TC * p + 128 * mt + 128],
                                        b2b[:, 512 * nh2:512 * nh2 + 512],
                                        start=False, stop=True)
                                inst.then_inc(sems["w2mm"], 1)
            barrier(blk)
        e3.close()

    if "setup_tap" in taps:
        with nc.Block() as blk:
            @blk.scalar
            def _(g):
                base = cnt["ost"]
                for nm, buf in [("gb_t", gb), ("bb_t", bb), ("rbb_t", rbb),
                                ("b1T_t", b1T), ("sb1T_t", sb1T)]:
                    dst = tapout(nm, list(buf.shape), F32)
                    g.dma_start(out=dst, in_=buf[:]).then_inc(sems["ost"], 16)
                    cnt["ost"] += 16
                dst = tapout("lam_t", [1, HD], BF16)
                g.dma_start(out=dst, in_=lam_b[:]).then_inc(sems["ost"], 16)
                cnt["ost"] += 16
                g.wait_ge(sems["ost"], cnt["ost"])
            barrier(blk)

    if "p1" in phases:
        emit_p1(0)
        if "p2" in phases:
            emit_p2(0)
        emit_p1(1)
        if "p2" in phases:
            emit_p2(1)
    e12.close()
    if "p3" in phases:
        emit_p3()
    eg.close()
    return nc, tap_out


def _consts():
    bf = ml_dtypes.bfloat16
    ident = np.eye(128, dtype=np.float32)
    ones = np.ones((1, 128), dtype=np.float32)
    oh = np.zeros((E, E * 128), dtype=np.float32)
    for e in range(E):
        oh[e, 128 * e:128 * e + 128] = 1.0
    return {
        "id_bf": ident.astype(bf), "id_f32": ident,
        "ones_f32": ones, "ones_bf": ones.astype(bf),
        "oh_bf": oh.astype(bf),
    }


def _weights(inputs):
    w = {
        "wq": inputs["L0_attn_Wq"], "wk": inputs["L0_attn_Wk"],
        "wv": inputs["L0_attn_Wv"], "wo": inputs["L0_attn_Wo"],
        "lq1": inputs["L0_attn_lambda_q1"].reshape(1, HD),
        "lk1": inputs["L0_attn_lambda_k1"].reshape(1, HD),
        "lq2": inputs["L0_attn_lambda_q2"].reshape(1, HD),
        "lk2": inputs["L0_attn_lambda_k2"].reshape(1, HD),
        "gam": inputs["L0_attn_gamma"].reshape(1, D),
        "bet": inputs["L0_attn_beta"].reshape(1, D),
        "rw": inputs["L0_ffn_router_w"],
        "rb": inputs["L0_ffn_router_b"].reshape(1, E),
        "w1": inputs["L0_ffn_w1"], "b1": inputs["L0_ffn_b1"],
        "w2": inputs["L0_ffn_w2"], "b2": inputs["L0_ffn_b2"],
        "sw1": inputs["L0_ffn_sw1"], "sb1": inputs["L0_ffn_sb1"],
        "sw2": inputs["L0_ffn_sw2"], "sb2": inputs["L0_ffn_sb2"],
    }
    return {k: np.ascontiguousarray(v, dtype=np.float32) for k, v in w.items()}


def kernel(**inputs):
    x = np.ascontiguousarray(inputs["L0_attn_x"], dtype=np.float32)
    consts = _consts()
    weights = _weights(inputs)
    nc, _ = build_nc()
    in_maps = []
    for c in range(8):
        m = dict(weights)
        m.update(consts)
        m["x_sh"] = np.ascontiguousarray(x[2 * c:2 * c + 2])
        in_maps.append(m)
    res = run_bass_kernel_spmd(nc, in_maps, core_ids=list(range(8)))
    return np.concatenate([r["out"] for r in res.results], axis=0)
